# version 15
# speedup vs baseline: 2.8905x; 1.0388x over previous
import numpy as np
import ml_dtypes
import concourse.bass as bass
import concourse.mybir as mybir
from concourse.bass_utils import run_bass_kernel_spmd
from concourse import library_config

F32 = mybir.dt.float32
BF16 = mybir.dt.bfloat16
I32 = mybir.dt.int32
AF = mybir.ActivationFunctionType
ALU = mybir.AluOpType
AX = mybir.AxisListType
NPBF16 = ml_dtypes.bfloat16

D = 2048
TPC = 1024
HALO = 3
TP = 1032
TG = 344
C = 128
NCH = 8
HV, HQ, DK, DV = 16, 8, 128, 128
NDB = 16
NFQ, NFG = 32, 16
NFB = NFQ + NFG + 1
FFH = 8192
NHB = 64
NEG = -1e30
EPS_LN = 1e-5
EPS_L2 = 1e-6
EPS_RMS = 1e-6
EPS_FFN = 1e-8
LSD = float(np.log(np.sqrt(DK)))
NEWTON_ITERS = 6


class Ctr:
    def __init__(self):
        self.v = {}

    def inc(self, sem, k=1):
        self.v[sem.name] = self.v.get(sem.name, 0) + k
        return self.v[sem.name]

    def get(self, sem):
        return self.v.get(sem.name, 0)


def build(probe=None, upto="all"):
    nc = bass.Bass("TRN2", target_bir_lowering=False, debug=False)

    x_in = nc.dram_tensor("x", [TP, D], F32, kind="ExternalInput").ap()
    wqkv_b = nc.dram_tensor("wqkv_b", [NFQ, D, 128], F32, kind="ExternalInput").ap()
    wgate_b = nc.dram_tensor("wgate_b", [NFG, D, 128], F32, kind="ExternalInput").ap()
    wba_in = nc.dram_tensor("wba", [D, 32], F32, kind="ExternalInput").ap()
    convw_in = nc.dram_tensor("convw", [128, NFQ * 4], F32, kind="ExternalInput").ap()
    la_in = nc.dram_tensor("loga_params", [HV, 2], F32, kind="ExternalInput").ap()
    rmsw_in = nc.dram_tensor("rmsw", [128, 1], F32, kind="ExternalInput").ap()
    wout_b = nc.dram_tensor("wout_b", [4, D, 512], F32, kind="ExternalInput").ap()
    wg_b = nc.dram_tensor("wg_b", [NHB, D, 128], F32, kind="ExternalInput").ap()
    wu_b = nc.dram_tensor("wu_b", [NHB, D, 128], F32, kind="ExternalInput").ap()
    wd_b = nc.dram_tensor("wd_b", [16, FFH, 128], F32, kind="ExternalInput").ap()
    ident_in = nc.dram_tensor("ident_bf", [128, 128], BF16, kind="ExternalInput").ap()
    identf_in = nc.dram_tensor("ident_f", [128, 128], F32, kind="ExternalInput").ap()
    maskLT_in = nc.dram_tensor("maskLT", [128, 128], F32, kind="ExternalInput").ap()
    maskUT_in = nc.dram_tensor("maskUT", [128, 128], F32, kind="ExternalInput").ap()
    maskUD_in = nc.dram_tensor("maskUD", [128, 128], F32, kind="ExternalInput").ap()
    onesc_in = nc.dram_tensor("ones_col_bf", [128, 1], BF16, kind="ExternalInput").ap()
    onesrf_in = nc.dram_tensor("ones_row_f", [1, 128], F32, kind="ExternalInput").ap()
    onesr_in = nc.dram_tensor("ones_row_bf", [1, 128], BF16, kind="ExternalInput").ap()
    zeros_in = nc.dram_tensor("zeros_f", [128, 128], F32, kind="ExternalInput").ap()
    ccols_in = nc.dram_tensor("ccols", [128, 8], F32, kind="ExternalInput").ap()
    hopm_in = nc.dram_tensor("hop_mcol", [128, 7], F32, kind="ExternalInput").ap()
    hopimi_in = nc.dram_tensor("hop_imi", [128, 7 * 128], F32, kind="ExternalInput").ap()

    y_out = nc.dram_tensor("y", [TPC, D], F32, kind="ExternalOutput").ap()

    PROBE_SHAPES = {
        "pre": ([NFB, 128, TP], BF16),
        "qkv": ([4, 1024, TPC], BF16),
        "rows": ([HV, 4 * TPC], F32),
        "spill": ([NCH, 4, 128, HV * 128], BF16),
        "comp": ([2, 128, HV * 128], BF16),
        "state": ([128, HV * 128], BF16),
        "pf": ([NCH, 128, NDB * 128], BF16),
        "x2": ([TPC, D], F32),
        "xq": ([128, NDB * TPC], BF16),
        "p2": ([128, NHB * TPC], BF16),
        "ffn": ([NDB, 128, TPC], BF16),
    }
    probe_out = None
    if probe is not None:
        shp, dt_ = PROBE_SHAPES[probe]
        probe_out = nc.dram_tensor("probe", shp, dt_, kind="ExternalOutput").ap()

    pre_scr = nc.dram_tensor("pre_scr", [NFB, 128, TP], BF16, kind="Internal").ap()
    spill = nc.dram_tensor("spill", [NCH, 4, 128, HV * 128], BF16, kind="Internal").ap()
    p_scr = nc.dram_tensor("p_scr", [NCH, 128, NDB * 128], BF16, kind="Internal").ap()
    x2_scr = nc.dram_tensor("x2_scr", [TPC, D], F32, kind="Internal").ap()
    ffn_scr = nc.dram_tensor("ffn_scr", [NDB, 128, TPC], BF16, kind="Internal").ap()
    cc_in = nc.dram_tensor("cc_in", [128, HV * 2 * 128], BF16, kind="Internal").ap()
    cc_out = nc.dram_tensor("cc_out", [8 * 128, HV * 2 * 128], BF16, kind="Internal").ap()

    ctr = Ctr()
    from contextlib import ExitStack

    with ExitStack() as top:
        E = top.enter_context
        s_ld = E(nc.semaphore("s_ld"))
        s_lda = E(nc.semaphore("s_lda"))
        s_ldb = E(nc.semaphore("s_ldb"))
        s_st = E(nc.semaphore("s_st"))
        s_sta = E(nc.semaphore("s_sta"))
        s_stb = E(nc.semaphore("s_stb"))
        s_ga = E(nc.semaphore("s_ga"))
        s_gb = E(nc.semaphore("s_gb"))
        s_mm = E(nc.semaphore("s_mm"))
        s_v = E(nc.semaphore("s_v"))
        s_s = E(nc.semaphore("s_s"))
        s_g = E(nc.semaphore("s_g"))
        s_cc = E(nc.semaphore("s_cc"))
        sems = dict(ld=s_ld, lda=s_lda, ldb=s_ldb, st=s_st, sta=s_sta, stb=s_stb,
                    ga=s_ga, gb=s_gb, mm=s_mm, v=s_v, s=s_s, g=s_g, cc=s_cc)

        c_ident = E(nc.sbuf_tensor("c_ident", [128, 128], BF16))
        c_identf = E(nc.sbuf_tensor("c_identf", [128, 128], F32))
        c_maskLT = E(nc.sbuf_tensor("c_maskLT", [128, 128], F32))
        c_maskUT = E(nc.sbuf_tensor("c_maskUT", [128, 128], F32))
        c_maskUD = E(nc.sbuf_tensor("c_maskUD", [128, 128], F32))
        c_onec = E(nc.sbuf_tensor("c_onec", [128, 1], BF16))
        c_onerf = E(nc.sbuf_tensor("c_onerf", [1, 128], F32))
        c_oner = E(nc.sbuf_tensor("c_oner", [1, 128], BF16))
        c_zero = E(nc.sbuf_tensor("c_zero", [128, 128], F32))
        c_cols = E(nc.sbuf_tensor("c_cols", [128, 8], F32))
        c_hopm = E(nc.sbuf_tensor("c_hopm", [128, 7], F32))
        c_hopimi = E(nc.sbuf_tensor("c_hopimi", [128, 7 * 128], F32))
        c_rmsw = E(nc.sbuf_tensor("c_rmsw", [128, 1], F32))
        c_convw = E(nc.sbuf_tensor("c_convw", [128, NFQ * 4], F32))
        c_la = E(nc.sbuf_tensor("c_la", [HV, 8], F32))

        blk = E(nc.Block())

        @blk.sync
        def _(q):
            for dst, src in [
                (c_ident[:, :], ident_in[:, :]), (c_identf[:, :], identf_in[:, :]),
                (c_maskLT[:, :], maskLT_in[:, :]), (c_maskUT[:, :], maskUT_in[:, :]),
                (c_maskUD[:, :], maskUD_in[:, :]), (c_onec[:, :], onesc_in[:, :]),
                (c_onerf[:, :], onesrf_in[:, :]), (c_oner[:, :], onesr_in[:, :]),
                (c_zero[:, :], zeros_in[:, :]), (c_cols[:, :], ccols_in[:, :]),
                (c_hopm[:, :], hopm_in[:, :]),
                (c_hopimi[:, :], hopimi_in[:, :]),
                (c_rmsw[:, :], rmsw_in[:, :]),
                (c_convw[:, :], convw_in[:, :]),
                (c_la[:, 0:2], la_in[:, :]),
            ]:
                q.dma_start(out=dst, in_=src).then_inc(s_ld, 16)
                ctr.inc(s_ld, 16)
        CONSTS = ctr.get(s_ld)

        @blk.scalar
        def _(s):
            s.wait_ge(s_ld, CONSTS)
            ctr.inc(s_s, 1)
            s.activation(c_la[:, 4:5], c_la[:, 0:1], AF.Exp).then_inc(s_s, 1)

        @blk.vector
        def _(v):
            v.wait_ge(s_s, ctr.get(s_s))
            v.tensor_scalar(c_la[:, 3:4], c_la[:, 4:5], -1.0, None, ALU.mult)
            ctr.inc(s_v, 1)
            v.tensor_scalar(c_la[:, 5:6], c_la[:, 1:2], -1.0, None, ALU.mult).then_inc(s_v, 1)

        with ExitStack() as sc2:
            E2 = sc2.enter_context
            qk_f = E2(nc.sbuf_tensor("qk_f", [128, 16 * TPC], BF16))
            k_tok = E2(nc.sbuf_tensor("k_tok", [128, HQ * TPC], BF16))
            v_tok = E2(nc.sbuf_tensor("v_tok", [128, HV * TPC], BF16))
            barows = E2(nc.sbuf_tensor("barows", [HV, 4 * TPC], F32))
            state = E2(nc.sbuf_tensor("state", [128, HV * 128], BF16))
            compP = E2(nc.sbuf_tensor("compP", [128, HV * 128], BF16))
            compR = E2(nc.sbuf_tensor("compR", [128, HV * 128], BF16))

            _p1(nc, blk, ctr, sems, locals())
            if probe == "pre":
                _probe_dump(nc, blk, ctr, sems, probe_out[0:NFB - 1, :, :], pre_scr[0:NFB - 1, :, :])
                _probe_dump(nc, blk, ctr, sems, probe_out[NFB - 1, 0:32, :], pre_scr[NFB - 1, 0:32, :])
                _finish_zero(nc, blk, ctr, sems, y_out)
                return nc, probe

            _p15(nc, blk, ctr, sems, locals())
            if probe == "qkv":
                @blk.sync
                def _(q):
                    q.wait_ge(s_v, ctr.get(s_v))
                    q.wait_ge(s_s, ctr.get(s_s))
                    for i, (src, off) in enumerate([(qk_f, 0), (qk_f, 8 * TPC), (k_tok, 0), (v_tok, 0)]):
                        q.dma_start(
                            out=probe_out[i].rearrange("(hb p) t -> p hb t", p=128)[:, 0:8, :],
                            in_=src.ap()[:, off:off + 8 * TPC].rearrange("p (hb t) -> p hb t", t=TPC),
                        ).then_inc(s_ld, 16)
                        ctr.inc(s_ld, 16)
                    q.wait_ge(s_ld, ctr.get(s_ld))
                _finish_zero(nc, blk, ctr, sems, y_out)
                return nc, probe
            if probe == "rows":
                @blk.sync
                def _(q):
                    q.wait_ge(s_v, ctr.get(s_v))
                    q.wait_ge(s_s, ctr.get(s_s))
                    q.dma_start(out=probe_out[:, :], in_=barows[:, :]).then_inc(s_ld, 16)
                    ctr.inc(s_ld, 16)
                    q.wait_ge(s_ld, ctr.get(s_ld))
                _finish_zero(nc, blk, ctr, sems, y_out)
                return nc, probe

            _p2a(nc, blk, ctr, sems, locals())
            if probe == "spill":
                _probe_dump(nc, blk, ctr, sems, probe_out[:, :, :, :], spill[:, :, :, :])
                _finish_zero(nc, blk, ctr, sems, y_out)
                return nc, probe
            if probe == "comp":
                @blk.sync
                def _(q):
                    q.wait_ge(s_v, ctr.get(s_v))
                    q.wait_ge(s_s, ctr.get(s_s))
                    q.dma_start(out=probe_out[0], in_=compP[:, :]).then_inc(s_ld, 16)
                    q.dma_start(out=probe_out[1], in_=compR[:, :]).then_inc(s_ld, 16)
                    ctr.inc(s_ld, 32)
                    q.wait_ge(s_ld, ctr.get(s_ld))
                _finish_zero(nc, blk, ctr, sems, y_out)
                return nc, probe

            _cc_hop(nc, blk, ctr, sems, locals())
            if probe == "state":
                @blk.sync
                def _(q):
                    q.wait_ge(s_v, ctr.get(s_v))
                    q.dma_start(out=probe_out[:, :], in_=state[:, :]).then_inc(s_ld, 16)
                    ctr.inc(s_ld, 16)
                    q.wait_ge(s_ld, ctr.get(s_ld))
                _finish_zero(nc, blk, ctr, sems, y_out)
                return nc, probe

            _p2b(nc, blk, ctr, sems, locals())
            if probe == "pf":
                _probe_dump(nc, blk, ctr, sems, probe_out[:, :, :], p_scr[:, :, :])
                _finish_zero(nc, blk, ctr, sems, y_out)
                return nc, probe

        with ExitStack() as sc3:
            E3 = sc3.enter_context
            xq_f = E3(nc.sbuf_tensor("xq_f", [128, NDB * TPC], BF16))

            _p3(nc, blk, ctr, sems, locals())
            if probe in ("x2", "xq"):
                if probe == "x2":
                    _probe_dump(nc, blk, ctr, sems, probe_out[:, :], x2_scr[:, :])
                else:
                    @blk.sync
                    def _(q):
                        q.wait_ge(s_s, ctr.get(s_s))
                        q.dma_start(out=probe_out[:, :], in_=xq_f[:, :]).then_inc(s_ld, 16)
                        ctr.inc(s_ld, 16)
                        q.wait_ge(s_ld, ctr.get(s_ld))
                _finish_zero(nc, blk, ctr, sems, y_out)
                return nc, probe

            with ExitStack() as sff:
                EF = sff.enter_context
                p2 = EF(nc.sbuf_tensor("p2", [128, NHB * TPC], BF16))
                _ffn_gu(nc, blk, ctr, sems, locals())
                if probe == "p2":
                    @blk.sync
                    def _(q):
                        q.wait_ge(s_v, ctr.get(s_v))
                        q.dma_start(out=probe_out[:, :], in_=p2[:, :]).then_inc(s_ld, 16)
                        ctr.inc(s_ld, 16)
                        q.wait_ge(s_ld, ctr.get(s_ld))
                    _finish_zero(nc, blk, ctr, sems, y_out)
                    return nc, probe
                _ffn_down(nc, blk, ctr, sems, locals())
            if probe == "ffn":
                _probe_dump(nc, blk, ctr, sems, probe_out[:, :, :], ffn_scr[:, :, :])
                _finish_zero(nc, blk, ctr, sems, y_out)
                return nc, probe

        _final(nc, blk, ctr, sems, locals())
    return nc, probe


def _phase_barrier(nc, blk, ctr, sems):
    names = ("ld", "lda", "ldb", "st", "sta", "stb", "ga", "gb", "mm", "v", "s", "g", "cc")
    vals = {nm: ctr.get(sems[nm]) for nm in names}
    for eng in ("sync", "tensor", "vector", "scalar", "gpsimd"):
        @getattr(blk, eng)
        def _(e, eng=eng):
            for nm in names:
                e.wait_ge(sems[nm], vals[nm])


def _probe_dump(nc, blk, ctr, sems, dst_ap, src_ap):
    @blk.sync
    def _(q):
        for nm in ("st", "sta", "stb", "v", "s", "g", "mm"):
            q.wait_ge(sems[nm], ctr.get(sems[nm]))
        q.dma_start(out=dst_ap, in_=src_ap).then_inc(sems["ld"], 16)
        ctr.inc(sems["ld"], 16)
        q.wait_ge(sems["ld"], ctr.get(sems["ld"]))


def _finish_zero(nc, blk, ctr, sems, y_out):
    s_st, s_g = sems["st"], sems["g"]
    _phase_barrier(nc, blk, ctr, sems)
    with nc.sbuf_tensor("yzero", [128, D], F32) as yz:
        @blk.gpsimd
        def _(g):
            ctr.inc(s_g, 1)
            g.memset(yz[:], 0.0).then_inc(s_g, 1)
        GV = ctr.get(s_g)

        @blk.sync
        def _(q):
            q.wait_ge(s_g, GV)
            for tb in range(TPC // 128):
                q.dma_start(out=y_out[tb * 128:(tb + 1) * 128, :], in_=yz[:]).then_inc(s_st, 16)
                ctr.inc(s_st, 16)
            q.wait_ge(s_st, ctr.get(s_st))


P1_STORE_HIST = {}


def _p1(nc, blk, ctr, sems, env):
    s_ld, s_st, s_mm, s_v, s_s = sems["ld"], sems["st"], sems["mm"], sems["v"], sems["s"]
    s_lda, s_ldb, s_sta, s_stb = sems["lda"], sems["ldb"], sems["sta"], sems["stb"]
    ldp = [s_lda, s_ldb]
    stp = [s_sta, s_stb]
    x_in, c_ident = env["x_in"], env["c_ident"]
    pre_scr = env["pre_scr"]
    wqkv_b, wgate_b, wba_in = env["wqkv_b"], env["wgate_b"], env["wba_in"]
    CONSTS = env["CONSTS"]
    from contextlib import ExitStack

    with ExitStack() as st1:
        E1 = st1.enter_context
        xtile = E1(nc.sbuf_tensor("xtile", [128, 2 * D], F32))
        xn_t = E1(nc.sbuf_tensor("xn_t", [128, D], BF16))
        xn_f = E1(nc.sbuf_tensor("xn_f", [128, NDB * TP], BF16))
        lnt = E1(nc.sbuf_tensor("lnt", [128, 32], F32))
        ps_ln0 = E1(nc.psum_tensor("ps_ln0", [128, D // 2], F32))
        ps_ln1 = E1(nc.psum_tensor("ps_ln1", [128, D // 2], F32))

        LN_HIST = {}
        for tb in range(9):
            rows = 128 if tb < 8 else TP - 8 * 128
            xt = xtile[:rows, (tb % 2) * D:(tb % 2) * D + D]
            psl = ps_ln0 if tb % 2 == 0 else ps_ln1

            sl = ldp[tb % 2]

            @blk.sync
            def _(q, tb=tb, rows=rows, xt=xt, sl=sl):
                if tb >= 2:
                    q.wait_ge(s_s, ctr.get(s_s))
                q.wait_ge(sl, ctr.get(sl))
                q.dma_start(out=xt, in_=x_in[tb * 128: tb * 128 + rows, :]).then_inc(sl, 16)
                ctr.inc(sl, 16)
            XW = ctr.get(sl)

            @blk.vector
            def _(v, rows=rows, xt=xt, XW=XW, sl=sl):
                v.wait_ge(sl, XW)
                ins = None
                for i4 in range(4):
                    ins = v.bn_stats(lnt[:rows, i4 * 6:(i4 + 1) * 6], xt[:, i4 * 512:(i4 + 1) * 512])
                ctr.inc(s_v, 1); ins.then_inc(s_v, 1); v.wait_ge(s_v, ctr.get(s_v))
                ins = v.bn_aggr(lnt[:rows, 24:26], lnt[:rows, 0:24])
                ctr.inc(s_v, 1); ins.then_inc(s_v, 1); v.wait_ge(s_v, ctr.get(s_v))
                ctr.inc(s_v, 1)
                v.tensor_scalar(lnt[:rows, 26:27], lnt[:rows, 25:26], EPS_LN, None, ALU.add).then_inc(s_v, 1)
            V1 = ctr.get(s_v)

            @blk.scalar
            def _(s, rows=rows, V1=V1):
                s.wait_ge(s_v, V1)
                ctr.inc(s_s, 1)
                s.activation(lnt[:rows, 27:28], lnt[:rows, 26:27], AF.Sqrt).then_inc(s_s, 1)
            S1 = ctr.get(s_s)

            @blk.vector
            def _(v, rows=rows, S1=S1):
                v.wait_ge(s_s, S1)
                ins = v.reciprocal(lnt[:rows, 28:29], lnt[:rows, 27:28])
                ctr.inc(s_v, 1); ins.then_inc(s_v, 1); v.wait_ge(s_v, ctr.get(s_v))
                ctr.inc(s_v, 1)
                v.scalar_tensor_tensor(out=lnt[:rows, 29:30], in0=lnt[:rows, 24:25],
                                       scalar=-1.0, in1=lnt[:rows, 28:29],
                                       op0=ALU.mult, op1=ALU.mult).then_inc(s_v, 1)
            V2 = ctr.get(s_v)

            @blk.scalar
            def _(s, tb=tb, rows=rows, xt=xt, V2=V2):
                s.wait_ge(s_v, V2)
                if tb >= 1:
                    s.wait_ge(s_mm, ctr.get(s_mm))
                ctr.inc(s_s, 1)
                s.activation(xn_t[:rows, :], xt, AF.Identity,
                             bias=lnt[:rows, 29:30], scale=lnt[:rows, 28:29]).then_inc(s_s, 1)
            S2 = ctr.get(s_s)

            @blk.tensor
            def _(t, tb=tb, rows=rows, S2=S2, psl=psl):
                t.wait_ge(s_s, S2)
                t.wait_ge(s_ld, CONSTS)
                if tb >= 1:
                    t.wait_ge(s_s, ctr.get(s_s))
                mm = None
                for db in range(NDB):
                    pp = (ps_ln0 if db < 8 else ps_ln1)
                    mm = t.matmul(pp[:, (db % 8) * 128:(db % 8) * 128 + rows],
                                  xn_t[:rows, db * 128:(db + 1) * 128],
                                  c_ident[:rows, :rows], start=True, stop=True)
                ctr.inc(s_mm, 1)
                mm.then_inc(s_mm, 1)
            M1 = ctr.get(s_mm)

            @blk.scalar
            def _(s, tb=tb, rows=rows, M1=M1):
                s.wait_ge(s_mm, M1)
                s.copy(
                    xn_f.ap().rearrange("p (db c) -> p db c", db=NDB)[:, 0:8, tb * 128:tb * 128 + rows],
                    ps_ln0.ap().rearrange("p (db c) -> p db c", db=8)[:, :, :rows],
                )
                ctr.inc(s_s, 1)
                s.copy(
                    xn_f.ap().rearrange("p (db c) -> p db c", db=NDB)[:, 8:16, tb * 128:tb * 128 + rows],
                    ps_ln1.ap().rearrange("p (db c) -> p db c", db=8)[:, :, :rows],
                ).then_inc(s_s, 1)
            LN_HIST[tb] = ctr.get(s_s)
        XNF = ctr.get(s_s)

        wf32 = E1(nc.sbuf_tensor("wf32", [128, 2 * NDB * 128], F32))
        wbf = E1(nc.sbuf_tensor("wbf", [128, 2 * NDB * 128], BF16))
        pre_sb = E1(nc.sbuf_tensor("pre_sb", [128, 2 * TP], BF16))
        ps_pa_t = E1(nc.psum_tensor("ps_pa_t", [128, 1536], F32))
        ps_pa = [ps_pa_t[:, 0:TG], ps_pa_t[:, 512:512 + TG], ps_pa_t[:, 1024:1024 + TG]]
        EV_HIST = {}

        def wsrc(fb):
            if fb < NFQ:
                return wqkv_b[fb]
            if fb < NFQ + NFG:
                return wgate_b[fb - NFQ]
            return wba_in

        for fb in range(NFB):
            buf = fb % 2
            fcols = 128 if fb < NFB - 1 else 32
            wsl = wf32[:, buf * NDB * 128:][:, :NDB * fcols]
            wbl = wbf[:, buf * NDB * 128:][:, :NDB * fcols]
            psl = ps_pa
            prl = pre_sb[:fcols, buf * TP:buf * TP + TP]

            sl = ldp[fb % 2]

            @blk.sync
            def _(q, fb=fb, wsl=wsl, fcols=fcols, sl=sl):
                if fb == 0:
                    q.wait_ge(s_lda, ctr.get(s_lda))
                    q.wait_ge(s_ldb, ctr.get(s_ldb))
                if fb >= 2:
                    q.wait_ge(s_v, ctr.get(s_v))
                q.wait_ge(sl, ctr.get(sl))
                q.dma_start(out=wsl.rearrange("p (db f) -> p db f", f=fcols),
                            in_=wsrc(fb).rearrange("(db p) f -> p db f", p=128)).then_inc(sl, 16)
                ctr.inc(sl, 16)
            WL = ctr.get(sl)

            @blk.vector
            def _(v, fb=fb, wsl=wsl, wbl=wbl, WL=WL, sl=sl):
                v.wait_ge(sl, WL)
                if fb >= 2:
                    v.wait_ge(s_mm, ctr.get(s_mm))
                ctr.inc(s_v, 1)
                v.tensor_copy(wbl, wsl).then_inc(s_v, 1)
            WC = ctr.get(s_v)

            TG_NEED = {0: LN_HIST[2], 1: LN_HIST[5], 2: LN_HIST[8]}

            @blk.tensor
            def _(t, fb=fb, wbl=wbl, psl=psl, fcols=fcols, WC=WC):
                t.wait_ge(s_v, WC)
                if fb >= 1:
                    t.wait_ge(s_s, EV_HIST[fb - 1])
                mm = None
                for tg in range(3):
                    if fb < 1:
                        t.wait_ge(s_s, TG_NEED[tg])
                    for db in range(NDB):
                        mm = t.matmul(psl[tg][:fcols, :],
                                      wbl.rearrange("p (db f) -> p db f", db=NDB)[:, db, :],
                                      xn_f.ap().rearrange("p (db c) -> p db c", db=NDB)[:, db, tg * TG:(tg + 1) * TG],
                                      start=(db == 0), stop=(db == NDB - 1))
                ctr.inc(s_mm, 1)
                mm.then_inc(s_mm, 1)
            PM = ctr.get(s_mm)

            @blk.scalar
            def _(s, fb=fb, psl=psl, prl=prl, fcols=fcols, PM=PM):
                s.wait_ge(s_mm, PM)
                if fb >= 2:
                    s.wait_ge(stp[fb % 2], ctr.get(stp[fb % 2]))
                cp = None
                for tg in range(3):
                    cp = s.copy(prl[:, tg * TG:(tg + 1) * TG], psl[tg][:fcols, :])
                ctr.inc(s_s, 1)
                cp.then_inc(s_s, 1)
            PE = ctr.get(s_s)
            EV_HIST[fb] = PE

            @blk.gpsimd
            def _(g, fb=fb, prl=prl, fcols=fcols, PE=PE):
                g.wait_ge(s_s, PE)
                g.wait_ge(stp[fb % 2], ctr.get(stp[fb % 2]))
                g.dma_start(out=pre_scr[fb, :fcols, :], in_=prl).then_inc(stp[fb % 2], 16)
                ctr.inc(stp[fb % 2], 16)
            P1_STORE_HIST[fb] = (ctr.get(s_sta), ctr.get(s_stb))


def _p15(nc, blk, ctr, sems, env):
    s_ld, s_st, s_mm, s_v, s_s = sems["ld"], sems["st"], sems["mm"], sems["v"], sems["s"]
    s_lda, s_ldb, s_sta, s_stb = sems["lda"], sems["ldb"], sems["sta"], sems["stb"]
    ldp = [s_lda, s_ldb]
    pre_scr, c_convw, c_ident, c_onec, c_onerf = (
        env["pre_scr"], env["c_convw"], env["c_ident"], env["c_onec"], env["c_onerf"])
    qk_f, k_tok, v_tok, barows, c_la, c_zero, c_cols = (
        env["qk_f"], env["k_tok"], env["v_tok"], env["barows"], env["c_la"],
        env["c_zero"], env["c_cols"])
    from contextlib import ExitStack

    with ExitStack() as st15:
        E1 = st15.enter_context
        pre_t = E1(nc.sbuf_tensor("pre_t", [128, 2 * TP], BF16))
        cv = E1(nc.sbuf_tensor("cv", [128, TPC], F32))
        sil = E1(nc.sbuf_tensor("sil", [128, TPC], F32))
        sqb = E1(nc.sbuf_tensor("sqb", [128, TPC], BF16))
        rs_row = E1(nc.sbuf_tensor("rs_row", [1, TPC], F32))
        rs_f32 = E1(nc.sbuf_tensor("rs_f32", [1, TPC], F32))
        vtile = E1(nc.sbuf_tensor("vtile", [128, TPC], BF16))
        ps_l2 = E1(nc.psum_tensor("ps_l2", [1, TPC], F32))
        ps_bc = E1(nc.psum_tensor("ps_bc", [128, TPC], F32))
        ps_tr = E1(nc.psum_tensor("ps_tr", [128, TPC], F32))

        for fb in range(NFQ):
            buf = fb % 2
            pt = pre_t[:, buf * TP:buf * TP + TP]
            is_q = fb < 8
            is_k = 8 <= fb < 16
            head = fb if is_q else (fb - 8 if is_k else fb - 16)

            sl = ldp[fb % 2]

            @blk.sync
            def _(q, fb=fb, pt=pt, sl=sl):
                if fb == 0:
                    q.wait_ge(s_lda, ctr.get(s_lda))
                    q.wait_ge(s_ldb, ctr.get(s_ldb))
                q.wait_ge(s_sta, P1_STORE_HIST[fb][0])
                q.wait_ge(s_stb, P1_STORE_HIST[fb][1])
                if fb >= 2:
                    q.wait_ge(s_v, ctr.get(s_v))
                q.wait_ge(sl, ctr.get(sl))
                q.dma_start(out=pt, in_=pre_scr[fb, :, :]).then_inc(sl, 16)
                ctr.inc(sl, 16)
            LD = ctr.get(sl)

            @blk.vector
            def _(v, fb=fb, pt=pt, LD=LD, sl=sl):
                v.wait_ge(sl, LD)
                if fb > 0:
                    v.wait_ge(s_s, ctr.get(s_s))
                ins = v.tensor_scalar(cv[:, :], pt[:, 0:TPC], c_convw[:, fb * 4:fb * 4 + 1], None, ALU.mult)
                for j in range(1, 4):
                    ctr.inc(s_v, 1); ins.then_inc(s_v, 1); v.wait_ge(s_v, ctr.get(s_v))
                    ins = v.scalar_tensor_tensor(
                        out=cv[:, :], in0=pt[:, j:j + TPC],
                        scalar=c_convw[:, fb * 4 + j:fb * 4 + j + 1], in1=cv[:, :],
                        op0=ALU.mult, op1=ALU.add)
                ctr.inc(s_v, 1)
                ins.then_inc(s_v, 1)
            CV = ctr.get(s_v)

            if is_q or is_k:
                @blk.scalar
                def _(s, CV=CV, fb=fb):
                    s.wait_ge(s_v, CV)
                    if fb >= 2:
                        s.wait_ge(s_v, ctr.get(s_v))
                    ins = s.activation(sil[:, :], cv[:, :], AF.Silu)
                    ctr.inc(s_s, 1); ins.then_inc(s_s, 1); s.wait_ge(s_s, ctr.get(s_s))
                    ctr.inc(s_s, 1)
                    s.activation(sqb[:, :], sil[:, :], AF.Square).then_inc(s_s, 1)
                SQ = ctr.get(s_s)

                @blk.tensor
                def _(t, SQ=SQ, fb=fb):
                    t.wait_ge(s_s, SQ)
                    if fb >= 1:
                        t.wait_ge(s_s, ctr.get(s_s))
                    mm = None
                    for half in range(2):
                        mm = t.matmul(ps_l2[0:1, half * 512:(half + 1) * 512], c_onec[:, :],
                                      sqb[:, half * 512:(half + 1) * 512], start=True, stop=True)
                    ctr.inc(s_mm, 1)
                    mm.then_inc(s_mm, 1)
                MMS = ctr.get(s_mm)

                @blk.scalar
                def _(s, MMS=MMS):
                    s.wait_ge(s_mm, MMS)
                    ctr.inc(s_s, 1)
                    s.activation(rs_f32[0:1, :], ps_l2[0:1, :], AF.Sqrt, bias=c_cols[0:1, 0:1]).then_inc(s_s, 1)
                RS = ctr.get(s_s)

                @blk.vector
                def _(v, RS=RS):
                    v.wait_ge(s_s, RS)
                    ctr.inc(s_v, 1)
                    v.reciprocal(rs_row[0:1, :], rs_f32[0:1, :]).then_inc(s_v, 1)
                RR = ctr.get(s_v)

                @blk.tensor
                def _(t, RR=RR, fb=fb):
                    t.wait_ge(s_v, RR)
                    if fb >= 1:
                        t.wait_ge(s_v, ctr.get(s_v))
                    mm = None
                    for half in range(2):
                        mm = t.matmul(ps_bc[:, half * 512:(half + 1) * 512],
                                      c_onerf[0:1, :],
                                      rs_row[0:1, half * 512:(half + 1) * 512], start=True, stop=True)
                    ctr.inc(s_mm, 1)
                    mm.then_inc(s_mm, 1)
                BC = ctr.get(s_mm)

                @blk.vector
                def _(v, fb=fb, head=head, is_q=is_q, BC=BC):
                    v.wait_ge(s_mm, BC)
                    dst = qk_f.ap()[:, (head + (0 if is_q else 8)) * TPC:][:, :TPC]
                    ctr.inc(s_v, 1)
                    v.tensor_tensor(dst, sil[:, :], ps_bc[:, :], ALU.mult).then_inc(s_v, 1)
                QN = ctr.get(s_v)

                if is_k:
                    @blk.tensor
                    def _(t, head=head, QN=QN, fb=fb):
                        t.wait_ge(s_v, QN)
                        if fb >= 9:
                            t.wait_ge(s_s, ctr.get(s_s))
                        src = qk_f.ap()[:, (head + 8) * TPC:][:, :TPC]
                        mm = None
                        for ci in range(NCH):
                            mm = t.matmul(ps_tr[:, ci * 128:(ci + 1) * 128],
                                          src[:, ci * 128:(ci + 1) * 128],
                                          c_ident[:, :], start=True, stop=True)
                        ctr.inc(s_mm, 1)
                        mm.then_inc(s_mm, 1)
                    KT = ctr.get(s_mm)

                    @blk.scalar
                    def _(s, head=head, KT=KT):
                        s.wait_ge(s_mm, KT)
                        ctr.inc(s_s, 1)
                        s.copy(k_tok.ap().rearrange("p (ci q s) -> p ci q s", q=HQ, s=128)[:, :, head, :],
                               ps_tr.ap().rearrange("p (ci s) -> p ci s", s=128)).then_inc(s_s, 1)
            else:
                @blk.scalar
                def _(s, CV=CV, fb=fb):
                    s.wait_ge(s_v, CV)
                    if fb >= 17:
                        s.wait_ge(s_mm, ctr.get(s_mm))
                    ctr.inc(s_s, 1)
                    s.activation(vtile[:, :], cv[:, :], AF.Silu).then_inc(s_s, 1)
                VS = ctr.get(s_s)

                @blk.tensor
                def _(t, head=head, VS=VS, fb=fb):
                    t.wait_ge(s_s, VS)
                    if fb >= 17:
                        t.wait_ge(s_s, ctr.get(s_s))
                    mm = None
                    for ci in range(NCH):
                        mm = t.matmul(ps_tr[:, ci * 128:(ci + 1) * 128],
                                      vtile[:, ci * 128:(ci + 1) * 128],
                                      c_ident[:, :], start=True, stop=True)
                    ctr.inc(s_mm, 1)
                    mm.then_inc(s_mm, 1)
                VT = ctr.get(s_mm)

                @blk.scalar
                def _(s, head=head, VT=VT):
                    s.wait_ge(s_mm, VT)
                    ctr.inc(s_s, 1)
                    s.copy(v_tok.ap().rearrange("p (ci hh s) -> p ci hh s", hh=HV, s=128)[:, :, head, :],
                           ps_tr.ap().rearrange("p (ci s) -> p ci s", s=128)).then_inc(s_s, 1)

        ba_t = E1(nc.sbuf_tensor("ba_t", [16, TPC], BF16))
        aa_t = E1(nc.sbuf_tensor("aa_t", [16, TPC], BF16))
        tmp16 = E1(nc.sbuf_tensor("tmp16", [HV, TPC], F32))

        @blk.sync
        def _(q):
            q.wait_ge(s_lda, ctr.get(s_lda))
            q.wait_ge(s_ldb, ctr.get(s_ldb))
            q.wait_ge(s_sta, P1_STORE_HIST[NFB - 1][0])
            q.wait_ge(s_stb, P1_STORE_HIST[NFB - 1][1])
            q.dma_start(out=ba_t[:, :], in_=pre_scr[NFB - 1, 0:16, HALO:HALO + TPC]).then_inc(s_lda, 16)
            q.dma_start(out=aa_t[:, :], in_=pre_scr[NFB - 1, 16:32, HALO:HALO + TPC]).then_inc(s_lda, 32 - 16)
            ctr.inc(s_lda, 32)
        BA = ctr.get(s_lda)

        @blk.scalar
        def _(s, BA=BA):
            s.wait_ge(s_lda, BA)
            s.wait_ge(s_v, ctr.get(s_v))
            ins = s.activation(barows[:, 0:TPC], ba_t[:, :], AF.Sigmoid)
            ctr.inc(s_s, 1); ins.then_inc(s_s, 1); s.wait_ge(s_s, ctr.get(s_s))
            s.activation(barows[:, TPC:2 * TPC], barows[:, 0:TPC], AF.Ln)
            ins = s.activation(tmp16[:, :], aa_t[:, :], AF.Sigmoid, scale=-1.0, bias=c_la[:, 5:6])
            ctr.inc(s_s, 1); ins.then_inc(s_s, 1); s.wait_ge(s_s, ctr.get(s_s))
            ctr.inc(s_s, 1)
            s.activation(tmp16[:, :], tmp16[:, :], AF.Ln).then_inc(s_s, 1)
        SB = ctr.get(s_s)

        @blk.vector
        def _(v, SB=SB):
            v.wait_ge(s_s, SB)
            ins = v.tensor_scalar(barows[:, 2 * TPC:3 * TPC], tmp16[:, :], c_la[:, 4:5], None, ALU.mult)
            ctr.inc(s_v, 1); ins.then_inc(s_v, 1); v.wait_ge(s_v, ctr.get(s_v))
            ins = None
            for ci in range(NCH):
                ins = v.tensor_tensor_scan(
                    out=barows[:, 3 * TPC + ci * 128:3 * TPC + (ci + 1) * 128],
                    data0=barows[:, 2 * TPC + ci * 128:2 * TPC + (ci + 1) * 128],
                    data1=barows[:, 2 * TPC + ci * 128:2 * TPC + (ci + 1) * 128],
                    initial=c_zero[0:HV, 0:1],
                    op0=ALU.add, op1=ALU.bypass)
            ctr.inc(s_v, 1)
            ins.then_inc(s_v, 1)


def _p2a(nc, blk, ctr, sems, env):
    s_mm, s_v, s_s, s_g = sems["mm"], sems["v"], sems["s"], sems["g"]
    s_sta, s_stb = sems["sta"], sems["stb"]
    stp = [s_sta, s_stb]
    qk_f, k_tok, v_tok, barows = env["qk_f"], env["k_tok"], env["v_tok"], env["barows"]
    compP, compR = env["compP"], env["compR"]
    c_ident, c_identf, c_onerf = env["c_ident"], env["c_identf"], env["c_onerf"]
    c_maskLT, c_maskUT = env["c_maskLT"], env["c_maskUT"]
    spill = env["spill"]
    from contextlib import ExitStack

    W = HV * 128
    NQ4 = 4
    W2 = W // NQ4
    HH = HV // NQ4
    QH = HQ // NQ4

    with ExitStack() as sta_:
        E1 = sta_.enter_context
        mrow_all = E1(nc.sbuf_tensor("mrow_all", [HV, 128], F32))
        cols = E1(nc.sbuf_tensor("colsA", [128, 64], F32))
        rowstall = E1(nc.sbuf_tensor("rowstall", [1, 2 * W], F32))
        e_t = E1(nc.sbuf_tensor("e_t_all", [128, W], F32))
        e_tt = E1(nc.sbuf_tensor("e_tt_all", [128, W], F32))
        X = E1(nc.sbuf_tensor("X_all", [128, W], BF16))
        Xt = E1(nc.sbuf_tensor("Xt_all", [128, W], BF16))
        NT = E1(nc.sbuf_tensor("NT_all", [128, W], BF16))
        Hb = E1(nc.sbuf_tensor("Hb_all", [128, W], BF16))
        G_sb = E1(nc.sbuf_tensor("G_all", [128, HQ * 128], BF16))
        bVt = E1(nc.sbuf_tensor("bVt_all", [128, W], BF16))
        bAKt = E1(nc.sbuf_tensor("bAKt_all", [128, W], BF16))
        kht = E1(nc.sbuf_tensor("kht_all", [128, W], BF16))
        U0 = E1(nc.sbuf_tensor("U0_all", [128, W], BF16))
        Yc = E1(nc.sbuf_tensor("Yc_all", [128, W], BF16))
        Rsb = E1(nc.sbuf_tensor("Rsb_all", [128, W], BF16))
        PT = E1(nc.sbuf_tensor("PT_all", [128, W], BF16))
        acb = E1(nc.sbuf_tensor("acbA", [128, 48], F32))
        psA = [E1(nc.psum_tensor(f"ps_A{q}", [128, W2], F32)) for q in range(NQ4)]
        psB = [E1(nc.psum_tensor(f"ps_B{q}", [128, W2], F32)) for q in range(NQ4)]

        ROWS_V = ctr.get(s_v)

        def colv(ap2):
            return ap2.unsqueeze(2).broadcast_to((128, HH, 128))

        def mv(ap2):
            return ap2.unsqueeze(1).broadcast_to((128, HH, 128))

        def gv(ap3):
            return ap3.unsqueeze(2).broadcast_to((128, QH, 2, 128))

        def h3(ap):
            return ap.rearrange("p (h s) -> p h s", s=128)

        for ci in range(NCH):
            ell_r = barows[:, 3 * TPC + ci * 128:3 * TPC + (ci + 1) * 128]
            lb_r = barows[:, TPC + ci * 128:TPC + (ci + 1) * 128]
            beta_r = barows[:, ci * 128:(ci + 1) * 128]

            @blk.vector
            def _(v, ell_r=ell_r, lb_r=lb_r, ci=ci):
                v.wait_ge(s_v, ROWS_V)
                if ci > 0:
                    v.wait_ge(s_mm, ctr.get(s_mm))
                ctr.inc(s_v, 1)
                v.tensor_tensor(mrow_all[:, :], ell_r, lb_r, ALU.add).then_inc(s_v, 1)
            MR = ctr.get(s_v)

            @blk.tensor
            def _(t, ell_r=ell_r, beta_r=beta_r, MR=MR, ci=ci):
                t.wait_ge(s_v, MR)
                if ci > 0:
                    t.wait_ge(s_s, ctr.get(s_s))
                    t.wait_ge(s_v, ctr.get(s_v))
                t.matmul(psA[0][:, 0:16], ell_r, c_identf[0:HV, 0:HV], start=True, stop=True)
                t.matmul(psA[0][:, 16:32], mrow_all[:, :], c_identf[0:HV, 0:HV], start=True, stop=True)
                ctr.inc(s_mm, 1)
                t.matmul(psA[0][:, 32:48], beta_r, c_identf[0:HV, 0:HV], start=True, stop=True).then_inc(s_mm, 1)
            CT = ctr.get(s_mm)

            @blk.scalar
            def _(s, CT=CT):
                s.wait_ge(s_mm, CT)
                ctr.inc(s_s, 1)
                s.copy(cols[:, 0:48], psA[0][:, 0:48]).then_inc(s_s, 1)
            COLS = ctr.get(s_s)

            st = [dict() for _ in range(NQ4)]

            def S_rows(q2):
                d = st[q2]
                @blk.tensor
                def _(t, q2=q2):
                    t.wait_ge(s_s, COLS)
                    mm = None
                    for i in range(HH):
                        h = q2 * HH + i
                        mm = t.matmul(psA[q2][0:1, i * 128:(i + 1) * 128], cols[:, h:h + 1],
                                      c_identf[:, :], start=True, stop=True)
                        mm = t.matmul(psB[q2][0:1, i * 128:(i + 1) * 128], cols[:, 16 + h:16 + h + 1],
                                      c_identf[:, :], start=True, stop=True)
                    ctr.inc(s_mm, 1)
                    mm.then_inc(s_mm, 1)
                d["RWM"] = ctr.get(s_mm)

            def S_rowse(q2):
                d = st[q2]
                @blk.scalar
                def _(s, q2=q2):
                    s.wait_ge(s_mm, d["RWM"])
                    s.copy(rowstall[0:1, q2 * W2:(q2 + 1) * W2], psA[q2][0:1, 0:W2])
                    ctr.inc(s_s, 1)
                    s.copy(rowstall[0:1, W + q2 * W2:W + (q2 + 1) * W2], psB[q2][0:1, 0:W2]).then_inc(s_s, 1)
                d["RWS"] = ctr.get(s_s)

            def S_bcast(q2):
                d = st[q2]
                @blk.tensor
                def _(t, q2=q2):
                    t.wait_ge(s_s, d["RWS"])
                    mm = None
                    for qq in range(W2 // 512):
                        sl = slice(qq * 512, (qq + 1) * 512)
                        slr = slice(q2 * W2 + qq * 512, q2 * W2 + (qq + 1) * 512)
                        t.matmul(psA[q2][:, sl], c_onerf[0:1, :], rowstall[0:1, slr], start=True, stop=True)
                        mm = t.matmul(psB[q2][:, sl], c_onerf[0:1, :],
                                      rowstall[0:1, W + q2 * W2 + qq * 512:W + q2 * W2 + (qq + 1) * 512],
                                      start=True, stop=True)
                    ctr.inc(s_mm, 1)
                    mm.then_inc(s_mm, 1)
                d["BC"] = ctr.get(s_mm)

            def S_ea(q2):
                d = st[q2]
                sl2 = slice(q2 * W2, (q2 + 1) * W2)
                cl = slice(q2 * HH, q2 * HH + HH)
                @blk.vector
                def _(v, q2=q2, sl2=sl2, cl=cl):
                    v.wait_ge(s_mm, d["BC"])
                    ins = v.tensor_tensor(h3(e_t.ap()[:, sl2]), colv(cols[:, 16 + q2 * HH:16 + q2 * HH + HH]),
                                          h3(psA[q2].ap()[:, 0:W2]), ALU.subtract)
                    ins = v.tensor_tensor(h3(e_tt.ap()[:, sl2]), h3(psB[q2].ap()[:, 0:W2]),
                                          colv(cols[:, q2 * HH:q2 * HH + HH]), ALU.subtract)
                    ins = v.tensor_tensor(acb[:, 16 + q2 * HH:16 + q2 * HH + HH], psA[q2].ap()[:, 127::128],
                                          cols[:, q2 * HH:q2 * HH + HH], ALU.subtract)
                    ins = v.tensor_copy(acb[:, q2 * HH:q2 * HH + HH], psA[q2].ap()[:, 127::128])
                    ctr.inc(s_v, 1); ins.then_inc(s_v, 1); v.wait_ge(s_v, ctr.get(s_v))
                    v.tensor_tensor(h3(e_t.ap()[:, sl2]), h3(e_t.ap()[:, sl2]), mv(c_maskLT.ap()), ALU.add)
                    ctr.inc(s_v, 1)
                    v.tensor_tensor(h3(e_tt.ap()[:, sl2]), h3(e_tt.ap()[:, sl2]), mv(c_maskUT.ap()), ALU.add).then_inc(s_v, 1)
                d["EA"] = ctr.get(s_v)

            def S_exp(q2):
                d = st[q2]
                sl2 = slice(q2 * W2, (q2 + 1) * W2)
                @blk.scalar
                def _(s, q2=q2, sl2=sl2):
                    s.wait_ge(s_v, d["EA"])
                    s.activation(e_t[:, sl2], e_t[:, sl2], AF.Exp)
                    s.activation(e_tt[:, sl2], e_tt[:, sl2], AF.Exp)
                    s.activation(acb[:, 32 + q2 * HH:32 + q2 * HH + HH], cols[:, 16 + q2 * HH:16 + q2 * HH + HH], AF.Exp)
                    ins = s.activation(acb[:, 16 + q2 * HH:16 + q2 * HH + HH], acb[:, 16 + q2 * HH:16 + q2 * HH + HH], AF.Exp)
                    ctr.inc(s_s, 1); ins.then_inc(s_s, 1); s.wait_ge(s_s, ctr.get(s_s))
                    ctr.inc(s_s, 1)
                    s.activation(acb[:, q2 * HH:q2 * HH + HH], acb[:, q2 * HH:q2 * HH + HH], AF.Exp).then_inc(s_s, 1)
                d["EX"] = ctr.get(s_s)

            def S_g(q2):
                d = st[q2]
                @blk.tensor
                def _(t, q2=q2, ci=ci):
                    t.wait_ge(s_s, d["EX"])
                    mm = None
                    for i in range(QH):
                        hq = q2 * QH + i
                        kf = qk_f.ap()[:, (8 + hq) * TPC + ci * 128:][:, :128]
                        mm = t.matmul(psA[q2][:, i * 128:(i + 1) * 128], kf, kf, start=True, stop=True)
                    ctr.inc(s_mm, 1)
                    mm.then_inc(s_mm, 1)
                d["GM"] = ctr.get(s_mm)

            def S_ge(q2):
                d = st[q2]
                @blk.scalar
                def _(s, q2=q2):
                    s.wait_ge(s_mm, d["GM"])
                    ctr.inc(s_s, 1)
                    s.copy(G_sb[:, q2 * QH * 128:(q2 + 1) * QH * 128], psA[q2][:, 0:QH * 128]).then_inc(s_s, 1)
                d["GE"] = ctr.get(s_s)

            def S_tx(q2):
                d = st[q2]
                sl2 = slice(q2 * W2, (q2 + 1) * W2)
                gsl = G_sb.ap()[:, q2 * QH * 128:(q2 + 1) * QH * 128].rearrange("p (q s) -> p q s", s=128)
                @blk.vector
                def _(v, q2=q2, sl2=sl2, gsl=gsl):
                    v.wait_ge(s_s, d["GE"])
                    r4 = lambda ap: h3(ap).rearrange("p (q r) s -> p q r s", r=2)
                    ins = v.tensor_tensor(r4(e_t.ap()[:, sl2]), gv(gsl), r4(e_t.ap()[:, sl2]), ALU.mult)
                    ins = v.tensor_tensor(r4(e_tt.ap()[:, sl2]), gv(gsl), r4(e_tt.ap()[:, sl2]), ALU.mult)
                    ctr.inc(s_v, 1); ins.then_inc(s_v, 1); v.wait_ge(s_v, ctr.get(s_v))
                    v.tensor_tensor(h3(X.ap()[:, sl2]), mv(c_identf.ap()), h3(e_t.ap()[:, sl2]), ALU.subtract)
                    v.tensor_tensor(h3(Xt.ap()[:, sl2]), mv(c_identf.ap()), h3(e_tt.ap()[:, sl2]), ALU.subtract)
                    ctr.inc(s_v, 1)
                    v.tensor_tensor(h3(NT.ap()[:, sl2]), mv(c_identf.ap()), h3(e_tt.ap()[:, sl2]), ALU.add).then_inc(s_v, 1)
                d["TX"] = ctr.get(s_v)
                d["last_v"] = d["TX"]
                d["last_s"] = ctr.get(s_s)

            def S_newt_h(q2):
                d = st[q2]
                @blk.tensor
                def _(t, q2=q2):
                    t.wait_ge(s_v, d["last_v"])
                    t.wait_ge(s_s, d["last_s"])
                    mm = None
                    for i in range(HH):
                        o = q2 * W2 + i * 128
                        mm = t.matmul(psA[q2][:, i * 128:(i + 1) * 128], NT[:, o:o + 128],
                                      X[:, o:o + 128], start=True, stop=True)
                    ctr.inc(s_mm, 1)
                    mm.then_inc(s_mm, 1)
                d["HM"] = ctr.get(s_mm)

            def S_newt_he(q2):
                d = st[q2]
                @blk.scalar
                def _(s, q2=q2):
                    s.wait_ge(s_mm, d["HM"])
                    ctr.inc(s_s, 1)
                    s.copy(Hb[:, q2 * W2:(q2 + 1) * W2], psA[q2][:, :]).then_inc(s_s, 1)
                d["HE"] = ctr.get(s_s)

            def S_newt_xh(q2):
                d = st[q2]
                @blk.tensor
                def _(t, q2=q2):
                    t.wait_ge(s_s, d["HE"])
                    mm = None
                    for i in range(HH):
                        o = q2 * W2 + i * 128
                        isl = slice(i * 128, (i + 1) * 128)
                        t.matmul(psB[q2][:, isl], Xt[:, o:o + 128], Hb[:, o:o + 128], start=True, stop=True)
                        mm = t.matmul(psA[q2][:, isl], Hb[:, o:o + 128], Xt[:, o:o + 128], start=True, stop=True)
                    ctr.inc(s_mm, 1)
                    mm.then_inc(s_mm, 1)
                d["XHM"] = ctr.get(s_mm)

            def S_newt_xn(q2):
                d = st[q2]
                @blk.vector
                def _(v, q2=q2):
                    v.wait_ge(s_mm, d["XHM"])
                    sl2 = slice(q2 * W2, (q2 + 1) * W2)
                    v.scalar_tensor_tensor(out=X[:, sl2], in0=X[:, sl2], scalar=2.0,
                                           in1=psB[q2][:, :], op0=ALU.mult, op1=ALU.subtract)
                    ctr.inc(s_v, 1)
                    v.scalar_tensor_tensor(out=Xt[:, sl2], in0=Xt[:, sl2], scalar=2.0,
                                           in1=psA[q2][:, :], op0=ALU.mult, op1=ALU.subtract).then_inc(s_v, 1)
                d["last_v"] = ctr.get(s_v)
                d["last_s"] = ctr.get(s_s)

            def S_ops(q2):
                d = st[q2]
                sl2 = slice(q2 * W2, (q2 + 1) * W2)
                ktok_chunk = k_tok.ap()[:, ci * HQ * 128 + q2 * QH * 128:][:, :QH * 128].rearrange("p (q s) -> p q s", s=128)
                vtok_chunk = v_tok.ap()[:, ci * W + q2 * W2:][:, :W2]
                @blk.vector
                def _(v, q2=q2, sl2=sl2, ktok_chunk=ktok_chunk, vtok_chunk=vtok_chunk):
                    v.wait_ge(s_s, d["last_s"])
                    r4 = lambda ap: h3(ap).rearrange("p (q r) s -> p q r s", r=2)
                    cv4 = lambda ap2: colv(ap2).rearrange("p (q r) s -> p q r s", r=2)
                    v.tensor_tensor(h3(bVt.ap()[:, sl2]), h3(vtok_chunk), colv(cols[:, 32 + q2 * HH:32 + q2 * HH + HH]), ALU.mult)
                    v.tensor_tensor(r4(bAKt.ap()[:, sl2]), gv(ktok_chunk), cv4(acb[:, 32 + q2 * HH:32 + q2 * HH + HH]), ALU.mult)
                    ctr.inc(s_v, 1)
                    v.tensor_tensor(r4(kht.ap()[:, sl2]), gv(ktok_chunk), cv4(acb[:, 16 + q2 * HH:16 + q2 * HH + HH]), ALU.mult).then_inc(s_v, 1)
                d["OPS"] = ctr.get(s_v)

            def S_sol(q2):
                d = st[q2]
                @blk.tensor
                def _(t, q2=q2):
                    t.wait_ge(s_v, d["OPS"])
                    mm = None
                    for i in range(HH):
                        o = q2 * W2 + i * 128
                        isl = slice(i * 128, (i + 1) * 128)
                        t.matmul(psA[q2][:, isl], Xt[:, o:o + 128], bVt[:, o:o + 128], start=True, stop=True)
                        mm = t.matmul(psB[q2][:, isl], Xt[:, o:o + 128], bAKt[:, o:o + 128], start=True, stop=True)
                    ctr.inc(s_mm, 1)
                    mm.then_inc(s_mm, 1)
                d["SOL"] = ctr.get(s_mm)

            def S_uy(q2):
                d = st[q2]
                @blk.scalar
                def _(s, q2=q2):
                    s.wait_ge(s_mm, d["SOL"])
                    s.wait_ge(s_sta, ctr.get(s_sta))
                    s.wait_ge(s_stb, ctr.get(s_stb))
                    s.copy(U0[:, q2 * W2:(q2 + 1) * W2], psA[q2][:, :])
                    ctr.inc(s_s, 1)
                    s.copy(Yc[:, q2 * W2:(q2 + 1) * W2], psB[q2][:, :]).then_inc(s_s, 1)
                d["UY"] = ctr.get(s_s)

            def S_spl1(q2):
                d = st[q2]
                @blk.gpsimd
                def _(g, q2=q2, ci=ci):
                    g.wait_ge(s_s, d["UY"])
                    g.wait_ge(stp[q2 % 2], ctr.get(stp[q2 % 2]))
                    g.dma_start(out=spill[ci, 0, :, q2 * W2:(q2 + 1) * W2], in_=U0[:, q2 * W2:(q2 + 1) * W2]).then_inc(stp[q2 % 2], 16)
                    g.dma_start(out=spill[ci, 1, :, q2 * W2:(q2 + 1) * W2], in_=Yc[:, q2 * W2:(q2 + 1) * W2]).then_inc(stp[q2 % 2], 16)
                    ctr.inc(stp[q2 % 2], 32)

            def S_rp(q2):
                d = st[q2]
                @blk.tensor
                def _(t, q2=q2):
                    t.wait_ge(s_s, d["UY"])
                    mm = None
                    for i in range(HH):
                        o = q2 * W2 + i * 128
                        isl = slice(i * 128, (i + 1) * 128)
                        t.matmul(psA[q2][:, isl], kht[:, o:o + 128], U0[:, o:o + 128], start=True, stop=True)
                        mm = t.matmul(psB[q2][:, isl], Yc[:, o:o + 128], kht[:, o:o + 128], start=True, stop=True)
                    ctr.inc(s_mm, 1)
                    mm.then_inc(s_mm, 1)
                d["RP"] = ctr.get(s_mm)

            def S_prv(q2):
                d = st[q2]
                sl2 = slice(q2 * W2, (q2 + 1) * W2)
                @blk.vector
                def _(v, q2=q2, sl2=sl2):
                    v.wait_ge(s_mm, d["RP"])
                    v.wait_ge(s_sta, ctr.get(s_sta))
                    v.wait_ge(s_stb, ctr.get(s_stb))
                    v.tensor_copy(Rsb[:, sl2], psA[q2][:, :])
                    ins = v.tensor_tensor(h3(e_t.ap()[:, sl2]), mv(c_identf.ap()),
                                          colv(acb[:, q2 * HH:q2 * HH + HH]), ALU.mult)
                    ctr.inc(s_v, 1); ins.then_inc(s_v, 1); v.wait_ge(s_v, ctr.get(s_v))
                    ctr.inc(s_v, 1)
                    v.tensor_tensor(PT[:, sl2], e_t[:, sl2], psB[q2][:, :], ALU.subtract).then_inc(s_v, 1)
                d["PRV"] = ctr.get(s_v)

            def S_spl2(q2):
                d = st[q2]
                @blk.gpsimd
                def _(g, q2=q2, ci=ci):
                    g.wait_ge(s_v, d["PRV"])
                    g.wait_ge(stp[q2 % 2], ctr.get(stp[q2 % 2]))
                    g.dma_start(out=spill[ci, 2, :, q2 * W2:(q2 + 1) * W2], in_=PT[:, q2 * W2:(q2 + 1) * W2]).then_inc(stp[q2 % 2], 16)
                    g.dma_start(out=spill[ci, 3, :, q2 * W2:(q2 + 1) * W2], in_=Rsb[:, q2 * W2:(q2 + 1) * W2]).then_inc(stp[q2 % 2], 16)
                    ctr.inc(stp[q2 % 2], 32)

            def S_cm(q2):
                d = st[q2]
                sl2 = slice(q2 * W2, (q2 + 1) * W2)
                if ci == 0:
                    @blk.vector
                    def _(v, q2=q2, sl2=sl2):
                        v.wait_ge(s_v, d["PRV"])
                        v.tensor_copy(compP.ap()[:, sl2], PT[:, sl2])
                        ctr.inc(s_v, 1)
                        v.tensor_copy(compR.ap()[:, sl2], Rsb[:, sl2]).then_inc(s_v, 1)
                else:
                    @blk.tensor
                    def _(t, q2=q2):
                        t.wait_ge(s_v, d["PRV"])
                        mm = None
                        for i in range(HH):
                            o = q2 * W2 + i * 128
                            isl = slice(i * 128, (i + 1) * 128)
                            t.matmul(psA[q2][:, isl], PT[:, o:o + 128], compP.ap()[:, o:o + 128], start=True, stop=True)
                            mm = t.matmul(psB[q2][:, isl], PT[:, o:o + 128], compR.ap()[:, o:o + 128], start=True, stop=True)
                        ctr.inc(s_mm, 1)
                        mm.then_inc(s_mm, 1)
                    d["CM"] = ctr.get(s_mm)

                    @blk.vector
                    def _(v, q2=q2, sl2=sl2):
                        v.wait_ge(s_mm, d["CM"])
                        v.tensor_copy(compP.ap()[:, sl2], psA[q2][:, :])
                        ctr.inc(s_v, 1)
                        v.tensor_tensor(compR.ap()[:, sl2], psB[q2][:, :], Rsb[:, sl2], ALU.add).then_inc(s_v, 1)

            stages = [S_rows, S_rowse, S_bcast, S_ea, S_exp, S_g, S_ge, S_tx]
            for _it in range(NEWTON_ITERS):
                stages += [S_newt_h, S_newt_he, S_newt_xh, S_newt_xn]
            stages += [S_ops, S_sol, S_uy, S_spl1, S_rp, S_prv, S_spl2, S_cm]
            for stg in stages:
                for qq4 in range(NQ4):
                    stg(qq4)


def _cc_hop(nc, blk, ctr, sems, env):
    s_ld, s_st, s_mm, s_v, s_s, s_g, s_cc = (
        sems["ld"], sems["st"], sems["mm"], sems["v"], sems["s"], sems["g"], sems["cc"])
    s_lda, s_ldb = sems["lda"], sems["ldb"]
    ldp = [s_lda, s_ldb]
    compP, compR, state = env["compP"], env["compR"], env["state"]
    c_ident, c_hopm, c_hopimi = env["c_ident"], env["c_hopm"], env["c_hopimi"]
    cc_in, cc_out = env["cc_in"], env["cc_out"]
    from contextlib import ExitStack

    _phase_barrier(nc, blk, ctr, sems)
    with ExitStack() as stc:
        E1 = stc.enter_context
        cPt = E1(nc.sbuf_tensor("cPt", [128, HV * 128], BF16))
        hopbuf = E1(nc.sbuf_tensor("hopbuf", [128, 2 * HV * 2 * 128], BF16))
        Pm = E1(nc.sbuf_tensor("Pm", [128, HV * 128], BF16))
        ps_hopA = E1(nc.psum_tensor("ps_hopA", [128, HV * 128], F32))
        CPD = ctr.get(s_v)

        @blk.tensor
        def _(t, CPD=CPD):
            t.wait_ge(s_v, CPD)
            mm = None
            for h in range(HV):
                mm = t.matmul(ps_hopA[:, h * 128:(h + 1) * 128], compP.ap()[:, h * 128:(h + 1) * 128],
                              c_ident[:, :], start=True, stop=True)
            ctr.inc(s_mm, 1)
            mm.then_inc(s_mm, 1)
        TM = ctr.get(s_mm)

        @blk.scalar
        def _(s, TM=TM):
            s.wait_ge(s_mm, TM)
            ctr.inc(s_s, 1)
            s.copy(cPt[:, :], ps_hopA[:, :]).then_inc(s_s, 1)
        CPT = ctr.get(s_s)

        @blk.sync
        def _(q, CPT=CPT):
            q.wait_ge(s_s, CPT)
            q.wait_ge(s_lda, ctr.get(s_lda))
            q.wait_ge(s_ldb, ctr.get(s_ldb))
            q.dma_start(out=cc_in[:, 0:HV * 128], in_=cPt[:, :]).then_inc(s_lda, 16)
            q.dma_start(out=cc_in[:, HV * 128:], in_=compR[:, :]).then_inc(s_lda, 16)
            ctr.inc(s_lda, 32)
        CCI = ctr.get(s_lda)

        @blk.gpsimd
        def _(g, CCI=CCI):
            g.wait_ge(s_lda, CCI)
            ctr.inc(s_cc, 1)
            g.collective_compute(
                "AllGather", ALU.bypass,
                ins=[cc_in[:, :]], outs=[cc_out[:, :]],
                replica_groups=[list(range(8))],
            ).then_inc(s_cc)
            ctr.inc(s_g, 1)
            g.memset(state[:, :], 0.0).then_inc(s_g, 1)
        STZ = ctr.get(s_g)

        W = HV * 128
        for j in range(7):
            sl = ldp[j % 2]
            hb = hopbuf[:, (j % 2) * HV * 256:(j % 2 + 1) * HV * 256]

            @blk.sync
            def _(q, j=j, sl=sl, hb=hb):
                q.wait_ge(s_cc, ctr.get(s_cc))
                if j >= 2:
                    q.wait_ge(s_v, ctr.get(s_v))
                q.wait_ge(sl, ctr.get(sl))
                q.dma_start(out=hb, in_=cc_out[j * 128:(j + 1) * 128, :]).then_inc(sl, 16)
                ctr.inc(sl, 16)
            HL = ctr.get(sl)

            @blk.vector
            def _(v, j=j, sl=sl, hb=hb, HL=HL, STZ=STZ):
                v.wait_ge(sl, HL)
                v.wait_ge(s_g, STZ)
                if j > 0:
                    v.wait_ge(s_mm, ctr.get(s_mm))
                ins = v.tensor_scalar(Pm[:, :], hb[:, 0:W], c_hopm[:, j:j + 1], None, ALU.mult)
                ctr.inc(s_v, 1); ins.then_inc(s_v, 1); v.wait_ge(s_v, ctr.get(s_v))
                ctr.inc(s_v, 1)
                v.tensor_tensor(Pm.ap().rearrange("p (h f) -> p h f", f=128),
                                Pm.ap().rearrange("p (h f) -> p h f", f=128),
                                c_hopimi.ap().rearrange("p (jj f) -> p jj f", f=128)[:, j:j + 1, :].broadcast_to((128, HV, 128)),
                                ALU.add).then_inc(s_v, 1)
            PMV = ctr.get(s_v)

            @blk.tensor
            def _(t, PMV=PMV):
                t.wait_ge(s_v, PMV)
                t.wait_ge(s_s, ctr.get(s_s))
                mm = None
                for h in range(HV):
                    hsl = slice(h * 128, (h + 1) * 128)
                    mm = t.matmul(ps_hopA[:, hsl], Pm[:, hsl],
                                  state.ap()[:, hsl], start=True, stop=True)
                ctr.inc(s_mm, 1)
                mm.then_inc(s_mm, 1)
            SMM = ctr.get(s_mm)

            @blk.vector
            def _(v, j=j, hb=hb, SMM=SMM):
                v.wait_ge(s_mm, SMM)
                ins = v.tensor_scalar(state.ap()[:, :], hb[:, W:2 * W], c_hopm[:, j:j + 1], None, ALU.mult)
                ctr.inc(s_v, 1); ins.then_inc(s_v, 1); v.wait_ge(s_v, ctr.get(s_v))
                ctr.inc(s_v, 1)
                v.tensor_tensor(state.ap()[:, :], state.ap()[:, :], ps_hopA[:, :], ALU.add).then_inc(s_v, 1)
def _p2b(nc, blk, ctr, sems, env):
    s_mm, s_v, s_s, s_g = sems["mm"], sems["v"], sems["s"], sems["g"]
    s_lda, s_ldb, s_sta, s_stb = sems["lda"], sems["ldb"], sems["sta"], sems["stb"]
    ldp = [s_lda, s_ldb]
    stp = [s_sta, s_stb]
    qk_f, barows, state = env["qk_f"], env["barows"], env["state"]
    c_ident, c_identf, c_maskUD, c_rmsw, c_onerf = (
        env["c_ident"], env["c_identf"], env["c_maskUD"], env["c_rmsw"], env["c_onerf"])
    c_cols = env["c_cols"]
    spill, pre_scr, p_scr = env["spill"], env["pre_scr"], env["p_scr"]
    from contextlib import ExitStack

    W = HV * 128
    W2 = W // 2
    HH = HV // 2
    QH = HQ // 2

    _phase_barrier(nc, blk, ctr, sems)
    with ExitStack() as stb_:
        E1 = stb_.enter_context
        cols = E1(nc.sbuf_tensor("colsB", [128, 16], F32))
        rowstall = E1(nc.sbuf_tensor("rowstallB", [1, W], F32))
        e_m = E1(nc.sbuf_tensor("e_mB", [128, W], F32))
        qa = E1(nc.sbuf_tensor("qaB", [128, W], BF16))
        MT = E1(nc.sbuf_tensor("MTB", [128, W], BF16))
        qAf = E1(nc.sbuf_tensor("qAfB", [128, W], BF16))
        nMYT = E1(nc.sbuf_tensor("nMYTB", [128, W], BF16))
        U0 = E1(nc.sbuf_tensor("U0B", [128, W], BF16))
        Yc = E1(nc.sbuf_tensor("YcB", [128, W], BF16))
        PTl = E1(nc.sbuf_tensor("PTlB", [128, W], BF16))
        Rl = E1(nc.sbuf_tensor("RlB", [128, W], BF16))
        gt = E1(nc.sbuf_tensor("gtB", [128, 2 * W], BF16))
        Ofp = E1(nc.sbuf_tensor("OfpB", [128, W], F32))
        sqf = E1(nc.sbuf_tensor("sqfB", [128, W], F32))
        Onsb = E1(nc.sbuf_tensor("OnsbB", [128, W], BF16))
        gsil = E1(nc.sbuf_tensor("gsilB", [128, W], BF16))
        ptile = E1(nc.sbuf_tensor("ptileB", [128, 2 * W], BF16))
        rcol = E1(nc.sbuf_tensor("rcolB", [128, 16], F32))
        psA = [E1(nc.psum_tensor("ps_AB0", [128, W2], F32)), E1(nc.psum_tensor("ps_AB1", [128, W2], F32))]
        psB = [E1(nc.psum_tensor("ps_BB0", [128, W2], F32)), E1(nc.psum_tensor("ps_BB1", [128, W2], F32))]

        def colv(ap2):
            return ap2.unsqueeze(2).broadcast_to((128, HH, 128))

        def mv(ap2):
            return ap2.unsqueeze(1).broadcast_to((128, HH, 128))

        def gv(ap3):
            return ap3.unsqueeze(2).broadcast_to((128, QH, 2, 128))

        def h3(ap):
            return ap.rearrange("p (h s) -> p h s", s=128)

        for ci in range(NCH):
            buf = ci % 2
            ell_r = barows[:, 3 * TPC + ci * 128:3 * TPC + (ci + 1) * 128]
            gtl = gt[:, buf * W:buf * W + W]
            pt = ptile[:, buf * W:buf * W + W]

            @blk.sync
            def _(q, ci=ci, buf=buf, gtl=gtl):
                if ci < 2:
                    q.wait_ge(s_sta, ctr.get(s_sta))
                    q.wait_ge(s_stb, ctr.get(s_stb))
                q.wait_ge(s_mm, ctr.get(s_mm))
                q.wait_ge(s_s, ctr.get(s_s))
                q.wait_ge(s_v, ctr.get(s_v))
                sl = ldp[buf]
                q.wait_ge(sl, ctr.get(sl))
                q.dma_start(out=U0[:, :], in_=spill[ci, 0]).then_inc(sl, 16)
                q.dma_start(out=Yc[:, :], in_=spill[ci, 1]).then_inc(sl, 16)
                q.dma_start(out=PTl[:, :], in_=spill[ci, 2]).then_inc(sl, 16)
                q.dma_start(out=Rl[:, :], in_=spill[ci, 3]).then_inc(sl, 16)
                q.dma_start(out=gtl.rearrange("p (h s) -> p h s", s=128),
                            in_=pre_scr[NFQ:NFQ + HV, :, HALO + ci * 128:HALO + (ci + 1) * 128].rearrange("h p s -> p h s")).then_inc(sl, 16)
                ctr.inc(ldp[buf], 80)
            LDS = ctr.get(ldp[buf])

            @blk.tensor
            def _(t, ell_r=ell_r, ci=ci):
                t.wait_ge(s_s, ctr.get(s_s))
                t.wait_ge(s_v, ctr.get(s_v))
                ctr.inc(s_mm, 1)
                t.matmul(psA[0][:, 0:16], ell_r, c_identf[0:HV, 0:HV], start=True, stop=True).then_inc(s_mm, 1)
            CT = ctr.get(s_mm)

            @blk.scalar
            def _(s, CT=CT):
                s.wait_ge(s_mm, CT)
                ctr.inc(s_s, 1)
                s.copy(cols[:, 0:16], psA[0][:, 0:16]).then_inc(s_s, 1)
            COLS = ctr.get(s_s)

            st = [dict() for _ in range(2)]

            def S_rows(q2):
                d = st[q2]
                @blk.tensor
                def _(t, q2=q2):
                    t.wait_ge(s_s, COLS)
                    mm = None
                    for i in range(HH):
                        h = q2 * HH + i
                        mm = t.matmul(psB[q2][0:1, i * 128:(i + 1) * 128], cols[:, h:h + 1],
                                      c_identf[:, :], start=True, stop=True)
                    ctr.inc(s_mm, 1)
                    mm.then_inc(s_mm, 1)
                d["RWM"] = ctr.get(s_mm)

            def S_rowse(q2):
                d = st[q2]
                @blk.scalar
                def _(s, q2=q2):
                    s.wait_ge(s_mm, d["RWM"])
                    ctr.inc(s_s, 1)
                    s.copy(rowstall[0:1, q2 * W2:(q2 + 1) * W2], psB[q2][0:1, 0:W2]).then_inc(s_s, 1)
                d["RWS"] = ctr.get(s_s)

            def S_bcast(q2):
                d = st[q2]
                @blk.tensor
                def _(t, q2=q2):
                    t.wait_ge(s_s, d["RWS"])
                    mm = None
                    for qq in range(2):
                        sl = slice(qq * 512, (qq + 1) * 512)
                        mm = t.matmul(psA[q2][:, sl], c_onerf[0:1, :],
                                      rowstall[0:1, q2 * W2 + qq * 512:q2 * W2 + (qq + 1) * 512],
                                      start=True, stop=True)
                    ctr.inc(s_mm, 1)
                    mm.then_inc(s_mm, 1)
                d["BC"] = ctr.get(s_mm)

            def S_ea(q2):
                d = st[q2]
                sl2 = slice(q2 * W2, (q2 + 1) * W2)
                @blk.vector
                def _(v, q2=q2, sl2=sl2):
                    v.wait_ge(s_mm, d["BC"])
                    ins = v.tensor_tensor(h3(e_m.ap()[:, sl2]), h3(psA[q2].ap()[:, 0:W2]),
                                          colv(cols[:, q2 * HH:q2 * HH + HH]), ALU.subtract)
                    ctr.inc(s_v, 1); ins.then_inc(s_v, 1); v.wait_ge(s_v, ctr.get(s_v))
                    ctr.inc(s_v, 1)
                    v.tensor_tensor(h3(e_m.ap()[:, sl2]), h3(e_m.ap()[:, sl2]), mv(c_maskUD.ap()), ALU.add).then_inc(s_v, 1)
                d["EA"] = ctr.get(s_v)

            def S_exp(q2):
                d = st[q2]
                sl2 = slice(q2 * W2, (q2 + 1) * W2)
                @blk.scalar
                def _(s, q2=q2, sl2=sl2):
                    s.wait_ge(s_v, d["EA"])
                    ins = s.activation(qa[:, sl2], psA[q2][:, 0:W2], AF.Exp, bias=c_cols[:, 2:3])
                    ctr.inc(s_s, 1); ins.then_inc(s_s, 1); s.wait_ge(s_s, ctr.get(s_s))
                    ctr.inc(s_s, 1)
                    s.activation(e_m[:, sl2], e_m[:, sl2], AF.Exp).then_inc(s_s, 1)
                d["EX"] = ctr.get(s_s)

            def S_g(q2):
                d = st[q2]
                @blk.tensor
                def _(t, q2=q2, ci=ci):
                    t.wait_ge(s_s, d["EX"])
                    mm = None
                    for i in range(QH):
                        hq = q2 * QH + i
                        kf = qk_f.ap()[:, (8 + hq) * TPC + ci * 128:][:, :128]
                        qf = qk_f.ap()[:, hq * TPC + ci * 128:][:, :128]
                        mm = t.matmul(psA[q2][:, i * 128:(i + 1) * 128], kf, qf, start=True, stop=True)
                    ctr.inc(s_mm, 1)
                    mm.then_inc(s_mm, 1)
                d["GM"] = ctr.get(s_mm)

            def S_mt(q2):
                d = st[q2]
                sl2 = slice(q2 * W2, (q2 + 1) * W2)
                @blk.vector
                def _(v, q2=q2, sl2=sl2, ci=ci):
                    v.wait_ge(s_mm, d["GM"])
                    r4 = lambda ap: h3(ap).rearrange("p (q r) s -> p q r s", r=2)
                    ins = v.tensor_tensor(
                        r4(MT.ap()[:, sl2]),
                        psA[q2].ap()[:, 0:QH * 128].rearrange("p (q s) -> p q s", s=128).unsqueeze(2).broadcast_to((128, QH, 2, 128)),
                        r4(e_m.ap()[:, sl2]), ALU.mult)
                    qv = qk_f.ap().rearrange("p (hh t) -> p hh t", t=TPC)[:, q2 * QH:(q2 + 1) * QH, ci * 128:(ci + 1) * 128].unsqueeze(2).broadcast_to((128, QH, 2, 128))
                    ctr.inc(s_v, 1)
                    v.tensor_tensor(r4(qAf.ap()[:, sl2]), qv, r4(h3(qa.ap()[:, sl2]).rearrange("p (q r) s -> p q r s", r=2)) if False else h3(qa.ap()[:, sl2]).rearrange("p (q r) s -> p q r s", r=2), ALU.mult).then_inc(s_v, 1)
                d["MTV"] = ctr.get(s_v)

            def S_my(q2):
                d = st[q2]
                @blk.tensor
                def _(t, q2=q2):
                    t.wait_ge(s_v, d["MTV"])
                    t.wait_ge(ldp[buf], LDS)
                    mm = None
                    for i in range(HH):
                        o = q2 * W2 + i * 128
                        mm = t.matmul(psB[q2][:, i * 128:(i + 1) * 128], Yc[:, o:o + 128], MT[:, o:o + 128], start=True, stop=True)
                    ctr.inc(s_mm, 1)
                    mm.then_inc(s_mm, 1)
                d["MY"] = ctr.get(s_mm)

            def S_nm(q2):
                d = st[q2]
                @blk.scalar
                def _(s, q2=q2):
                    s.wait_ge(s_mm, d["MY"])
                    ctr.inc(s_s, 1)
                    s.activation(nMYT[:, q2 * W2:(q2 + 1) * W2], psB[q2][:, :], AF.Identity, scale=-1.0).then_inc(s_s, 1)
                d["NM"] = ctr.get(s_s)

            def S_o(q2):
                d = st[q2]
                @blk.tensor
                def _(t, q2=q2):
                    t.wait_ge(s_s, d["NM"])
                    mm = None
                    for i in range(HH):
                        o = q2 * W2 + i * 128
                        isl = slice(i * 128, (i + 1) * 128)
                        t.matmul(psB[q2][:, isl], MT[:, o:o + 128], U0[:, o:o + 128], start=True, stop=False)
                        t.matmul(psB[q2][:, isl], qAf[:, o:o + 128], state.ap()[:, o:o + 128], start=False, stop=False)
                        mm = t.matmul(psB[q2][:, isl], nMYT[:, o:o + 128], state.ap()[:, o:o + 128], start=False, stop=True)
                    ctr.inc(s_mm, 1)
                    mm.then_inc(s_mm, 1)
                d["OM"] = ctr.get(s_mm)

            def S_oe(q2):
                d = st[q2]
                sl2 = slice(q2 * W2, (q2 + 1) * W2)
                @blk.scalar
                def _(s, q2=q2, sl2=sl2):
                    s.wait_ge(s_mm, d["OM"])
                    ins = s.copy(Ofp[:, sl2], psB[q2][:, :])
                    ctr.inc(s_s, 1); ins.then_inc(s_s, 1); s.wait_ge(s_s, ctr.get(s_s))
                    ctr.inc(s_s, 1)
                    s.activation(sqf[:, sl2], Ofp[:, sl2], AF.Square).then_inc(s_s, 1)
                d["OE"] = ctr.get(s_s)

            def S_rm(q2):
                d = st[q2]
                sl2 = slice(q2 * W2, (q2 + 1) * W2)
                cl = slice(q2 * HH, q2 * HH + HH)
                @blk.vector
                def _(v, q2=q2, sl2=sl2, cl=cl):
                    v.wait_ge(s_s, d["OE"])
                    ins = v.tensor_reduce(rcol[:, cl].unsqueeze(2), h3(sqf.ap()[:, sl2]), AX.X, ALU.add)
                    ctr.inc(s_v, 1); ins.then_inc(s_v, 1); v.wait_ge(s_v, ctr.get(s_v))
                    ctr.inc(s_v, 1)
                    v.tensor_scalar(rcol[:, cl], rcol[:, cl], 1.0 / DV, EPS_RMS, ALU.mult, ALU.add).then_inc(s_v, 1)
                d["RM1"] = ctr.get(s_v)

            def S_sq(q2):
                d = st[q2]
                cl = slice(q2 * HH, q2 * HH + HH)
                @blk.scalar
                def _(s, q2=q2, cl=cl):
                    s.wait_ge(s_v, d["RM1"])
                    ctr.inc(s_s, 1)
                    s.activation(rcol[:, cl], rcol[:, cl], AF.Sqrt).then_inc(s_s, 1)
                d["RM2"] = ctr.get(s_s)

            def S_on(q2):
                d = st[q2]
                sl2 = slice(q2 * W2, (q2 + 1) * W2)
                cl = slice(q2 * HH, q2 * HH + HH)
                @blk.vector
                def _(v, q2=q2, sl2=sl2, cl=cl):
                    v.wait_ge(s_s, d["RM2"])
                    ins = v.reciprocal(rcol[:, cl], rcol[:, cl])
                    ctr.inc(s_v, 1); ins.then_inc(s_v, 1); v.wait_ge(s_v, ctr.get(s_v))
                    ctr.inc(s_v, 1)
                    v.tensor_tensor(h3(Onsb.ap()[:, sl2]), h3(Ofp.ap()[:, sl2]), colv(rcol[:, cl]), ALU.mult).then_inc(s_v, 1)
                d["ONV"] = ctr.get(s_v)

            def S_ot(q2):
                d = st[q2]
                @blk.tensor
                def _(t, q2=q2):
                    t.wait_ge(s_v, d["ONV"])
                    t.wait_ge(s_s, ctr.get(s_s))
                    mm = None
                    for i in range(HH):
                        o = q2 * W2 + i * 128
                        mm = t.matmul(psA[q2][:, i * 128:(i + 1) * 128], Onsb[:, o:o + 128], c_ident[:, :], start=True, stop=True)
                    ctr.inc(s_mm, 1)
                    mm.then_inc(s_mm, 1)
                d["OT"] = ctr.get(s_mm)

            def S_pg(q2):
                d = st[q2]
                sl2 = slice(q2 * W2, (q2 + 1) * W2)
                @blk.scalar
                def _(s, q2=q2, sl2=sl2, gtl=gtl):
                    s.wait_ge(s_mm, d["OT"])
                    ins = s.activation(gsil[:, sl2], gtl[:, sl2], AF.Silu)
                    ctr.inc(s_s, 1); ins.then_inc(s_s, 1); s.wait_ge(s_s, ctr.get(s_s))
                    ctr.inc(s_s, 1)
                    s.activation(Onsb[:, sl2], psA[q2][:, :], AF.Identity, scale=c_rmsw[:, 0:1]).then_inc(s_s, 1)
                d["PS1"] = ctr.get(s_s)

            def S_pv(q2):
                d = st[q2]
                sl2 = slice(q2 * W2, (q2 + 1) * W2)
                @blk.vector
                def _(v, q2=q2, sl2=sl2, pt=pt, buf=buf):
                    v.wait_ge(s_s, d["PS1"])
                    v.wait_ge(stp[buf], ctr.get(stp[buf]))
                    ctr.inc(s_v, 1)
                    v.tensor_tensor(pt[:, sl2], Onsb[:, sl2], gsil[:, sl2], ALU.mult).then_inc(s_v, 1)
                d["PV"] = ctr.get(s_v)

            def S_st(q2):
                d = st[q2]
                @blk.tensor
                def _(t, q2=q2):
                    t.wait_ge(s_s, ctr.get(s_s))
                    mm = None
                    for i in range(HH):
                        o = q2 * W2 + i * 128
                        mm = t.matmul(psB[q2][:, i * 128:(i + 1) * 128], PTl[:, o:o + 128], state.ap()[:, o:o + 128], start=True, stop=True)
                    ctr.inc(s_mm, 1)
                    mm.then_inc(s_mm, 1)
                d["SM"] = ctr.get(s_mm)

            def S_sv(q2):
                d = st[q2]
                sl2 = slice(q2 * W2, (q2 + 1) * W2)
                @blk.vector
                def _(v, q2=q2, sl2=sl2):
                    v.wait_ge(s_mm, d["SM"])
                    ctr.inc(s_v, 1)
                    v.tensor_tensor(state.ap()[:, sl2], psB[q2][:, :], Rl[:, sl2], ALU.add).then_inc(s_v, 1)

            stages = [S_rows, S_rowse, S_bcast, S_ea, S_exp, S_g, S_mt, S_my, S_nm,
                      S_o, S_oe, S_rm, S_sq, S_on, S_ot, S_pg, S_pv, S_st, S_sv]
            for stg in stages:
                stg(0)
                stg(1)

            PVmax = max(st[0]["PV"], st[1]["PV"])

            @blk.gpsimd
            def _(g, ci=ci, pt=pt, PVmax=PVmax, buf=buf):
                g.wait_ge(s_v, PVmax)
                g.wait_ge(stp[buf], ctr.get(stp[buf]))
                g.dma_start(out=p_scr[ci], in_=pt).then_inc(stp[buf], 16)
                ctr.inc(stp[buf], 16)


def _p3(nc, blk, ctr, sems, env):
    s_mm, s_v, s_s = sems["mm"], sems["v"], sems["s"]
    s_lda, s_ldb, s_sta, s_stb = sems["lda"], sems["ldb"], sems["sta"], sems["stb"]
    ldp = [s_lda, s_ldb]
    stp = [s_sta, s_stb]
    c_ident, c_cols = env["c_ident"], env["c_cols"]
    x_in, wout_b, p_scr, x2_scr = env["x_in"], env["wout_b"], env["p_scr"], env["x2_scr"]
    xq_f = env["xq_f"]
    from contextlib import ExitStack

    _phase_barrier(nc, blk, ctr, sems)
    with ExitStack() as st3:
        E1 = st3.enter_context
        pcol = E1(nc.sbuf_tensor("pcol", [128, 2 * 2048], BF16))
        woutb = E1(nc.sbuf_tensor("woutb", [128, 2 * 16 * 512], BF16))
        attn_all = E1(nc.sbuf_tensor("attn_all", [128, 8 * D], BF16))
        lnt_b = E1(nc.sbuf_tensor("lnt3", [128, 2 * 40], F32))
        x2t = E1(nc.sbuf_tensor("x2t", [128, 2 * D], F32))
        xnt_b = E1(nc.sbuf_tensor("xnt3", [128, 2 * D], F32))
        xtl = E1(nc.sbuf_tensor("xtl3", [128, 2 * D], F32))
        inti_b = E1(nc.sbuf_tensor("inti", [128, 2 * D], I32))
        xq_t_b = E1(nc.sbuf_tensor("xq_t", [128, 2 * D], BF16))
        ps_op0 = E1(nc.psum_tensor("ps_op0", [128, 512], F32))
        ps_op1 = E1(nc.psum_tensor("ps_op1", [128, 512], F32))
        ps_big = E1(nc.psum_tensor("ps_big3", [128, D], F32))

        def emit_lnq(tb, ready):
            at = attn_all[:, tb * D:(tb + 1) * D]
            xt = xtl[:, (tb % 2) * D:(tb % 2) * D + D]
            x2 = x2t[:, (tb % 2) * D:(tb % 2) * D + D]
            lnt = lnt_b[:, (tb % 2) * 40:(tb % 2) * 40 + 40]
            xnt = xnt_b[:, (tb % 2) * D:(tb % 2) * D + D]
            inti = inti_b[:, (tb % 2) * D:(tb % 2) * D + D]
            xq_t = xq_t_b[:, (tb % 2) * D:(tb % 2) * D + D]

            @blk.sync
            def _(q, tb=tb, xt=xt):
                if tb >= 2:
                    q.wait_ge(s_v, ctr.get(s_v))
                q.wait_ge(ldp[tb % 2], ctr.get(ldp[tb % 2]))
                q.dma_start(out=xt, in_=x_in[HALO + tb * 128: HALO + (tb + 1) * 128, :]).then_inc(ldp[tb % 2], 16)
                ctr.inc(ldp[tb % 2], 16)
            XL = ctr.get(ldp[tb % 2])

            def emit_ln(src_ap, eps_val, dst_col):
                @blk.vector
                def _(v):
                    v.wait_ge(s_v, ctr.get(s_v))
                    ins = None
                    for i4 in range(4):
                        ins = v.bn_stats(lnt[:, i4 * 6:(i4 + 1) * 6], src_ap[:, i4 * 512:(i4 + 1) * 512])
                    ctr.inc(s_v, 1); ins.then_inc(s_v, 1); v.wait_ge(s_v, ctr.get(s_v))
                    ins = v.bn_aggr(lnt[:, 24:26], lnt[:, 0:24])
                    ctr.inc(s_v, 1); ins.then_inc(s_v, 1); v.wait_ge(s_v, ctr.get(s_v))
                    ctr.inc(s_v, 1)
                    v.tensor_scalar(lnt[:, 26:27], lnt[:, 25:26], eps_val, None, ALU.add).then_inc(s_v, 1)
                V1 = ctr.get(s_v)

                @blk.scalar
                def _(s, V1=V1):
                    s.wait_ge(s_v, V1)
                    ctr.inc(s_s, 1)
                    s.activation(lnt[:, 27:28], lnt[:, 26:27], AF.Sqrt).then_inc(s_s, 1)
                S1 = ctr.get(s_s)

                @blk.vector
                def _(v, S1=S1):
                    v.wait_ge(s_s, S1)
                    ins = v.reciprocal(lnt[:, dst_col:dst_col + 1], lnt[:, 27:28])
                    ctr.inc(s_v, 1); ins.then_inc(s_v, 1); v.wait_ge(s_v, ctr.get(s_v))
                    ctr.inc(s_v, 1)
                    v.scalar_tensor_tensor(out=lnt[:, dst_col + 1:dst_col + 2], in0=lnt[:, 24:25],
                                           scalar=-1.0, in1=lnt[:, dst_col:dst_col + 1],
                                           op0=ALU.mult, op1=ALU.mult).then_inc(s_v, 1)
                return ctr.get(s_v)

            @blk.vector
            def _(v, ready=ready):
                v.wait_ge(s_s, ready)
            LN1 = emit_ln(at, EPS_LN, 28)

            @blk.scalar
            def _(s, at=at, LN1=LN1):
                s.wait_ge(s_v, LN1)
                ctr.inc(s_s, 1)
                s.activation(xnt[:, :], at, AF.Identity,
                             bias=lnt[:, 29:30], scale=lnt[:, 28:29]).then_inc(s_s, 1)
            A1 = ctr.get(s_s)

            @blk.vector
            def _(v, x2=x2, xt=xt, A1=A1, XL=XL, tb=tb):
                v.wait_ge(s_s, A1)
                v.wait_ge(ldp[tb % 2], XL)
                v.wait_ge(stp[tb % 2], ctr.get(stp[tb % 2]))
                ctr.inc(s_v, 1)
                v.tensor_tensor(x2, xnt[:, :], xt, ALU.add).then_inc(s_v, 1)
            X2V = ctr.get(s_v)

            @blk.gpsimd
            def _(g, tb=tb, x2=x2, X2V=X2V):
                g.wait_ge(s_v, X2V)
                g.wait_ge(stp[tb % 2], ctr.get(stp[tb % 2]))
                g.dma_start(out=x2_scr[tb * 128:(tb + 1) * 128, :], in_=x2).then_inc(stp[tb % 2], 16)
                ctr.inc(stp[tb % 2], 16)

            LN2 = emit_ln(x2, EPS_FFN, 30)

            @blk.scalar
            def _(s, x2=x2, LN2=LN2):
                s.wait_ge(s_v, LN2)
                ctr.inc(s_s, 1)
                s.activation(xnt[:, :], x2, AF.Identity,
                             bias=lnt[:, 31:32], scale=lnt[:, 30:31]).then_inc(s_s, 1)
            A2 = ctr.get(s_s)

            @blk.vector
            def _(v, A2=A2):
                v.wait_ge(s_s, A2)
                ins = v.tensor_reduce(lnt[:, 32:33], xnt[:, :], AX.X, ALU.max, apply_absolute_value=True)
                ctr.inc(s_v, 1); ins.then_inc(s_v, 1); v.wait_ge(s_v, ctr.get(s_v))
                ins = v.reciprocal(lnt[:, 33:34], lnt[:, 32:33])
                ctr.inc(s_v, 1); ins.then_inc(s_v, 1); v.wait_ge(s_v, ctr.get(s_v))
                v.tensor_scalar(lnt[:, 34:35], lnt[:, 33:34], 127.0, None, ALU.mult)
                ins = v.tensor_scalar(lnt[:, 35:36], lnt[:, 32:33], 1.0 / 127.0, None, ALU.mult)
                ctr.inc(s_v, 1); ins.then_inc(s_v, 1); v.wait_ge(s_v, ctr.get(s_v))
                ins = v.tensor_scalar(xnt[:, :], xnt[:, :], lnt[:, 34:35], None, ALU.mult)
                ctr.inc(s_v, 1); ins.then_inc(s_v, 1); v.wait_ge(s_v, ctr.get(s_v))
                ins = v.tensor_copy(inti[:, :], xnt[:, :])
                ctr.inc(s_v, 1); ins.then_inc(s_v, 1); v.wait_ge(s_v, ctr.get(s_v))
                ctr.inc(s_v, 1)
                v.tensor_scalar(xq_t[:, :], inti[:, :], lnt[:, 35:36], None, ALU.mult).then_inc(s_v, 1)
            QV = ctr.get(s_v)

            @blk.tensor
            def _(t, tb=tb, QV=QV):
                t.wait_ge(s_v, QV)
                if tb > 0:
                    t.wait_ge(s_s, ctr.get(s_s))
                mm = None
                for db in range(NDB):
                    mm = t.matmul(ps_big[:, db * 128:(db + 1) * 128],
                                  xq_t[:, db * 128:(db + 1) * 128],
                                  c_ident[:, :], start=True, stop=True)
                ctr.inc(s_mm, 1)
                mm.then_inc(s_mm, 1)
            TQ = ctr.get(s_mm)

            @blk.scalar
            def _(s, tb=tb, TQ=TQ):
                s.wait_ge(s_mm, TQ)
                ctr.inc(s_s, 1)
                s.copy(
                    xq_f.ap().rearrange("p (db c) -> p db c", db=NDB)[:, :, tb * 128:(tb + 1) * 128],
                    ps_big.ap().rearrange("p (db c) -> p db c", db=NDB),
                ).then_inc(s_s, 1)



        for fg in range(4):
            wsl = woutb[:, (fg % 2) * 16 * 512:][:, :16 * 512]

            @blk.gpsimd
            def _(g, fg=fg, wsl=wsl):
                if fg >= 2:
                    g.wait_ge(s_mm, ctr.get(s_mm))
                g.wait_ge(stp[fg % 2], ctr.get(stp[fg % 2]))
                g.dma_start(out=wsl.rearrange("p (db f) -> p db f", f=512),
                            in_=wout_b[fg].rearrange("(db p) f -> p db f", p=128)).then_inc(stp[fg % 2], 16)
                ctr.inc(stp[fg % 2], 16)
            WL = ctr.get(stp[fg % 2])

            for tb in range(8):
                it = fg * 8 + tb
                pb = it % 2
                psl = ps_op0 if tb % 2 == 0 else ps_op1
                pc = pcol[:, pb * 2048:pb * 2048 + 2048]

                @blk.sync
                def _(q, tb=tb, pc=pc, pb=pb, it=it):
                    if it >= 2:
                        q.wait_ge(s_mm, ctr.get(s_mm))
                    q.wait_ge(ldp[pb], ctr.get(ldp[pb]))
                    q.dma_start(out=pc, in_=p_scr[tb]).then_inc(ldp[pb], 16)
                    ctr.inc(ldp[pb], 16)
                PL = ctr.get(ldp[pb])

                @blk.tensor
                def _(t, fg=fg, tb=tb, pc=pc, wsl=wsl, psl=psl, PL=PL, WL=WL, pb=pb):
                    t.wait_ge(ldp[pb], PL)
                    t.wait_ge(stp[fg % 2], WL)
                    if fg + tb > 0:
                        t.wait_ge(s_s, ctr.get(s_s))
                    mm = None
                    for db in range(NDB):
                        mm = t.matmul(psl[:, :],
                                      pc[:, db * 128:(db + 1) * 128],
                                      wsl.rearrange("p (db f) -> p db f", f=512)[:, db, :],
                                      start=(db == 0), stop=(db == NDB - 1))
                    ctr.inc(s_mm, 1)
                    mm.then_inc(s_mm, 1)
                OPM = ctr.get(s_mm)

                @blk.scalar
                def _(s, fg=fg, tb=tb, psl=psl, OPM=OPM):
                    s.wait_ge(s_mm, OPM)
                    ctr.inc(s_s, 1)
                    s.copy(attn_all[:, tb * D + fg * 512: tb * D + fg * 512 + 512], psl[:, :]).then_inc(s_s, 1)
                if fg == 3:
                    emit_lnq(tb, ctr.get(s_s))
        OPDONE = ctr.get(s_s)

def _ffn_gu(nc, blk, ctr, sems, env):
    s_mm, s_v, s_s = sems["mm"], sems["v"], sems["s"]
    s_lda, s_ldb, s_sta, s_stb = sems["lda"], sems["ldb"], sems["sta"], sems["stb"]
    ldp = [sems["ga"], sems["gb"]]
    xq_f, p2 = env["xq_f"], env["p2"]
    wg_b, wu_b = env["wg_b"], env["wu_b"]
    from contextlib import ExitStack

    _phase_barrier(nc, blk, ctr, sems)
    with ExitStack() as stf:
        E1 = stf.enter_context
        wgb = E1(nc.sbuf_tensor("wgb", [128, 2 * 2048], BF16))
        wub = E1(nc.sbuf_tensor("wub", [128, 2 * 2048], BF16))
        gs = E1(nc.sbuf_tensor("gs", [128, 2 * 512], BF16))
        ps_g0 = E1(nc.psum_tensor("ps_g0", [128, 512], F32))
        ps_g1 = E1(nc.psum_tensor("ps_g1", [128, 512], F32))
        ps_u0 = E1(nc.psum_tensor("ps_u0", [128, 512], F32))
        ps_u1 = E1(nc.psum_tensor("ps_u1", [128, 512], F32))

        mm_hist = {}
        load_done = {}
        sign_done = {}

        def emit_load(hb):
            buf = hb % 2
            wg = wgb[:, buf * 2048:buf * 2048 + 2048]
            wu = wub[:, buf * 2048:buf * 2048 + 2048]

            @blk.gpsimd
            def _(g, hb=hb, wg=wg, wu=wu, buf=buf):
                if hb >= 2:
                    g.wait_ge(s_mm, mm_hist[hb - 2])
                g.wait_ge(ldp[buf], ctr.get(ldp[buf]))
                g.dma_start(out=wg.rearrange("p (db f) -> p db f", f=128),
                            in_=wg_b[hb].rearrange("(db p) f -> p db f", p=128)).then_inc(ldp[buf], 16)
                g.dma_start(out=wu.rearrange("p (db f) -> p db f", f=128),
                            in_=wu_b[hb].rearrange("(db p) f -> p db f", p=128)).then_inc(ldp[buf], 16)
                ctr.inc(ldp[buf], 32)
            load_done[hb] = ctr.get(ldp[buf])

        def emit_sign(hb):
            buf = hb % 2
            wg = wgb[:, buf * 2048:buf * 2048 + 2048]
            wu = wub[:, buf * 2048:buf * 2048 + 2048]

            @blk.scalar
            def _(s, hb=hb, wg=wg, wu=wu, buf=buf):
                s.wait_ge(ldp[buf], load_done[hb])
                s.sign(wg, wg)
                ctr.inc(s_s, 1)
                s.sign(wu, wu).then_inc(s_s, 1)
            sign_done[hb] = ctr.get(s_s)

        emit_load(0)
        emit_load(1)
        emit_sign(0)

        for hb in range(NHB):
            buf = hb % 2
            wg = wgb[:, buf * 2048:buf * 2048 + 2048]
            wu = wub[:, buf * 2048:buf * 2048 + 2048]
            if hb + 1 < NHB:
                emit_sign(hb + 1)
            for th in range(2):
                psg = ps_g0 if th == 0 else ps_g1
                psu = ps_u0 if th == 0 else ps_u1

                @blk.tensor
                def _(t, hb=hb, th=th, wg=wg, wu=wu, psg=psg, psu=psu):
                    t.wait_ge(s_s, sign_done[hb])
                    if hb > 0:
                        t.wait_ge(s_v, ctr.get(s_v) - 1)
                    mm = None
                    for db in range(NDB):
                        mm = t.matmul(psg[:, :], wg[:, db * 128:(db + 1) * 128],
                                      xq_f.ap()[:, db * TPC + th * 512:][:, :512],
                                      start=(db == 0), stop=(db == NDB - 1))
                    for db in range(NDB):
                        mm = t.matmul(psu[:, :], wu[:, db * 128:(db + 1) * 128],
                                      xq_f.ap()[:, db * TPC + th * 512:][:, :512],
                                      start=(db == 0), stop=(db == NDB - 1))
                    ctr.inc(s_mm, 1)
                    mm.then_inc(s_mm, 1)
                GM = ctr.get(s_mm)

                @blk.scalar
                def _(s, psg=psg, GM=GM, th=th):
                    s.wait_ge(s_mm, GM)
                    ctr.inc(s_s, 1)
                    s.activation(gs[:, th * 512:(th + 1) * 512], psg[:, :], AF.Silu).then_inc(s_s, 1)
                GS = ctr.get(s_s)

                @blk.vector
                def _(v, hb=hb, th=th, psu=psu, GS=GS):
                    v.wait_ge(s_s, GS)
                    ctr.inc(s_v, 1)
                    v.tensor_tensor(p2.ap()[:, hb * TPC + th * 512:][:, :512],
                                    psu[:, :], gs[:, th * 512:(th + 1) * 512], ALU.mult).then_inc(s_v, 1)
            mm_hist[hb] = ctr.get(s_mm)
            if hb + 2 < NHB:
                emit_load(hb + 2)


def _ffn_down(nc, blk, ctr, sems, env):
    s_mm, s_v, s_s = sems["mm"], sems["v"], sems["s"]
    s_lda, s_ldb, s_sta, s_stb = sems["lda"], sems["ldb"], sems["sta"], sems["stb"]
    ldp = [sems["ga"], sems["gb"]]
    stp = [s_sta, s_stb]
    p2, wd_b, ffn_scr = env["p2"], env["wd_b"], env["ffn_scr"]
    from contextlib import ExitStack

    _phase_barrier(nc, blk, ctr, sems)
    with ExitStack() as std_:
        E1 = std_.enter_context
        wdb = E1(nc.sbuf_tensor("wdb", [128, 2 * NHB * 128], BF16))
        fout = E1(nc.sbuf_tensor("fout", [128, 2 * TPC], BF16))
        ps_d0 = E1(nc.psum_tensor("ps_d0", [128, TPC], F32))
        ps_d1 = E1(nc.psum_tensor("ps_d1", [128, TPC], F32))

        mm_hist = {}
        load_done = {}
        sign_done = {}
        fo_hist = {}

        def emit_load(fcb):
            buf = fcb % 2
            wd = wdb[:, buf * NHB * 128:][:, :NHB * 128]

            @blk.gpsimd
            def _(g, fcb=fcb, wd=wd, buf=buf):
                if fcb >= 2:
                    g.wait_ge(s_mm, mm_hist[fcb - 2])
                g.wait_ge(ldp[buf], ctr.get(ldp[buf]))
                g.dma_start(out=wd.rearrange("p (hb f) -> p hb f", f=128),
                            in_=wd_b[fcb].rearrange("(hb p) f -> p hb f", p=128)).then_inc(ldp[buf], 16)
                ctr.inc(ldp[buf], 16)
            load_done[fcb] = ctr.get(ldp[buf])

        def emit_sign(fcb):
            buf = fcb % 2
            wd = wdb[:, buf * NHB * 128:][:, :NHB * 128]

            @blk.scalar
            def _(s, fcb=fcb, wd=wd, buf=buf):
                s.wait_ge(ldp[buf], load_done[fcb])
                ctr.inc(s_s, 1)
                s.sign(wd, wd).then_inc(s_s, 1)
            sign_done[fcb] = ctr.get(s_s)

        emit_load(0)
        emit_load(1)
        emit_sign(0)

        for fcb in range(16):
            buf = fcb % 2
            wd = wdb[:, buf * NHB * 128:][:, :NHB * 128]
            psd = ps_d0 if fcb % 2 == 0 else ps_d1
            fo = fout[:, buf * TPC:buf * TPC + TPC]
            if fcb + 1 < 16:
                emit_sign(fcb + 1)

            @blk.tensor
            def _(t, fcb=fcb, wd=wd, psd=psd):
                t.wait_ge(s_s, sign_done[fcb])
                if fcb >= 2:
                    t.wait_ge(s_s, fo_hist[fcb - 2])
                mm = None
                for th in range(2):
                    for hb in range(NHB):
                        mm = t.matmul(psd[:, th * 512:(th + 1) * 512],
                                      wd[:, hb * 128:(hb + 1) * 128],
                                      p2.ap()[:, hb * TPC + th * 512:][:, :512],
                                      start=(hb == 0), stop=(hb == NHB - 1))
                ctr.inc(s_mm, 1)
                mm.then_inc(s_mm, 1)
            DM = ctr.get(s_mm)

            @blk.scalar
            def _(s, psd=psd, fo=fo, DM=DM, buf=buf, fcb=fcb):
                s.wait_ge(s_mm, DM)
                if fcb >= 2:
                    s.wait_ge(stp[buf], ctr.get(stp[buf]))
                ctr.inc(s_s, 1)
                s.copy(fo, psd[:, :]).then_inc(s_s, 1)
            FE = ctr.get(s_s)
            fo_hist[fcb] = FE

            @blk.gpsimd
            def _(g, fcb=fcb, fo=fo, FE=FE, buf=buf):
                g.wait_ge(s_s, FE)
                g.wait_ge(stp[buf], ctr.get(stp[buf]))
                g.dma_start(out=ffn_scr[fcb], in_=fo).then_inc(stp[buf], 16)
                ctr.inc(stp[buf], 16)
            mm_hist[fcb] = ctr.get(s_mm)
            if fcb + 2 < 16:
                emit_load(fcb + 2)


def _final(nc, blk, ctr, sems, env):
    s_mm, s_v, s_s = sems["mm"], sems["v"], sems["s"]
    s_lda, s_ldb, s_sta, s_stb = sems["lda"], sems["ldb"], sems["sta"], sems["stb"]
    ldp = [s_lda, s_ldb]
    stp = [s_sta, s_stb]
    c_ident = env["c_ident"]
    ffn_scr, x2_scr, y_out = env["ffn_scr"], env["x2_scr"], env["y_out"]
    from contextlib import ExitStack

    _phase_barrier(nc, blk, ctr, sems)
    with ExitStack() as stf:
        E1 = stf.enter_context
        ftile = E1(nc.sbuf_tensor("ftile", [128, 2 * D], BF16))
        x2l = E1(nc.sbuf_tensor("x2l", [128, 2 * D], F32))
        yt = E1(nc.sbuf_tensor("yt", [128, 2 * D], F32))
        lnt_b = E1(nc.sbuf_tensor("lntF", [128, 64], F32))
        ps_fin0 = E1(nc.psum_tensor("ps_fin0", [128, D], F32))
        ps_fin1 = E1(nc.psum_tensor("ps_fin1", [128, D], F32))

        for tb in range(8):
            buf = tb % 2
            ft = ftile[:, buf * D:buf * D + D]
            xx = x2l[:, buf * D:buf * D + D]
            yy = yt[:, buf * D:buf * D + D]
            lnt = lnt_b[:, buf * 32:buf * 32 + 32]
            psf = ps_fin0 if buf == 0 else ps_fin1

            @blk.sync
            def _(q, tb=tb, ft=ft, xx=xx, buf=buf):
                if tb >= 2:
                    q.wait_ge(s_mm, ctr.get(s_mm))
                    q.wait_ge(s_v, ctr.get(s_v))
                q.wait_ge(ldp[buf], ctr.get(ldp[buf]))
                q.dma_start(out=ft.rearrange("p (db t) -> p db t", t=128),
                            in_=ffn_scr[:, :, tb * 128:(tb + 1) * 128].rearrange("db p t -> p db t")).then_inc(ldp[buf], 16)
                q.dma_start(out=xx, in_=x2_scr[tb * 128:(tb + 1) * 128, :]).then_inc(ldp[buf], 16)
                ctr.inc(ldp[buf], 32)
            FL = ctr.get(ldp[buf])

            @blk.tensor
            def _(t, tb=tb, ft=ft, psf=psf, FL=FL, buf=buf):
                t.wait_ge(ldp[buf], FL)
                if tb >= 2:
                    t.wait_ge(s_s, ctr.get(s_s))
                mm = None
                for db in range(NDB):
                    mm = t.matmul(psf[:, db * 128:(db + 1) * 128],
                                  ft[:, db * 128:(db + 1) * 128],
                                  c_ident[:, :], start=True, stop=True)
                ctr.inc(s_mm, 1)
                mm.then_inc(s_mm, 1)
            FM = ctr.get(s_mm)

            @blk.vector
            def _(v, psf=psf, FM=FM):
                v.wait_ge(s_mm, FM)
                ins = None
                for i4 in range(4):
                    ins = v.bn_stats(lnt[:, i4 * 6:(i4 + 1) * 6], psf[:, i4 * 512:(i4 + 1) * 512])
                ctr.inc(s_v, 1); ins.then_inc(s_v, 1); v.wait_ge(s_v, ctr.get(s_v))
                ins = v.bn_aggr(lnt[:, 24:26], lnt[:, 0:24])
                ctr.inc(s_v, 1); ins.then_inc(s_v, 1); v.wait_ge(s_v, ctr.get(s_v))
                ctr.inc(s_v, 1)
                v.tensor_scalar(lnt[:, 26:27], lnt[:, 25:26], EPS_LN, None, ALU.add).then_inc(s_v, 1)
            V1 = ctr.get(s_v)

            @blk.scalar
            def _(s, V1=V1):
                s.wait_ge(s_v, V1)
                ctr.inc(s_s, 1)
                s.activation(lnt[:, 27:28], lnt[:, 26:27], AF.Sqrt).then_inc(s_s, 1)
            S1 = ctr.get(s_s)

            @blk.vector
            def _(v, S1=S1):
                v.wait_ge(s_s, S1)
                ins = v.reciprocal(lnt[:, 28:29], lnt[:, 27:28])
                ctr.inc(s_v, 1); ins.then_inc(s_v, 1); v.wait_ge(s_v, ctr.get(s_v))
                ctr.inc(s_v, 1)
                v.scalar_tensor_tensor(out=lnt[:, 29:30], in0=lnt[:, 24:25],
                                       scalar=-1.0, in1=lnt[:, 28:29],
                                       op0=ALU.mult, op1=ALU.mult).then_inc(s_v, 1)
            V2 = ctr.get(s_v)

            @blk.scalar
            def _(s, psf=psf, yy=yy, V2=V2, buf=buf, tb=tb):
                s.wait_ge(s_v, V2)
                if tb >= 2:
                    s.wait_ge(stp[buf], ctr.get(stp[buf]))
                ctr.inc(s_s, 1)
                s.activation(yy, psf[:, :], AF.Identity,
                             bias=lnt[:, 29:30], scale=lnt[:, 28:29]).then_inc(s_s, 1)
            S2 = ctr.get(s_s)

            @blk.vector
            def _(v, yy=yy, xx=xx, S2=S2):
                v.wait_ge(s_s, S2)
                ctr.inc(s_v, 1)
                v.tensor_tensor(yy, yy, xx, ALU.add).then_inc(s_v, 1)
            YV = ctr.get(s_v)

            @blk.gpsimd
            def _(g, tb=tb, yy=yy, YV=YV, buf=buf):
                g.wait_ge(s_v, YV)
                g.wait_ge(stp[buf], ctr.get(stp[buf]))
                g.dma_start(out=y_out[tb * 128:(tb + 1) * 128, :], in_=yy).then_inc(stp[buf], 16)
                ctr.inc(stp[buf], 16)

        @blk.sync
        def _(q):
            q.wait_ge(s_sta, ctr.get(s_sta))
            q.wait_ge(s_stb, ctr.get(s_stb))


def host_prep(inputs):
    x = np.ascontiguousarray(np.asarray(inputs["L0_attn_x"], np.float32))
    w_qkv = np.asarray(inputs["L0_attn_w_qkv"], np.float32)
    w_gate = np.asarray(inputs["L0_attn_w_gate"], np.float32)
    w_beta = np.asarray(inputs["L0_attn_w_beta"], np.float32)
    w_alpha = np.asarray(inputs["L0_attn_w_alpha"], np.float32)
    log_A = np.asarray(inputs["L0_attn_log_A"], np.float32)
    dt_bias = np.asarray(inputs["L0_attn_dt_bias"], np.float32)
    conv_w = np.asarray(inputs["L0_attn_conv_w"], np.float32)
    rms_w = np.asarray(inputs["L0_attn_rms_w"], np.float32)
    w_out = np.asarray(inputs["L0_attn_w_out"], np.float32)
    W_g = np.asarray(inputs["L0_ffn_W_g"], np.float32)
    W_u = np.asarray(inputs["L0_ffn_W_u"], np.float32)
    W_d = np.asarray(inputs["L0_ffn_W_d"], np.float32)

    wqkv_bl = np.ascontiguousarray(w_qkv.T.reshape(D, NFQ, 128).transpose(1, 0, 2))
    wgate_bl = np.ascontiguousarray(w_gate.T.reshape(D, NFG, 128).transpose(1, 0, 2))
    wba = np.ascontiguousarray(np.concatenate([w_beta, w_alpha], axis=0).T)
    convw = np.ascontiguousarray(
        conv_w[:, 0, :].reshape(NFQ, 128, 4).transpose(1, 0, 2).reshape(128, NFQ * 4))
    la = np.ascontiguousarray(np.stack([log_A, dt_bias], axis=1))
    rmsw = np.ascontiguousarray(rms_w[:, None])
    wout_bl = np.ascontiguousarray(w_out.T.reshape(D, 4, 512).transpose(1, 0, 2))
    wg_bl = np.ascontiguousarray(W_g.T.reshape(D, NHB, 128).transpose(1, 0, 2))
    wu_bl = np.ascontiguousarray(W_u.T.reshape(D, NHB, 128).transpose(1, 0, 2))
    wd_bl = np.ascontiguousarray(W_d.T.reshape(FFH, 16, 128).transpose(1, 0, 2))

    ident_bf = np.eye(128, dtype=NPBF16)
    ident_f = np.eye(128, dtype=np.float32)
    r_idx, c_idx = np.meshgrid(np.arange(128), np.arange(128), indexing="ij")
    maskLT = np.where(c_idx < r_idx, 0.0, NEG).astype(np.float32)
    maskUT = np.where(r_idx < c_idx, 0.0, NEG).astype(np.float32)
    maskUD = np.where(r_idx <= c_idx, -LSD, NEG).astype(np.float32)
    ones_col = np.ones((128, 1), dtype=NPBF16)
    ones_row_f = np.ones((1, 128), dtype=np.float32)
    zeros_f = np.zeros((128, 128), dtype=np.float32)
    ccols = np.zeros((128, 8), np.float32)
    ccols[:, 0] = EPS_L2
    ccols[:, 1] = EPS_RMS
    ccols[:, 2] = -LSD
    ccols[:, 3] = 2.0
    ccols[:, 4] = EPS_LN
    ccols[:, 5] = EPS_FFN
    ccols[:, 6] = 127.0
    ccols[:, 7] = 1.0

    in_maps = []
    for core in range(8):
        b, g = core // 4, core % 4
        t0 = g * TPC
        xs = np.zeros((TP, D), np.float32)
        lo = max(0, t0 - HALO)
        xs[HALO - (t0 - lo): HALO - (t0 - lo) + (t0 + TPC - lo)] = x[b, lo:t0 + TPC]
        hop_m = np.zeros((128, 7), np.float32)
        hop_imi = np.zeros((128, 7 * 128), np.float32)
        for j in range(7):
            use = 1.0 if (j // 4 == b and j % 4 < g) else 0.0
            hop_m[:, j] = use
            hop_imi[:, j * 128:(j + 1) * 128] = (1.0 - use) * np.eye(128, dtype=np.float32)
        in_maps.append({
            "x": xs, "wqkv_b": wqkv_bl, "wgate_b": wgate_bl, "wba": wba,
            "convw": convw, "loga_params": la, "rmsw": rmsw,
            "wout_b": wout_bl, "wg_b": wg_bl, "wu_b": wu_bl, "wd_b": wd_bl,
            "ident_bf": ident_bf, "ident_f": ident_f, "maskLT": maskLT,
            "maskUT": maskUT, "maskUD": maskUD,
            "ones_col_bf": ones_col, "ones_row_f": ones_row_f, "ccols": ccols,
            "ones_row_bf": np.ones((1, 128), dtype=NPBF16), "zeros_f": zeros_f,
            "hop_mcol": hop_m, "hop_imi": hop_imi,
        })
    return in_maps


def kernel(**inputs):
    in_maps = host_prep(inputs)
    nc, _ = build()
    res = run_bass_kernel_spmd(nc, in_maps, core_ids=list(range(8)))
    y = np.empty((2, 4096, D), np.float32)
    for core in range(8):
        b, g = core // 4, core % 4
        y[b, g * TPC:(g + 1) * TPC] = res.results[core]["y"]
    return y


# version 16
# speedup vs baseline: 2.9984x; 1.0374x over previous
import numpy as np
import ml_dtypes
import concourse.bass as bass
import concourse.mybir as mybir
from concourse.bass_utils import run_bass_kernel_spmd
from concourse import library_config

F32 = mybir.dt.float32
BF16 = mybir.dt.bfloat16
I32 = mybir.dt.int32
AF = mybir.ActivationFunctionType
ALU = mybir.AluOpType
AX = mybir.AxisListType
NPBF16 = ml_dtypes.bfloat16

D = 2048
TPC = 1024
HALO = 3
TP = 1032
TG = 344
C = 128
NCH = 8
HV, HQ, DK, DV = 16, 8, 128, 128
NDB = 16
NFQ, NFG = 32, 16
NFB = NFQ + NFG + 1
FFH = 8192
NHB = 64
NEG = -1e30
EPS_LN = 1e-5
EPS_L2 = 1e-6
EPS_RMS = 1e-6
EPS_FFN = 1e-8
LSD = float(np.log(np.sqrt(DK)))
NEWTON_ITERS = 6


class Ctr:
    def __init__(self):
        self.v = {}

    def inc(self, sem, k=1):
        self.v[sem.name] = self.v.get(sem.name, 0) + k
        return self.v[sem.name]

    def get(self, sem):
        return self.v.get(sem.name, 0)


def build(probe=None, upto="all"):
    nc = bass.Bass("TRN2", target_bir_lowering=False, debug=False)

    x_in = nc.dram_tensor("x", [TP, D], F32, kind="ExternalInput").ap()
    wqkv_b = nc.dram_tensor("wqkv_b", [NFQ, D, 128], F32, kind="ExternalInput").ap()
    wgate_b = nc.dram_tensor("wgate_b", [NFG, D, 128], F32, kind="ExternalInput").ap()
    wba_in = nc.dram_tensor("wba", [D, 32], F32, kind="ExternalInput").ap()
    convw_in = nc.dram_tensor("convw", [128, NFQ * 4], F32, kind="ExternalInput").ap()
    la_in = nc.dram_tensor("loga_params", [HV, 2], F32, kind="ExternalInput").ap()
    rmsw_in = nc.dram_tensor("rmsw", [128, 1], F32, kind="ExternalInput").ap()
    wout_b = nc.dram_tensor("wout_b", [4, D, 512], F32, kind="ExternalInput").ap()
    wg_b = nc.dram_tensor("wg_b", [NHB, D, 128], F32, kind="ExternalInput").ap()
    wu_b = nc.dram_tensor("wu_b", [NHB, D, 128], F32, kind="ExternalInput").ap()
    wd_b = nc.dram_tensor("wd_b", [16, FFH, 128], F32, kind="ExternalInput").ap()
    ident_in = nc.dram_tensor("ident_bf", [128, 128], BF16, kind="ExternalInput").ap()
    identf_in = nc.dram_tensor("ident_f", [128, 128], F32, kind="ExternalInput").ap()
    maskLT_in = nc.dram_tensor("maskLT", [128, 128], F32, kind="ExternalInput").ap()
    maskUT_in = nc.dram_tensor("maskUT", [128, 128], F32, kind="ExternalInput").ap()
    maskUD_in = nc.dram_tensor("maskUD", [128, 128], F32, kind="ExternalInput").ap()
    onesc_in = nc.dram_tensor("ones_col_bf", [128, 1], BF16, kind="ExternalInput").ap()
    onesrf_in = nc.dram_tensor("ones_row_f", [1, 128], F32, kind="ExternalInput").ap()
    onesr_in = nc.dram_tensor("ones_row_bf", [1, 128], BF16, kind="ExternalInput").ap()
    zeros_in = nc.dram_tensor("zeros_f", [128, 128], F32, kind="ExternalInput").ap()
    ccols_in = nc.dram_tensor("ccols", [128, 8], F32, kind="ExternalInput").ap()
    hopm_in = nc.dram_tensor("hop_mcol", [128, 7], F32, kind="ExternalInput").ap()
    hopimi_in = nc.dram_tensor("hop_imi", [128, 7 * 128], F32, kind="ExternalInput").ap()

    y_out = nc.dram_tensor("y", [TPC, D], F32, kind="ExternalOutput").ap()

    PROBE_SHAPES = {
        "pre": ([NFB, 128, TP], BF16),
        "qkv": ([4, 1024, TPC], BF16),
        "rows": ([HV, 4 * TPC], F32),
        "spill": ([NCH, 4, 128, HV * 128], BF16),
        "comp": ([2, 128, HV * 128], BF16),
        "state": ([128, HV * 128], BF16),
        "pf": ([NCH, 128, NDB * 128], BF16),
        "x2": ([TPC, D], F32),
        "xq": ([128, NDB * TPC], BF16),
        "p2": ([128, NHB * TPC], BF16),
        "ffn": ([NDB, 128, TPC], BF16),
    }
    probe_out = None
    if probe is not None:
        shp, dt_ = PROBE_SHAPES[probe]
        probe_out = nc.dram_tensor("probe", shp, dt_, kind="ExternalOutput").ap()

    pre_scr = nc.dram_tensor("pre_scr", [NFB, 128, TP], BF16, kind="Internal").ap()
    spill = nc.dram_tensor("spill", [NCH, 4, 128, HV * 128], BF16, kind="Internal").ap()
    p_scr = nc.dram_tensor("p_scr", [NCH, 128, NDB * 128], BF16, kind="Internal").ap()
    x2_scr = nc.dram_tensor("x2_scr", [TPC, D], F32, kind="Internal").ap()
    ffn_scr = nc.dram_tensor("ffn_scr", [NDB, 128, TPC], BF16, kind="Internal").ap()
    cc_in = nc.dram_tensor("cc_in", [128, HV * 2 * 128], BF16, kind="Internal").ap()
    cc_out = nc.dram_tensor("cc_out", [8 * 128, HV * 2 * 128], BF16, kind="Internal").ap()

    ctr = Ctr()
    from contextlib import ExitStack

    with ExitStack() as top:
        E = top.enter_context
        s_ld = E(nc.semaphore("s_ld"))
        s_lda = E(nc.semaphore("s_lda"))
        s_ldb = E(nc.semaphore("s_ldb"))
        s_st = E(nc.semaphore("s_st"))
        s_sta = E(nc.semaphore("s_sta"))
        s_stb = E(nc.semaphore("s_stb"))
        s_ga = E(nc.semaphore("s_ga"))
        s_gb = E(nc.semaphore("s_gb"))
        s_mm = E(nc.semaphore("s_mm"))
        s_v = E(nc.semaphore("s_v"))
        s_s = E(nc.semaphore("s_s"))
        s_g = E(nc.semaphore("s_g"))
        s_cc = E(nc.semaphore("s_cc"))
        sems = dict(ld=s_ld, lda=s_lda, ldb=s_ldb, st=s_st, sta=s_sta, stb=s_stb,
                    ga=s_ga, gb=s_gb, mm=s_mm, v=s_v, s=s_s, g=s_g, cc=s_cc)

        c_ident = E(nc.sbuf_tensor("c_ident", [128, 128], BF16))
        c_identf = E(nc.sbuf_tensor("c_identf", [128, 128], F32))
        c_maskLT = E(nc.sbuf_tensor("c_maskLT", [128, 128], F32))
        c_maskUT = E(nc.sbuf_tensor("c_maskUT", [128, 128], F32))
        c_maskUD = E(nc.sbuf_tensor("c_maskUD", [128, 128], F32))
        c_onec = E(nc.sbuf_tensor("c_onec", [128, 1], BF16))
        c_onerf = E(nc.sbuf_tensor("c_onerf", [1, 128], F32))
        c_oner = E(nc.sbuf_tensor("c_oner", [1, 128], BF16))
        c_zero = E(nc.sbuf_tensor("c_zero", [128, 128], F32))
        c_cols = E(nc.sbuf_tensor("c_cols", [128, 8], F32))
        c_hopm = E(nc.sbuf_tensor("c_hopm", [128, 7], F32))
        c_hopimi = E(nc.sbuf_tensor("c_hopimi", [128, 7 * 128], F32))
        c_rmsw = E(nc.sbuf_tensor("c_rmsw", [128, 1], F32))
        c_convw = E(nc.sbuf_tensor("c_convw", [128, NFQ * 4], F32))
        c_la = E(nc.sbuf_tensor("c_la", [HV, 8], F32))

        blk = E(nc.Block())

        @blk.sync
        def _(q):
            for dst, src in [
                (c_ident[:, :], ident_in[:, :]), (c_identf[:, :], identf_in[:, :]),
                (c_maskLT[:, :], maskLT_in[:, :]), (c_maskUT[:, :], maskUT_in[:, :]),
                (c_maskUD[:, :], maskUD_in[:, :]), (c_onec[:, :], onesc_in[:, :]),
                (c_onerf[:, :], onesrf_in[:, :]), (c_oner[:, :], onesr_in[:, :]),
                (c_zero[:, :], zeros_in[:, :]), (c_cols[:, :], ccols_in[:, :]),
                (c_hopm[:, :], hopm_in[:, :]),
                (c_hopimi[:, :], hopimi_in[:, :]),
                (c_rmsw[:, :], rmsw_in[:, :]),
                (c_convw[:, :], convw_in[:, :]),
                (c_la[:, 0:2], la_in[:, :]),
            ]:
                q.dma_start(out=dst, in_=src).then_inc(s_ld, 16)
                ctr.inc(s_ld, 16)
        CONSTS = ctr.get(s_ld)

        @blk.scalar
        def _(s):
            s.wait_ge(s_ld, CONSTS)
            ctr.inc(s_s, 1)
            s.activation(c_la[:, 4:5], c_la[:, 0:1], AF.Exp).then_inc(s_s, 1)

        @blk.vector
        def _(v):
            v.wait_ge(s_s, ctr.get(s_s))
            v.tensor_scalar(c_la[:, 3:4], c_la[:, 4:5], -1.0, None, ALU.mult)
            ctr.inc(s_v, 1)
            v.tensor_scalar(c_la[:, 5:6], c_la[:, 1:2], -1.0, None, ALU.mult).then_inc(s_v, 1)

        with ExitStack() as sc2:
            E2 = sc2.enter_context
            qk_f = E2(nc.sbuf_tensor("qk_f", [128, 16 * TPC], BF16))
            k_tok = E2(nc.sbuf_tensor("k_tok", [128, HQ * TPC], BF16))
            v_tok = E2(nc.sbuf_tensor("v_tok", [128, HV * TPC], BF16))
            barows = E2(nc.sbuf_tensor("barows", [HV, 4 * TPC], F32))
            state = E2(nc.sbuf_tensor("state", [128, HV * 128], BF16))
            compP = E2(nc.sbuf_tensor("compP", [128, HV * 128], BF16))
            compR = E2(nc.sbuf_tensor("compR", [128, HV * 128], BF16))

            _p1(nc, blk, ctr, sems, locals())
            if probe == "pre":
                _probe_dump(nc, blk, ctr, sems, probe_out[0:NFB - 1, :, :], pre_scr[0:NFB - 1, :, :])
                _probe_dump(nc, blk, ctr, sems, probe_out[NFB - 1, 0:32, :], pre_scr[NFB - 1, 0:32, :])
                _finish_zero(nc, blk, ctr, sems, y_out)
                return nc, probe

            _p15(nc, blk, ctr, sems, locals())
            if probe == "qkv":
                @blk.sync
                def _(q):
                    q.wait_ge(s_v, ctr.get(s_v))
                    q.wait_ge(s_s, ctr.get(s_s))
                    for i, (src, off) in enumerate([(qk_f, 0), (qk_f, 8 * TPC), (k_tok, 0), (v_tok, 0)]):
                        q.dma_start(
                            out=probe_out[i].rearrange("(hb p) t -> p hb t", p=128)[:, 0:8, :],
                            in_=src.ap()[:, off:off + 8 * TPC].rearrange("p (hb t) -> p hb t", t=TPC),
                        ).then_inc(s_ld, 16)
                        ctr.inc(s_ld, 16)
                    q.wait_ge(s_ld, ctr.get(s_ld))
                _finish_zero(nc, blk, ctr, sems, y_out)
                return nc, probe
            if probe == "rows":
                @blk.sync
                def _(q):
                    q.wait_ge(s_v, ctr.get(s_v))
                    q.wait_ge(s_s, ctr.get(s_s))
                    q.dma_start(out=probe_out[:, :], in_=barows[:, :]).then_inc(s_ld, 16)
                    ctr.inc(s_ld, 16)
                    q.wait_ge(s_ld, ctr.get(s_ld))
                _finish_zero(nc, blk, ctr, sems, y_out)
                return nc, probe

            _p2a(nc, blk, ctr, sems, locals())
            if probe == "spill":
                _probe_dump(nc, blk, ctr, sems, probe_out[:, :, :, :], spill[:, :, :, :])
                _finish_zero(nc, blk, ctr, sems, y_out)
                return nc, probe
            if probe == "comp":
                @blk.sync
                def _(q):
                    q.wait_ge(s_v, ctr.get(s_v))
                    q.wait_ge(s_s, ctr.get(s_s))
                    q.dma_start(out=probe_out[0], in_=compP[:, :]).then_inc(s_ld, 16)
                    q.dma_start(out=probe_out[1], in_=compR[:, :]).then_inc(s_ld, 16)
                    ctr.inc(s_ld, 32)
                    q.wait_ge(s_ld, ctr.get(s_ld))
                _finish_zero(nc, blk, ctr, sems, y_out)
                return nc, probe

            _cc_hop(nc, blk, ctr, sems, locals())
            if probe == "state":
                @blk.sync
                def _(q):
                    q.wait_ge(s_v, ctr.get(s_v))
                    q.dma_start(out=probe_out[:, :], in_=state[:, :]).then_inc(s_ld, 16)
                    ctr.inc(s_ld, 16)
                    q.wait_ge(s_ld, ctr.get(s_ld))
                _finish_zero(nc, blk, ctr, sems, y_out)
                return nc, probe

            _p2b(nc, blk, ctr, sems, locals())
            if probe == "pf":
                _probe_dump(nc, blk, ctr, sems, probe_out[:, :, :], p_scr[:, :, :])
                _finish_zero(nc, blk, ctr, sems, y_out)
                return nc, probe

        with ExitStack() as sc3:
            E3 = sc3.enter_context
            xq_f = E3(nc.sbuf_tensor("xq_f", [128, NDB * TPC], BF16))

            _p3(nc, blk, ctr, sems, locals())
            if probe in ("x2", "xq"):
                if probe == "x2":
                    _probe_dump(nc, blk, ctr, sems, probe_out[:, :], x2_scr[:, :])
                else:
                    @blk.sync
                    def _(q):
                        q.wait_ge(s_s, ctr.get(s_s))
                        q.dma_start(out=probe_out[:, :], in_=xq_f[:, :]).then_inc(s_ld, 16)
                        ctr.inc(s_ld, 16)
                        q.wait_ge(s_ld, ctr.get(s_ld))
                _finish_zero(nc, blk, ctr, sems, y_out)
                return nc, probe

            with ExitStack() as sff:
                EF = sff.enter_context
                p2 = EF(nc.sbuf_tensor("p2", [128, NHB * TPC], BF16))
                _ffn_gu(nc, blk, ctr, sems, locals())
                if probe == "p2":
                    @blk.sync
                    def _(q):
                        q.wait_ge(s_v, ctr.get(s_v))
                        q.dma_start(out=probe_out[:, :], in_=p2[:, :]).then_inc(s_ld, 16)
                        ctr.inc(s_ld, 16)
                        q.wait_ge(s_ld, ctr.get(s_ld))
                    _finish_zero(nc, blk, ctr, sems, y_out)
                    return nc, probe
                _ffn_down(nc, blk, ctr, sems, locals())
            if probe == "ffn":
                _probe_dump(nc, blk, ctr, sems, probe_out[:, :, :], ffn_scr[:, :, :])
                _finish_zero(nc, blk, ctr, sems, y_out)
                return nc, probe

        _final(nc, blk, ctr, sems, locals())
    return nc, probe


def _phase_barrier(nc, blk, ctr, sems):
    names = ("ld", "lda", "ldb", "st", "sta", "stb", "ga", "gb", "mm", "v", "s", "g", "cc")
    vals = {nm: ctr.get(sems[nm]) for nm in names}
    for eng in ("sync", "tensor", "vector", "scalar", "gpsimd"):
        @getattr(blk, eng)
        def _(e, eng=eng):
            for nm in names:
                e.wait_ge(sems[nm], vals[nm])


def _probe_dump(nc, blk, ctr, sems, dst_ap, src_ap):
    @blk.sync
    def _(q):
        for nm in ("st", "sta", "stb", "v", "s", "g", "mm"):
            q.wait_ge(sems[nm], ctr.get(sems[nm]))
        q.dma_start(out=dst_ap, in_=src_ap).then_inc(sems["ld"], 16)
        ctr.inc(sems["ld"], 16)
        q.wait_ge(sems["ld"], ctr.get(sems["ld"]))


def _finish_zero(nc, blk, ctr, sems, y_out):
    s_st, s_g = sems["st"], sems["g"]
    _phase_barrier(nc, blk, ctr, sems)
    with nc.sbuf_tensor("yzero", [128, D], F32) as yz:
        @blk.gpsimd
        def _(g):
            ctr.inc(s_g, 1)
            g.memset(yz[:], 0.0).then_inc(s_g, 1)
        GV = ctr.get(s_g)

        @blk.sync
        def _(q):
            q.wait_ge(s_g, GV)
            for tb in range(TPC // 128):
                q.dma_start(out=y_out[tb * 128:(tb + 1) * 128, :], in_=yz[:]).then_inc(s_st, 16)
                ctr.inc(s_st, 16)
            q.wait_ge(s_st, ctr.get(s_st))


P1_STORE_HIST = {}


def _p1(nc, blk, ctr, sems, env):
    s_ld, s_st, s_mm, s_v, s_s = sems["ld"], sems["st"], sems["mm"], sems["v"], sems["s"]
    s_lda, s_ldb, s_sta, s_stb = sems["lda"], sems["ldb"], sems["sta"], sems["stb"]
    ldp = [s_lda, s_ldb]
    stp = [s_sta, s_stb]
    x_in, c_ident = env["x_in"], env["c_ident"]
    pre_scr = env["pre_scr"]
    wqkv_b, wgate_b, wba_in = env["wqkv_b"], env["wgate_b"], env["wba_in"]
    CONSTS = env["CONSTS"]
    from contextlib import ExitStack

    with ExitStack() as st1:
        E1 = st1.enter_context
        xtile = E1(nc.sbuf_tensor("xtile", [128, 2 * D], F32))
        xn_t = E1(nc.sbuf_tensor("xn_t", [128, D], BF16))
        xn_f = E1(nc.sbuf_tensor("xn_f", [128, NDB * TP], BF16))
        lnt = E1(nc.sbuf_tensor("lnt", [128, 32], F32))
        ps_ln0 = E1(nc.psum_tensor("ps_ln0", [128, D // 2], F32))
        ps_ln1 = E1(nc.psum_tensor("ps_ln1", [128, D // 2], F32))

        LN_HIST = {}
        for tb in range(9):
            rows = 128 if tb < 8 else TP - 8 * 128
            xt = xtile[:rows, (tb % 2) * D:(tb % 2) * D + D]
            psl = ps_ln0 if tb % 2 == 0 else ps_ln1

            sl = ldp[tb % 2]

            @blk.sync
            def _(q, tb=tb, rows=rows, xt=xt, sl=sl):
                if tb >= 2:
                    q.wait_ge(s_s, ctr.get(s_s))
                q.wait_ge(sl, ctr.get(sl))
                q.dma_start(out=xt, in_=x_in[tb * 128: tb * 128 + rows, :]).then_inc(sl, 16)
                ctr.inc(sl, 16)
            XW = ctr.get(sl)

            @blk.vector
            def _(v, rows=rows, xt=xt, XW=XW, sl=sl):
                v.wait_ge(sl, XW)
                ins = None
                for i4 in range(4):
                    ins = v.bn_stats(lnt[:rows, i4 * 6:(i4 + 1) * 6], xt[:, i4 * 512:(i4 + 1) * 512])
                ctr.inc(s_v, 1); ins.then_inc(s_v, 1); v.wait_ge(s_v, ctr.get(s_v))
                ins = v.bn_aggr(lnt[:rows, 24:26], lnt[:rows, 0:24])
                ctr.inc(s_v, 1); ins.then_inc(s_v, 1); v.wait_ge(s_v, ctr.get(s_v))
                ctr.inc(s_v, 1)
                v.tensor_scalar(lnt[:rows, 26:27], lnt[:rows, 25:26], EPS_LN, None, ALU.add).then_inc(s_v, 1)
            V1 = ctr.get(s_v)

            @blk.scalar
            def _(s, rows=rows, V1=V1):
                s.wait_ge(s_v, V1)
                ctr.inc(s_s, 1)
                s.activation(lnt[:rows, 27:28], lnt[:rows, 26:27], AF.Sqrt).then_inc(s_s, 1)
            S1 = ctr.get(s_s)

            @blk.vector
            def _(v, rows=rows, S1=S1):
                v.wait_ge(s_s, S1)
                ins = v.reciprocal(lnt[:rows, 28:29], lnt[:rows, 27:28])
                ctr.inc(s_v, 1); ins.then_inc(s_v, 1); v.wait_ge(s_v, ctr.get(s_v))
                ctr.inc(s_v, 1)
                v.scalar_tensor_tensor(out=lnt[:rows, 29:30], in0=lnt[:rows, 24:25],
                                       scalar=-1.0, in1=lnt[:rows, 28:29],
                                       op0=ALU.mult, op1=ALU.mult).then_inc(s_v, 1)
            V2 = ctr.get(s_v)

            @blk.scalar
            def _(s, tb=tb, rows=rows, xt=xt, V2=V2):
                s.wait_ge(s_v, V2)
                if tb >= 1:
                    s.wait_ge(s_mm, ctr.get(s_mm))
                ctr.inc(s_s, 1)
                s.activation(xn_t[:rows, :], xt, AF.Identity,
                             bias=lnt[:rows, 29:30], scale=lnt[:rows, 28:29]).then_inc(s_s, 1)
            S2 = ctr.get(s_s)

            @blk.tensor
            def _(t, tb=tb, rows=rows, S2=S2, psl=psl):
                t.wait_ge(s_s, S2)
                t.wait_ge(s_ld, CONSTS)
                if tb >= 1:
                    t.wait_ge(s_s, ctr.get(s_s))
                mm = None
                for db in range(NDB):
                    pp = (ps_ln0 if db < 8 else ps_ln1)
                    mm = t.matmul(pp[:, (db % 8) * 128:(db % 8) * 128 + rows],
                                  xn_t[:rows, db * 128:(db + 1) * 128],
                                  c_ident[:rows, :rows], start=True, stop=True)
                ctr.inc(s_mm, 1)
                mm.then_inc(s_mm, 1)
            M1 = ctr.get(s_mm)

            @blk.scalar
            def _(s, tb=tb, rows=rows, M1=M1):
                s.wait_ge(s_mm, M1)
                s.copy(
                    xn_f.ap().rearrange("p (db c) -> p db c", db=NDB)[:, 0:8, tb * 128:tb * 128 + rows],
                    ps_ln0.ap().rearrange("p (db c) -> p db c", db=8)[:, :, :rows],
                )
                ctr.inc(s_s, 1)
                s.copy(
                    xn_f.ap().rearrange("p (db c) -> p db c", db=NDB)[:, 8:16, tb * 128:tb * 128 + rows],
                    ps_ln1.ap().rearrange("p (db c) -> p db c", db=8)[:, :, :rows],
                ).then_inc(s_s, 1)
            LN_HIST[tb] = ctr.get(s_s)
        XNF = ctr.get(s_s)

        wf32 = E1(nc.sbuf_tensor("wf32", [128, 2 * NDB * 128], F32))
        wbf = E1(nc.sbuf_tensor("wbf", [128, 2 * NDB * 128], BF16))
        pre_sb = E1(nc.sbuf_tensor("pre_sb", [128, 2 * TP], BF16))
        ps_pa_t = E1(nc.psum_tensor("ps_pa_t", [128, 1536], F32))
        ps_pa = [ps_pa_t[:, 0:TG], ps_pa_t[:, 512:512 + TG], ps_pa_t[:, 1024:1024 + TG]]
        EV_HIST = {}

        def wsrc(fb):
            if fb < NFQ:
                return wqkv_b[fb]
            if fb < NFQ + NFG:
                return wgate_b[fb - NFQ]
            return wba_in

        for fb in range(NFB):
            buf = fb % 2
            fcols = 128 if fb < NFB - 1 else 32
            wsl = wf32[:, buf * NDB * 128:][:, :NDB * fcols]
            wbl = wbf[:, buf * NDB * 128:][:, :NDB * fcols]
            psl = ps_pa
            prl = pre_sb[:fcols, buf * TP:buf * TP + TP]

            sl = ldp[fb % 2]

            @blk.sync
            def _(q, fb=fb, wsl=wsl, fcols=fcols, sl=sl):
                if fb == 0:
                    q.wait_ge(s_lda, ctr.get(s_lda))
                    q.wait_ge(s_ldb, ctr.get(s_ldb))
                if fb >= 2:
                    q.wait_ge(s_v, ctr.get(s_v))
                q.wait_ge(sl, ctr.get(sl))
                q.dma_start(out=wsl.rearrange("p (db f) -> p db f", f=fcols),
                            in_=wsrc(fb).rearrange("(db p) f -> p db f", p=128)).then_inc(sl, 16)
                ctr.inc(sl, 16)
            WL = ctr.get(sl)

            @blk.vector
            def _(v, fb=fb, wsl=wsl, wbl=wbl, WL=WL, sl=sl):
                v.wait_ge(sl, WL)
                if fb >= 2:
                    v.wait_ge(s_mm, ctr.get(s_mm))
                ctr.inc(s_v, 1)
                v.tensor_copy(wbl, wsl).then_inc(s_v, 1)
            WC = ctr.get(s_v)

            TG_NEED = {0: LN_HIST[2], 1: LN_HIST[5], 2: LN_HIST[8]}

            @blk.tensor
            def _(t, fb=fb, wbl=wbl, psl=psl, fcols=fcols, WC=WC):
                t.wait_ge(s_v, WC)
                if fb >= 1:
                    t.wait_ge(s_s, EV_HIST[fb - 1])
                mm = None
                for tg in range(3):
                    if fb < 1:
                        t.wait_ge(s_s, TG_NEED[tg])
                    for db in range(NDB):
                        mm = t.matmul(psl[tg][:fcols, :],
                                      wbl.rearrange("p (db f) -> p db f", db=NDB)[:, db, :],
                                      xn_f.ap().rearrange("p (db c) -> p db c", db=NDB)[:, db, tg * TG:(tg + 1) * TG],
                                      start=(db == 0), stop=(db == NDB - 1))
                ctr.inc(s_mm, 1)
                mm.then_inc(s_mm, 1)
            PM = ctr.get(s_mm)

            @blk.scalar
            def _(s, fb=fb, psl=psl, prl=prl, fcols=fcols, PM=PM):
                s.wait_ge(s_mm, PM)
                if fb >= 2:
                    s.wait_ge(stp[fb % 2], ctr.get(stp[fb % 2]))
                cp = None
                for tg in range(3):
                    cp = s.copy(prl[:, tg * TG:(tg + 1) * TG], psl[tg][:fcols, :])
                ctr.inc(s_s, 1)
                cp.then_inc(s_s, 1)
            PE = ctr.get(s_s)
            EV_HIST[fb] = PE

            @blk.gpsimd
            def _(g, fb=fb, prl=prl, fcols=fcols, PE=PE):
                g.wait_ge(s_s, PE)
                g.wait_ge(stp[fb % 2], ctr.get(stp[fb % 2]))
                g.dma_start(out=pre_scr[fb, :fcols, :], in_=prl).then_inc(stp[fb % 2], 16)
                ctr.inc(stp[fb % 2], 16)
            P1_STORE_HIST[fb] = (ctr.get(s_sta), ctr.get(s_stb))


def _p15(nc, blk, ctr, sems, env):
    s_ld, s_st, s_mm, s_v, s_s = sems["ld"], sems["st"], sems["mm"], sems["v"], sems["s"]
    s_lda, s_ldb, s_sta, s_stb = sems["lda"], sems["ldb"], sems["sta"], sems["stb"]
    ldp = [s_lda, s_ldb]
    pre_scr, c_convw, c_ident, c_onec, c_onerf = (
        env["pre_scr"], env["c_convw"], env["c_ident"], env["c_onec"], env["c_onerf"])
    qk_f, k_tok, v_tok, barows, c_la, c_zero, c_cols = (
        env["qk_f"], env["k_tok"], env["v_tok"], env["barows"], env["c_la"],
        env["c_zero"], env["c_cols"])
    from contextlib import ExitStack

    with ExitStack() as st15:
        E1 = st15.enter_context
        pre_t = E1(nc.sbuf_tensor("pre_t", [128, 2 * TP], BF16))
        cv = E1(nc.sbuf_tensor("cv", [128, TPC], F32))
        sil = E1(nc.sbuf_tensor("sil", [128, TPC], F32))
        sqb = E1(nc.sbuf_tensor("sqb", [128, TPC], BF16))
        rs_row = E1(nc.sbuf_tensor("rs_row", [1, TPC], F32))
        rs_f32 = E1(nc.sbuf_tensor("rs_f32", [1, TPC], F32))
        vtile = E1(nc.sbuf_tensor("vtile", [128, TPC], BF16))
        ps_l2 = E1(nc.psum_tensor("ps_l2", [1, TPC], F32))
        ps_bc = E1(nc.psum_tensor("ps_bc", [128, TPC], F32))
        ps_tr = E1(nc.psum_tensor("ps_tr", [128, TPC], F32))

        for fb in range(NFQ):
            buf = fb % 2
            pt = pre_t[:, buf * TP:buf * TP + TP]
            is_q = fb < 8
            is_k = 8 <= fb < 16
            head = fb if is_q else (fb - 8 if is_k else fb - 16)

            sl = ldp[fb % 2]

            @blk.sync
            def _(q, fb=fb, pt=pt, sl=sl):
                if fb == 0:
                    q.wait_ge(s_lda, ctr.get(s_lda))
                    q.wait_ge(s_ldb, ctr.get(s_ldb))
                q.wait_ge(s_sta, P1_STORE_HIST[fb][0])
                q.wait_ge(s_stb, P1_STORE_HIST[fb][1])
                if fb >= 2:
                    q.wait_ge(s_v, ctr.get(s_v))
                q.wait_ge(sl, ctr.get(sl))
                q.dma_start(out=pt, in_=pre_scr[fb, :, :]).then_inc(sl, 16)
                ctr.inc(sl, 16)
            LD = ctr.get(sl)

            @blk.vector
            def _(v, fb=fb, pt=pt, LD=LD, sl=sl):
                v.wait_ge(sl, LD)
                if fb > 0:
                    v.wait_ge(s_s, ctr.get(s_s))
                ins = v.tensor_scalar(cv[:, :], pt[:, 0:TPC], c_convw[:, fb * 4:fb * 4 + 1], None, ALU.mult)
                for j in range(1, 4):
                    ctr.inc(s_v, 1); ins.then_inc(s_v, 1); v.wait_ge(s_v, ctr.get(s_v))
                    ins = v.scalar_tensor_tensor(
                        out=cv[:, :], in0=pt[:, j:j + TPC],
                        scalar=c_convw[:, fb * 4 + j:fb * 4 + j + 1], in1=cv[:, :],
                        op0=ALU.mult, op1=ALU.add)
                ctr.inc(s_v, 1)
                ins.then_inc(s_v, 1)
            CV = ctr.get(s_v)

            if is_q or is_k:
                @blk.scalar
                def _(s, CV=CV, fb=fb):
                    s.wait_ge(s_v, CV)
                    if fb >= 2:
                        s.wait_ge(s_v, ctr.get(s_v))
                    ins = s.activation(sil[:, :], cv[:, :], AF.Silu)
                    ctr.inc(s_s, 1); ins.then_inc(s_s, 1); s.wait_ge(s_s, ctr.get(s_s))
                    ctr.inc(s_s, 1)
                    s.activation(sqb[:, :], sil[:, :], AF.Square).then_inc(s_s, 1)
                SQ = ctr.get(s_s)

                @blk.tensor
                def _(t, SQ=SQ, fb=fb):
                    t.wait_ge(s_s, SQ)
                    if fb >= 1:
                        t.wait_ge(s_s, ctr.get(s_s))
                    mm = None
                    for half in range(2):
                        mm = t.matmul(ps_l2[0:1, half * 512:(half + 1) * 512], c_onec[:, :],
                                      sqb[:, half * 512:(half + 1) * 512], start=True, stop=True)
                    ctr.inc(s_mm, 1)
                    mm.then_inc(s_mm, 1)
                MMS = ctr.get(s_mm)

                @blk.scalar
                def _(s, MMS=MMS):
                    s.wait_ge(s_mm, MMS)
                    ctr.inc(s_s, 1)
                    s.activation(rs_f32[0:1, :], ps_l2[0:1, :], AF.Sqrt, bias=c_cols[0:1, 0:1]).then_inc(s_s, 1)
                RS = ctr.get(s_s)

                @blk.vector
                def _(v, RS=RS):
                    v.wait_ge(s_s, RS)
                    ctr.inc(s_v, 1)
                    v.reciprocal(rs_row[0:1, :], rs_f32[0:1, :]).then_inc(s_v, 1)
                RR = ctr.get(s_v)

                @blk.tensor
                def _(t, RR=RR, fb=fb):
                    t.wait_ge(s_v, RR)
                    if fb >= 1:
                        t.wait_ge(s_v, ctr.get(s_v))
                    mm = None
                    for half in range(2):
                        mm = t.matmul(ps_bc[:, half * 512:(half + 1) * 512],
                                      c_onerf[0:1, :],
                                      rs_row[0:1, half * 512:(half + 1) * 512], start=True, stop=True)
                    ctr.inc(s_mm, 1)
                    mm.then_inc(s_mm, 1)
                BC = ctr.get(s_mm)

                @blk.vector
                def _(v, fb=fb, head=head, is_q=is_q, BC=BC):
                    v.wait_ge(s_mm, BC)
                    dst = qk_f.ap()[:, (head + (0 if is_q else 8)) * TPC:][:, :TPC]
                    ctr.inc(s_v, 1)
                    v.tensor_tensor(dst, sil[:, :], ps_bc[:, :], ALU.mult).then_inc(s_v, 1)
                QN = ctr.get(s_v)

                if is_k:
                    @blk.tensor
                    def _(t, head=head, QN=QN, fb=fb):
                        t.wait_ge(s_v, QN)
                        if fb >= 9:
                            t.wait_ge(s_s, ctr.get(s_s))
                        src = qk_f.ap()[:, (head + 8) * TPC:][:, :TPC]
                        mm = None
                        for ci in range(NCH):
                            mm = t.matmul(ps_tr[:, ci * 128:(ci + 1) * 128],
                                          src[:, ci * 128:(ci + 1) * 128],
                                          c_ident[:, :], start=True, stop=True)
                        ctr.inc(s_mm, 1)
                        mm.then_inc(s_mm, 1)
                    KT = ctr.get(s_mm)

                    @blk.scalar
                    def _(s, head=head, KT=KT):
                        s.wait_ge(s_mm, KT)
                        ctr.inc(s_s, 1)
                        s.copy(k_tok.ap().rearrange("p (ci q s) -> p ci q s", q=HQ, s=128)[:, :, head, :],
                               ps_tr.ap().rearrange("p (ci s) -> p ci s", s=128)).then_inc(s_s, 1)
            else:
                @blk.scalar
                def _(s, CV=CV, fb=fb):
                    s.wait_ge(s_v, CV)
                    if fb >= 17:
                        s.wait_ge(s_mm, ctr.get(s_mm))
                    ctr.inc(s_s, 1)
                    s.activation(vtile[:, :], cv[:, :], AF.Silu).then_inc(s_s, 1)
                VS = ctr.get(s_s)

                @blk.tensor
                def _(t, head=head, VS=VS, fb=fb):
                    t.wait_ge(s_s, VS)
                    if fb >= 17:
                        t.wait_ge(s_s, ctr.get(s_s))
                    mm = None
                    for ci in range(NCH):
                        mm = t.matmul(ps_tr[:, ci * 128:(ci + 1) * 128],
                                      vtile[:, ci * 128:(ci + 1) * 128],
                                      c_ident[:, :], start=True, stop=True)
                    ctr.inc(s_mm, 1)
                    mm.then_inc(s_mm, 1)
                VT = ctr.get(s_mm)

                @blk.scalar
                def _(s, head=head, VT=VT):
                    s.wait_ge(s_mm, VT)
                    ctr.inc(s_s, 1)
                    s.copy(v_tok.ap().rearrange("p (ci hh s) -> p ci hh s", hh=HV, s=128)[:, :, head, :],
                           ps_tr.ap().rearrange("p (ci s) -> p ci s", s=128)).then_inc(s_s, 1)

        ba_t = E1(nc.sbuf_tensor("ba_t", [16, TPC], BF16))
        aa_t = E1(nc.sbuf_tensor("aa_t", [16, TPC], BF16))
        tmp16 = E1(nc.sbuf_tensor("tmp16", [HV, TPC], F32))

        @blk.sync
        def _(q):
            q.wait_ge(s_lda, ctr.get(s_lda))
            q.wait_ge(s_ldb, ctr.get(s_ldb))
            q.wait_ge(s_sta, P1_STORE_HIST[NFB - 1][0])
            q.wait_ge(s_stb, P1_STORE_HIST[NFB - 1][1])
            q.dma_start(out=ba_t[:, :], in_=pre_scr[NFB - 1, 0:16, HALO:HALO + TPC]).then_inc(s_lda, 16)
            q.dma_start(out=aa_t[:, :], in_=pre_scr[NFB - 1, 16:32, HALO:HALO + TPC]).then_inc(s_lda, 32 - 16)
            ctr.inc(s_lda, 32)
        BA = ctr.get(s_lda)

        @blk.scalar
        def _(s, BA=BA):
            s.wait_ge(s_lda, BA)
            s.wait_ge(s_v, ctr.get(s_v))
            ins = s.activation(barows[:, 0:TPC], ba_t[:, :], AF.Sigmoid)
            ctr.inc(s_s, 1); ins.then_inc(s_s, 1); s.wait_ge(s_s, ctr.get(s_s))
            s.activation(barows[:, TPC:2 * TPC], barows[:, 0:TPC], AF.Ln)
            ins = s.activation(tmp16[:, :], aa_t[:, :], AF.Sigmoid, scale=-1.0, bias=c_la[:, 5:6])
            ctr.inc(s_s, 1); ins.then_inc(s_s, 1); s.wait_ge(s_s, ctr.get(s_s))
            ctr.inc(s_s, 1)
            s.activation(tmp16[:, :], tmp16[:, :], AF.Ln).then_inc(s_s, 1)
        SB = ctr.get(s_s)

        @blk.vector
        def _(v, SB=SB):
            v.wait_ge(s_s, SB)
            ins = v.tensor_scalar(barows[:, 2 * TPC:3 * TPC], tmp16[:, :], c_la[:, 4:5], None, ALU.mult)
            ctr.inc(s_v, 1); ins.then_inc(s_v, 1); v.wait_ge(s_v, ctr.get(s_v))
            ins = None
            for ci in range(NCH):
                ins = v.tensor_tensor_scan(
                    out=barows[:, 3 * TPC + ci * 128:3 * TPC + (ci + 1) * 128],
                    data0=barows[:, 2 * TPC + ci * 128:2 * TPC + (ci + 1) * 128],
                    data1=barows[:, 2 * TPC + ci * 128:2 * TPC + (ci + 1) * 128],
                    initial=c_zero[0:HV, 0:1],
                    op0=ALU.add, op1=ALU.bypass)
            ctr.inc(s_v, 1)
            ins.then_inc(s_v, 1)


def _p2a(nc, blk, ctr, sems, env):
    s_mm, s_v, s_s, s_g = sems["mm"], sems["v"], sems["s"], sems["g"]
    s_sta, s_stb = sems["sta"], sems["stb"]
    stp = [s_sta, s_stb]
    qk_f, k_tok, v_tok, barows = env["qk_f"], env["k_tok"], env["v_tok"], env["barows"]
    compP, compR = env["compP"], env["compR"]
    c_ident, c_identf, c_onerf = env["c_ident"], env["c_identf"], env["c_onerf"]
    c_maskLT, c_maskUT = env["c_maskLT"], env["c_maskUT"]
    spill = env["spill"]
    from contextlib import ExitStack

    W = HV * 128
    NQ4 = 4
    W2 = W // NQ4
    HH = HV // NQ4
    QH = HQ // NQ4

    with ExitStack() as sta_:
        E1 = sta_.enter_context
        mrow_all = E1(nc.sbuf_tensor("mrow_all", [HV, 128], F32))
        cols = E1(nc.sbuf_tensor("colsA", [128, 64], F32))
        rowstall = E1(nc.sbuf_tensor("rowstall", [1, 2 * W], F32))
        e_t = E1(nc.sbuf_tensor("e_t_all", [128, W], F32))
        e_tt = E1(nc.sbuf_tensor("e_tt_all", [128, W], F32))
        X = E1(nc.sbuf_tensor("X_all", [128, W], BF16))
        Xt = E1(nc.sbuf_tensor("Xt_all", [128, W], BF16))
        NT = E1(nc.sbuf_tensor("NT_all", [128, W], BF16))
        Hb = E1(nc.sbuf_tensor("Hb_all", [128, W], BF16))
        G_sb = E1(nc.sbuf_tensor("G_all", [128, HQ * 128], BF16))
        bVt = E1(nc.sbuf_tensor("bVt_all", [128, W], BF16))
        bAKt = E1(nc.sbuf_tensor("bAKt_all", [128, W], BF16))
        kht = E1(nc.sbuf_tensor("kht_all", [128, W], BF16))
        U0 = E1(nc.sbuf_tensor("U0_all", [128, W], BF16))
        Yc = E1(nc.sbuf_tensor("Yc_all", [128, W], BF16))
        Rsb = E1(nc.sbuf_tensor("Rsb_all", [128, W], BF16))
        PT = E1(nc.sbuf_tensor("PT_all", [128, W], BF16))
        acb = E1(nc.sbuf_tensor("acbA", [128, 48], F32))
        psA = [E1(nc.psum_tensor(f"ps_A{q}", [128, W2], F32)) for q in range(NQ4)]
        psB = [E1(nc.psum_tensor(f"ps_B{q}", [128, W2], F32)) for q in range(NQ4)]

        ROWS_V = ctr.get(s_v)

        def colv(ap2):
            return ap2.unsqueeze(2).broadcast_to((128, HH, 128))

        def mv(ap2):
            return ap2.unsqueeze(1).broadcast_to((128, HH, 128))

        def gv(ap3):
            return ap3.unsqueeze(2).broadcast_to((128, QH, 2, 128))

        def h3(ap):
            return ap.rearrange("p (h s) -> p h s", s=128)

        for ci in range(NCH):
            ell_r = barows[:, 3 * TPC + ci * 128:3 * TPC + (ci + 1) * 128]
            lb_r = barows[:, TPC + ci * 128:TPC + (ci + 1) * 128]
            beta_r = barows[:, ci * 128:(ci + 1) * 128]

            @blk.vector
            def _(v, ell_r=ell_r, lb_r=lb_r, ci=ci):
                v.wait_ge(s_v, ROWS_V)
                if ci > 0:
                    v.wait_ge(s_mm, ctr.get(s_mm))
                ctr.inc(s_v, 1)
                v.tensor_tensor(mrow_all[:, :], ell_r, lb_r, ALU.add).then_inc(s_v, 1)
            MR = ctr.get(s_v)

            @blk.tensor
            def _(t, ell_r=ell_r, beta_r=beta_r, MR=MR, ci=ci):
                t.wait_ge(s_v, MR)
                if ci > 0:
                    t.wait_ge(s_s, ctr.get(s_s))
                    t.wait_ge(s_v, ctr.get(s_v))
                t.matmul(psA[0][:, 0:16], ell_r, c_identf[0:HV, 0:HV], start=True, stop=True)
                t.matmul(psA[0][:, 16:32], mrow_all[:, :], c_identf[0:HV, 0:HV], start=True, stop=True)
                ctr.inc(s_mm, 1)
                t.matmul(psA[0][:, 32:48], beta_r, c_identf[0:HV, 0:HV], start=True, stop=True).then_inc(s_mm, 1)
            CT = ctr.get(s_mm)

            @blk.scalar
            def _(s, CT=CT):
                s.wait_ge(s_mm, CT)
                ctr.inc(s_s, 1)
                s.copy(cols[:, 0:48], psA[0][:, 0:48]).then_inc(s_s, 1)
            COLS = ctr.get(s_s)

            st = [dict() for _ in range(NQ4)]

            def S_rows(q2):
                d = st[q2]
                @blk.tensor
                def _(t, q2=q2):
                    t.wait_ge(s_s, COLS)
                    mm = None
                    for i in range(HH):
                        h = q2 * HH + i
                        mm = t.matmul(psA[q2][0:1, i * 128:(i + 1) * 128], cols[:, h:h + 1],
                                      c_identf[:, :], start=True, stop=True)
                        mm = t.matmul(psB[q2][0:1, i * 128:(i + 1) * 128], cols[:, 16 + h:16 + h + 1],
                                      c_identf[:, :], start=True, stop=True)
                    ctr.inc(s_mm, 1)
                    mm.then_inc(s_mm, 1)
                d["RWM"] = ctr.get(s_mm)

            def S_rowse(q2):
                d = st[q2]
                @blk.scalar
                def _(s, q2=q2):
                    s.wait_ge(s_mm, d["RWM"])
                    s.copy(rowstall[0:1, q2 * W2:(q2 + 1) * W2], psA[q2][0:1, 0:W2])
                    ctr.inc(s_s, 1)
                    s.copy(rowstall[0:1, W + q2 * W2:W + (q2 + 1) * W2], psB[q2][0:1, 0:W2]).then_inc(s_s, 1)
                d["RWS"] = ctr.get(s_s)

            def S_bcast(q2):
                d = st[q2]
                @blk.tensor
                def _(t, q2=q2):
                    t.wait_ge(s_s, d["RWS"])
                    mm = None
                    for qq in range(W2 // 512):
                        sl = slice(qq * 512, (qq + 1) * 512)
                        slr = slice(q2 * W2 + qq * 512, q2 * W2 + (qq + 1) * 512)
                        t.matmul(psA[q2][:, sl], c_onerf[0:1, :], rowstall[0:1, slr], start=True, stop=True)
                        mm = t.matmul(psB[q2][:, sl], c_onerf[0:1, :],
                                      rowstall[0:1, W + q2 * W2 + qq * 512:W + q2 * W2 + (qq + 1) * 512],
                                      start=True, stop=True)
                    ctr.inc(s_mm, 1)
                    mm.then_inc(s_mm, 1)
                d["BC"] = ctr.get(s_mm)

            def S_ea(q2):
                d = st[q2]
                sl2 = slice(q2 * W2, (q2 + 1) * W2)
                cl = slice(q2 * HH, q2 * HH + HH)
                @blk.vector
                def _(v, q2=q2, sl2=sl2, cl=cl):
                    v.wait_ge(s_mm, d["BC"])
                    ins = v.tensor_tensor(h3(e_t.ap()[:, sl2]), colv(cols[:, 16 + q2 * HH:16 + q2 * HH + HH]),
                                          h3(psA[q2].ap()[:, 0:W2]), ALU.subtract)
                    ins = v.tensor_tensor(h3(e_tt.ap()[:, sl2]), h3(psB[q2].ap()[:, 0:W2]),
                                          colv(cols[:, q2 * HH:q2 * HH + HH]), ALU.subtract)
                    ins = v.tensor_tensor(acb[:, 16 + q2 * HH:16 + q2 * HH + HH], psA[q2].ap()[:, 127::128],
                                          cols[:, q2 * HH:q2 * HH + HH], ALU.subtract)
                    ins = v.tensor_copy(acb[:, q2 * HH:q2 * HH + HH], psA[q2].ap()[:, 127::128])
                    ctr.inc(s_v, 1); ins.then_inc(s_v, 1); v.wait_ge(s_v, ctr.get(s_v))
                    v.tensor_tensor(h3(e_t.ap()[:, sl2]), h3(e_t.ap()[:, sl2]), mv(c_maskLT.ap()), ALU.add)
                    ctr.inc(s_v, 1)
                    v.tensor_tensor(h3(e_tt.ap()[:, sl2]), h3(e_tt.ap()[:, sl2]), mv(c_maskUT.ap()), ALU.add).then_inc(s_v, 1)
                d["EA"] = ctr.get(s_v)

            def S_exp(q2):
                d = st[q2]
                sl2 = slice(q2 * W2, (q2 + 1) * W2)
                @blk.scalar
                def _(s, q2=q2, sl2=sl2):
                    s.wait_ge(s_v, d["EA"])
                    s.activation(e_t[:, sl2], e_t[:, sl2], AF.Exp)
                    s.activation(e_tt[:, sl2], e_tt[:, sl2], AF.Exp)
                    s.activation(acb[:, 32 + q2 * HH:32 + q2 * HH + HH], cols[:, 16 + q2 * HH:16 + q2 * HH + HH], AF.Exp)
                    ins = s.activation(acb[:, 16 + q2 * HH:16 + q2 * HH + HH], acb[:, 16 + q2 * HH:16 + q2 * HH + HH], AF.Exp)
                    ctr.inc(s_s, 1); ins.then_inc(s_s, 1); s.wait_ge(s_s, ctr.get(s_s))
                    ctr.inc(s_s, 1)
                    s.activation(acb[:, q2 * HH:q2 * HH + HH], acb[:, q2 * HH:q2 * HH + HH], AF.Exp).then_inc(s_s, 1)
                d["EX"] = ctr.get(s_s)

            def S_g(q2):
                d = st[q2]
                @blk.tensor
                def _(t, q2=q2, ci=ci):
                    t.wait_ge(s_s, d["EX"])
                    mm = None
                    for i in range(QH):
                        hq = q2 * QH + i
                        kf = qk_f.ap()[:, (8 + hq) * TPC + ci * 128:][:, :128]
                        mm = t.matmul(psA[q2][:, i * 128:(i + 1) * 128], kf, kf, start=True, stop=True)
                    ctr.inc(s_mm, 1)
                    mm.then_inc(s_mm, 1)
                d["GM"] = ctr.get(s_mm)

            def S_ge(q2):
                d = st[q2]
                @blk.scalar
                def _(s, q2=q2):
                    s.wait_ge(s_mm, d["GM"])
                    ctr.inc(s_s, 1)
                    s.copy(G_sb[:, q2 * QH * 128:(q2 + 1) * QH * 128], psA[q2][:, 0:QH * 128]).then_inc(s_s, 1)
                d["GE"] = ctr.get(s_s)

            def S_tx(q2):
                d = st[q2]
                sl2 = slice(q2 * W2, (q2 + 1) * W2)
                gsl = G_sb.ap()[:, q2 * QH * 128:(q2 + 1) * QH * 128].rearrange("p (q s) -> p q s", s=128)
                @blk.vector
                def _(v, q2=q2, sl2=sl2, gsl=gsl):
                    v.wait_ge(s_s, d["GE"])
                    r4 = lambda ap: h3(ap).rearrange("p (q r) s -> p q r s", r=2)
                    ins = v.tensor_tensor(r4(e_t.ap()[:, sl2]), gv(gsl), r4(e_t.ap()[:, sl2]), ALU.mult)
                    ins = v.tensor_tensor(r4(e_tt.ap()[:, sl2]), gv(gsl), r4(e_tt.ap()[:, sl2]), ALU.mult)
                    ctr.inc(s_v, 1); ins.then_inc(s_v, 1); v.wait_ge(s_v, ctr.get(s_v))
                    v.tensor_tensor(h3(X.ap()[:, sl2]), mv(c_identf.ap()), h3(e_t.ap()[:, sl2]), ALU.subtract)
                    v.tensor_tensor(h3(Xt.ap()[:, sl2]), mv(c_identf.ap()), h3(e_tt.ap()[:, sl2]), ALU.subtract)
                    ctr.inc(s_v, 1)
                    v.tensor_tensor(h3(NT.ap()[:, sl2]), mv(c_identf.ap()), h3(e_tt.ap()[:, sl2]), ALU.add).then_inc(s_v, 1)
                d["TX"] = ctr.get(s_v)
                d["last_v"] = d["TX"]
                d["last_s"] = ctr.get(s_s)

            def S_newt_h(q2):
                d = st[q2]
                @blk.tensor
                def _(t, q2=q2):
                    t.wait_ge(s_v, d["last_v"])
                    t.wait_ge(s_s, d["last_s"])
                    mm = None
                    for i in range(HH):
                        o = q2 * W2 + i * 128
                        mm = t.matmul(psA[q2][:, i * 128:(i + 1) * 128], NT[:, o:o + 128],
                                      X[:, o:o + 128], start=True, stop=True)
                    ctr.inc(s_mm, 1)
                    mm.then_inc(s_mm, 1)
                d["HM"] = ctr.get(s_mm)

            def S_newt_he(q2):
                d = st[q2]
                @blk.scalar
                def _(s, q2=q2):
                    s.wait_ge(s_mm, d["HM"])
                    ctr.inc(s_s, 1)
                    s.copy(Hb[:, q2 * W2:(q2 + 1) * W2], psA[q2][:, :]).then_inc(s_s, 1)
                d["HE"] = ctr.get(s_s)

            def S_newt_xh(q2):
                d = st[q2]
                @blk.tensor
                def _(t, q2=q2):
                    t.wait_ge(s_s, d["HE"])
                    mm = None
                    for i in range(HH):
                        o = q2 * W2 + i * 128
                        isl = slice(i * 128, (i + 1) * 128)
                        t.matmul(psB[q2][:, isl], Xt[:, o:o + 128], Hb[:, o:o + 128], start=True, stop=True)
                        mm = t.matmul(psA[q2][:, isl], Hb[:, o:o + 128], Xt[:, o:o + 128], start=True, stop=True)
                    ctr.inc(s_mm, 1)
                    mm.then_inc(s_mm, 1)
                d["XHM"] = ctr.get(s_mm)

            def S_newt_xn(q2):
                d = st[q2]
                @blk.vector
                def _(v, q2=q2):
                    v.wait_ge(s_mm, d["XHM"])
                    sl2 = slice(q2 * W2, (q2 + 1) * W2)
                    v.scalar_tensor_tensor(out=X[:, sl2], in0=X[:, sl2], scalar=2.0,
                                           in1=psB[q2][:, :], op0=ALU.mult, op1=ALU.subtract)
                    ctr.inc(s_v, 1)
                    v.scalar_tensor_tensor(out=Xt[:, sl2], in0=Xt[:, sl2], scalar=2.0,
                                           in1=psA[q2][:, :], op0=ALU.mult, op1=ALU.subtract).then_inc(s_v, 1)
                d["last_v"] = ctr.get(s_v)
                d["last_s"] = ctr.get(s_s)

            def S_ops(q2):
                d = st[q2]
                sl2 = slice(q2 * W2, (q2 + 1) * W2)
                ktok_chunk = k_tok.ap()[:, ci * HQ * 128 + q2 * QH * 128:][:, :QH * 128].rearrange("p (q s) -> p q s", s=128)
                vtok_chunk = v_tok.ap()[:, ci * W + q2 * W2:][:, :W2]
                @blk.vector
                def _(v, q2=q2, sl2=sl2, ktok_chunk=ktok_chunk, vtok_chunk=vtok_chunk):
                    v.wait_ge(s_s, d["last_s"])
                    r4 = lambda ap: h3(ap).rearrange("p (q r) s -> p q r s", r=2)
                    cv4 = lambda ap2: colv(ap2).rearrange("p (q r) s -> p q r s", r=2)
                    v.tensor_tensor(h3(bVt.ap()[:, sl2]), h3(vtok_chunk), colv(cols[:, 32 + q2 * HH:32 + q2 * HH + HH]), ALU.mult)
                    v.tensor_tensor(r4(bAKt.ap()[:, sl2]), gv(ktok_chunk), cv4(acb[:, 32 + q2 * HH:32 + q2 * HH + HH]), ALU.mult)
                    ctr.inc(s_v, 1)
                    v.tensor_tensor(r4(kht.ap()[:, sl2]), gv(ktok_chunk), cv4(acb[:, 16 + q2 * HH:16 + q2 * HH + HH]), ALU.mult).then_inc(s_v, 1)
                d["OPS"] = ctr.get(s_v)

            def S_sol(q2):
                d = st[q2]
                @blk.tensor
                def _(t, q2=q2):
                    t.wait_ge(s_v, d["OPS"])
                    mm = None
                    for i in range(HH):
                        o = q2 * W2 + i * 128
                        isl = slice(i * 128, (i + 1) * 128)
                        t.matmul(psA[q2][:, isl], Xt[:, o:o + 128], bVt[:, o:o + 128], start=True, stop=True)
                        mm = t.matmul(psB[q2][:, isl], Xt[:, o:o + 128], bAKt[:, o:o + 128], start=True, stop=True)
                    ctr.inc(s_mm, 1)
                    mm.then_inc(s_mm, 1)
                d["SOL"] = ctr.get(s_mm)

            def S_uy(q2):
                d = st[q2]
                @blk.scalar
                def _(s, q2=q2):
                    s.wait_ge(s_mm, d["SOL"])
                    s.wait_ge(s_sta, ctr.get(s_sta))
                    s.wait_ge(s_stb, ctr.get(s_stb))
                    s.copy(U0[:, q2 * W2:(q2 + 1) * W2], psA[q2][:, :])
                    ctr.inc(s_s, 1)
                    s.copy(Yc[:, q2 * W2:(q2 + 1) * W2], psB[q2][:, :]).then_inc(s_s, 1)
                d["UY"] = ctr.get(s_s)

            def S_spl1(q2):
                d = st[q2]
                @blk.gpsimd
                def _(g, q2=q2, ci=ci):
                    g.wait_ge(s_s, d["UY"])
                    g.wait_ge(stp[q2 % 2], ctr.get(stp[q2 % 2]))
                    g.dma_start(out=spill[ci, 0, :, q2 * W2:(q2 + 1) * W2], in_=U0[:, q2 * W2:(q2 + 1) * W2]).then_inc(stp[q2 % 2], 16)
                    g.dma_start(out=spill[ci, 1, :, q2 * W2:(q2 + 1) * W2], in_=Yc[:, q2 * W2:(q2 + 1) * W2]).then_inc(stp[q2 % 2], 16)
                    ctr.inc(stp[q2 % 2], 32)

            def S_rp(q2):
                d = st[q2]
                @blk.tensor
                def _(t, q2=q2):
                    t.wait_ge(s_s, d["UY"])
                    mm = None
                    for i in range(HH):
                        o = q2 * W2 + i * 128
                        isl = slice(i * 128, (i + 1) * 128)
                        t.matmul(psA[q2][:, isl], kht[:, o:o + 128], U0[:, o:o + 128], start=True, stop=True)
                        mm = t.matmul(psB[q2][:, isl], Yc[:, o:o + 128], kht[:, o:o + 128], start=True, stop=True)
                    ctr.inc(s_mm, 1)
                    mm.then_inc(s_mm, 1)
                d["RP"] = ctr.get(s_mm)

            def S_prv(q2):
                d = st[q2]
                sl2 = slice(q2 * W2, (q2 + 1) * W2)
                @blk.vector
                def _(v, q2=q2, sl2=sl2):
                    v.wait_ge(s_mm, d["RP"])
                    v.wait_ge(s_sta, ctr.get(s_sta))
                    v.wait_ge(s_stb, ctr.get(s_stb))
                    v.tensor_copy(Rsb[:, sl2], psA[q2][:, :])
                    ins = v.tensor_tensor(h3(e_t.ap()[:, sl2]), mv(c_identf.ap()),
                                          colv(acb[:, q2 * HH:q2 * HH + HH]), ALU.mult)
                    ctr.inc(s_v, 1); ins.then_inc(s_v, 1); v.wait_ge(s_v, ctr.get(s_v))
                    ctr.inc(s_v, 1)
                    v.tensor_tensor(PT[:, sl2], e_t[:, sl2], psB[q2][:, :], ALU.subtract).then_inc(s_v, 1)
                d["PRV"] = ctr.get(s_v)

            def S_spl2(q2):
                d = st[q2]
                @blk.gpsimd
                def _(g, q2=q2, ci=ci):
                    g.wait_ge(s_v, d["PRV"])
                    g.wait_ge(stp[q2 % 2], ctr.get(stp[q2 % 2]))
                    g.dma_start(out=spill[ci, 2, :, q2 * W2:(q2 + 1) * W2], in_=PT[:, q2 * W2:(q2 + 1) * W2]).then_inc(stp[q2 % 2], 16)
                    g.dma_start(out=spill[ci, 3, :, q2 * W2:(q2 + 1) * W2], in_=Rsb[:, q2 * W2:(q2 + 1) * W2]).then_inc(stp[q2 % 2], 16)
                    ctr.inc(stp[q2 % 2], 32)

            def S_cm(q2):
                d = st[q2]
                sl2 = slice(q2 * W2, (q2 + 1) * W2)
                if ci == 0:
                    @blk.vector
                    def _(v, q2=q2, sl2=sl2):
                        v.wait_ge(s_v, d["PRV"])
                        v.tensor_copy(compP.ap()[:, sl2], PT[:, sl2])
                        ctr.inc(s_v, 1)
                        v.tensor_copy(compR.ap()[:, sl2], Rsb[:, sl2]).then_inc(s_v, 1)
                else:
                    @blk.tensor
                    def _(t, q2=q2):
                        t.wait_ge(s_v, d["PRV"])
                        mm = None
                        for i in range(HH):
                            o = q2 * W2 + i * 128
                            isl = slice(i * 128, (i + 1) * 128)
                            t.matmul(psA[q2][:, isl], PT[:, o:o + 128], compP.ap()[:, o:o + 128], start=True, stop=True)
                            mm = t.matmul(psB[q2][:, isl], PT[:, o:o + 128], compR.ap()[:, o:o + 128], start=True, stop=True)
                        ctr.inc(s_mm, 1)
                        mm.then_inc(s_mm, 1)
                    d["CM"] = ctr.get(s_mm)

                    @blk.vector
                    def _(v, q2=q2, sl2=sl2):
                        v.wait_ge(s_mm, d["CM"])
                        v.tensor_copy(compP.ap()[:, sl2], psA[q2][:, :])
                        ctr.inc(s_v, 1)
                        v.tensor_tensor(compR.ap()[:, sl2], psB[q2][:, :], Rsb[:, sl2], ALU.add).then_inc(s_v, 1)

            stages = [S_rows, S_rowse, S_bcast, S_ea, S_exp, S_g, S_ge, S_tx]
            for _it in range(NEWTON_ITERS):
                stages += [S_newt_h, S_newt_he, S_newt_xh, S_newt_xn]
            stages += [S_ops, S_sol, S_uy, S_spl1, S_rp, S_prv, S_spl2, S_cm]
            for stg in stages:
                for qq4 in range(NQ4):
                    stg(qq4)


def _cc_hop(nc, blk, ctr, sems, env):
    s_ld, s_st, s_mm, s_v, s_s, s_g, s_cc = (
        sems["ld"], sems["st"], sems["mm"], sems["v"], sems["s"], sems["g"], sems["cc"])
    s_lda, s_ldb = sems["lda"], sems["ldb"]
    ldp = [s_lda, s_ldb]
    compP, compR, state = env["compP"], env["compR"], env["state"]
    c_ident, c_hopm, c_hopimi = env["c_ident"], env["c_hopm"], env["c_hopimi"]
    cc_in, cc_out = env["cc_in"], env["cc_out"]
    from contextlib import ExitStack

    _phase_barrier(nc, blk, ctr, sems)
    with ExitStack() as stc:
        E1 = stc.enter_context
        cPt = E1(nc.sbuf_tensor("cPt", [128, HV * 128], BF16))
        hopbuf = E1(nc.sbuf_tensor("hopbuf", [128, 2 * HV * 2 * 128], BF16))
        Pm = E1(nc.sbuf_tensor("Pm", [128, HV * 128], BF16))
        ps_hopA = E1(nc.psum_tensor("ps_hopA", [128, HV * 128], F32))
        CPD = ctr.get(s_v)

        @blk.tensor
        def _(t, CPD=CPD):
            t.wait_ge(s_v, CPD)
            mm = None
            for h in range(HV):
                mm = t.matmul(ps_hopA[:, h * 128:(h + 1) * 128], compP.ap()[:, h * 128:(h + 1) * 128],
                              c_ident[:, :], start=True, stop=True)
            ctr.inc(s_mm, 1)
            mm.then_inc(s_mm, 1)
        TM = ctr.get(s_mm)

        @blk.scalar
        def _(s, TM=TM):
            s.wait_ge(s_mm, TM)
            ctr.inc(s_s, 1)
            s.copy(cPt[:, :], ps_hopA[:, :]).then_inc(s_s, 1)
        CPT = ctr.get(s_s)

        @blk.sync
        def _(q, CPT=CPT):
            q.wait_ge(s_s, CPT)
            q.wait_ge(s_lda, ctr.get(s_lda))
            q.wait_ge(s_ldb, ctr.get(s_ldb))
            q.dma_start(out=cc_in[:, 0:HV * 128], in_=cPt[:, :]).then_inc(s_lda, 16)
            q.dma_start(out=cc_in[:, HV * 128:], in_=compR[:, :]).then_inc(s_lda, 16)
            ctr.inc(s_lda, 32)
        CCI = ctr.get(s_lda)

        @blk.gpsimd
        def _(g, CCI=CCI):
            g.wait_ge(s_lda, CCI)
            ctr.inc(s_cc, 1)
            g.collective_compute(
                "AllGather", ALU.bypass,
                ins=[cc_in[:, :]], outs=[cc_out[:, :]],
                replica_groups=[list(range(8))],
            ).then_inc(s_cc)
            ctr.inc(s_g, 1)
            g.memset(state[:, :], 0.0).then_inc(s_g, 1)
        STZ = ctr.get(s_g)

        W = HV * 128
        for j in range(7):
            sl = ldp[j % 2]
            hb = hopbuf[:, (j % 2) * HV * 256:(j % 2 + 1) * HV * 256]

            @blk.sync
            def _(q, j=j, sl=sl, hb=hb):
                q.wait_ge(s_cc, ctr.get(s_cc))
                if j >= 2:
                    q.wait_ge(s_v, ctr.get(s_v))
                q.wait_ge(sl, ctr.get(sl))
                q.dma_start(out=hb, in_=cc_out[j * 128:(j + 1) * 128, :]).then_inc(sl, 16)
                ctr.inc(sl, 16)
            HL = ctr.get(sl)

            @blk.vector
            def _(v, j=j, sl=sl, hb=hb, HL=HL, STZ=STZ):
                v.wait_ge(sl, HL)
                v.wait_ge(s_g, STZ)
                if j > 0:
                    v.wait_ge(s_mm, ctr.get(s_mm))
                ins = v.tensor_scalar(Pm[:, :], hb[:, 0:W], c_hopm[:, j:j + 1], None, ALU.mult)
                ctr.inc(s_v, 1); ins.then_inc(s_v, 1); v.wait_ge(s_v, ctr.get(s_v))
                ctr.inc(s_v, 1)
                v.tensor_tensor(Pm.ap().rearrange("p (h f) -> p h f", f=128),
                                Pm.ap().rearrange("p (h f) -> p h f", f=128),
                                c_hopimi.ap().rearrange("p (jj f) -> p jj f", f=128)[:, j:j + 1, :].broadcast_to((128, HV, 128)),
                                ALU.add).then_inc(s_v, 1)
            PMV = ctr.get(s_v)

            @blk.tensor
            def _(t, PMV=PMV):
                t.wait_ge(s_v, PMV)
                t.wait_ge(s_s, ctr.get(s_s))
                mm = None
                for h in range(HV):
                    hsl = slice(h * 128, (h + 1) * 128)
                    mm = t.matmul(ps_hopA[:, hsl], Pm[:, hsl],
                                  state.ap()[:, hsl], start=True, stop=True)
                ctr.inc(s_mm, 1)
                mm.then_inc(s_mm, 1)
            SMM = ctr.get(s_mm)

            @blk.vector
            def _(v, j=j, hb=hb, SMM=SMM):
                v.wait_ge(s_mm, SMM)
                ins = v.tensor_scalar(state.ap()[:, :], hb[:, W:2 * W], c_hopm[:, j:j + 1], None, ALU.mult)
                ctr.inc(s_v, 1); ins.then_inc(s_v, 1); v.wait_ge(s_v, ctr.get(s_v))
                ctr.inc(s_v, 1)
                v.tensor_tensor(state.ap()[:, :], state.ap()[:, :], ps_hopA[:, :], ALU.add).then_inc(s_v, 1)
def _p2b(nc, blk, ctr, sems, env):
    s_mm, s_v, s_s, s_g = sems["mm"], sems["v"], sems["s"], sems["g"]
    s_lda, s_ldb, s_sta, s_stb = sems["lda"], sems["ldb"], sems["sta"], sems["stb"]
    ldp = [s_lda, s_ldb]
    stp = [s_sta, s_stb]
    qk_f, barows, state = env["qk_f"], env["barows"], env["state"]
    c_ident, c_identf, c_maskUD, c_rmsw, c_onerf = (
        env["c_ident"], env["c_identf"], env["c_maskUD"], env["c_rmsw"], env["c_onerf"])
    c_cols = env["c_cols"]
    spill, pre_scr, p_scr = env["spill"], env["pre_scr"], env["p_scr"]
    from contextlib import ExitStack

    W = HV * 128
    W2 = W // 2
    HH = HV // 2
    QH = HQ // 2

    _phase_barrier(nc, blk, ctr, sems)
    with ExitStack() as stb_:
        E1 = stb_.enter_context
        cols = E1(nc.sbuf_tensor("colsB", [128, 16], F32))
        rowstall = E1(nc.sbuf_tensor("rowstallB", [1, W], F32))
        e_m = E1(nc.sbuf_tensor("e_mB", [128, W], F32))
        qa = E1(nc.sbuf_tensor("qaB", [128, W], BF16))
        MT = E1(nc.sbuf_tensor("MTB", [128, W], BF16))
        qAf = E1(nc.sbuf_tensor("qAfB", [128, W], BF16))
        nMYT = E1(nc.sbuf_tensor("nMYTB", [128, W], BF16))
        U0 = E1(nc.sbuf_tensor("U0B", [128, W], BF16))
        Yc = E1(nc.sbuf_tensor("YcB", [128, W], BF16))
        PTl = E1(nc.sbuf_tensor("PTlB", [128, W], BF16))
        Rl = E1(nc.sbuf_tensor("RlB", [128, W], BF16))
        gt = E1(nc.sbuf_tensor("gtB", [128, 2 * W], BF16))
        Ofp = E1(nc.sbuf_tensor("OfpB", [128, W], F32))
        sqf = E1(nc.sbuf_tensor("sqfB", [128, W], F32))
        Onsb = E1(nc.sbuf_tensor("OnsbB", [128, W], BF16))
        gsil = E1(nc.sbuf_tensor("gsilB", [128, W], BF16))
        ptile = E1(nc.sbuf_tensor("ptileB", [128, 2 * W], BF16))
        rcol = E1(nc.sbuf_tensor("rcolB", [128, 16], F32))
        psA = [E1(nc.psum_tensor("ps_AB0", [128, W2], F32)), E1(nc.psum_tensor("ps_AB1", [128, W2], F32))]
        psB = [E1(nc.psum_tensor("ps_BB0", [128, W2], F32)), E1(nc.psum_tensor("ps_BB1", [128, W2], F32))]

        def colv(ap2):
            return ap2.unsqueeze(2).broadcast_to((128, HH, 128))

        def mv(ap2):
            return ap2.unsqueeze(1).broadcast_to((128, HH, 128))

        def gv(ap3):
            return ap3.unsqueeze(2).broadcast_to((128, QH, 2, 128))

        def h3(ap):
            return ap.rearrange("p (h s) -> p h s", s=128)

        for ci in range(NCH):
            buf = ci % 2
            ell_r = barows[:, 3 * TPC + ci * 128:3 * TPC + (ci + 1) * 128]
            gtl = gt[:, buf * W:buf * W + W]
            pt = ptile[:, buf * W:buf * W + W]

            @blk.sync
            def _(q, ci=ci, buf=buf, gtl=gtl):
                if ci < 2:
                    q.wait_ge(s_sta, ctr.get(s_sta))
                    q.wait_ge(s_stb, ctr.get(s_stb))
                q.wait_ge(s_mm, ctr.get(s_mm))
                q.wait_ge(s_s, ctr.get(s_s))
                q.wait_ge(s_v, ctr.get(s_v))
                sl = ldp[buf]
                q.wait_ge(sl, ctr.get(sl))
                q.dma_start(out=U0[:, :], in_=spill[ci, 0]).then_inc(sl, 16)
                q.dma_start(out=Yc[:, :], in_=spill[ci, 1]).then_inc(sl, 16)
                q.dma_start(out=PTl[:, :], in_=spill[ci, 2]).then_inc(sl, 16)
                q.dma_start(out=Rl[:, :], in_=spill[ci, 3]).then_inc(sl, 16)
                q.dma_start(out=gtl.rearrange("p (h s) -> p h s", s=128),
                            in_=pre_scr[NFQ:NFQ + HV, :, HALO + ci * 128:HALO + (ci + 1) * 128].rearrange("h p s -> p h s")).then_inc(sl, 16)
                ctr.inc(ldp[buf], 80)
            LDS = ctr.get(ldp[buf])

            @blk.tensor
            def _(t, ell_r=ell_r, ci=ci):
                t.wait_ge(s_s, ctr.get(s_s))
                t.wait_ge(s_v, ctr.get(s_v))
                ctr.inc(s_mm, 1)
                t.matmul(psA[0][:, 0:16], ell_r, c_identf[0:HV, 0:HV], start=True, stop=True).then_inc(s_mm, 1)
            CT = ctr.get(s_mm)

            @blk.scalar
            def _(s, CT=CT):
                s.wait_ge(s_mm, CT)
                ctr.inc(s_s, 1)
                s.copy(cols[:, 0:16], psA[0][:, 0:16]).then_inc(s_s, 1)
            COLS = ctr.get(s_s)

            st = [dict() for _ in range(2)]

            def S_rows(q2):
                d = st[q2]
                @blk.tensor
                def _(t, q2=q2):
                    t.wait_ge(s_s, COLS)
                    mm = None
                    for i in range(HH):
                        h = q2 * HH + i
                        mm = t.matmul(psB[q2][0:1, i * 128:(i + 1) * 128], cols[:, h:h + 1],
                                      c_identf[:, :], start=True, stop=True)
                    ctr.inc(s_mm, 1)
                    mm.then_inc(s_mm, 1)
                d["RWM"] = ctr.get(s_mm)

            def S_rowse(q2):
                d = st[q2]
                @blk.scalar
                def _(s, q2=q2):
                    s.wait_ge(s_mm, d["RWM"])
                    ctr.inc(s_s, 1)
                    s.copy(rowstall[0:1, q2 * W2:(q2 + 1) * W2], psB[q2][0:1, 0:W2]).then_inc(s_s, 1)
                d["RWS"] = ctr.get(s_s)

            def S_bcast(q2):
                d = st[q2]
                @blk.tensor
                def _(t, q2=q2):
                    t.wait_ge(s_s, d["RWS"])
                    mm = None
                    for qq in range(2):
                        sl = slice(qq * 512, (qq + 1) * 512)
                        mm = t.matmul(psA[q2][:, sl], c_onerf[0:1, :],
                                      rowstall[0:1, q2 * W2 + qq * 512:q2 * W2 + (qq + 1) * 512],
                                      start=True, stop=True)
                    ctr.inc(s_mm, 1)
                    mm.then_inc(s_mm, 1)
                d["BC"] = ctr.get(s_mm)

            def S_ea(q2):
                d = st[q2]
                sl2 = slice(q2 * W2, (q2 + 1) * W2)
                @blk.vector
                def _(v, q2=q2, sl2=sl2):
                    v.wait_ge(s_mm, d["BC"])
                    ins = v.tensor_tensor(h3(e_m.ap()[:, sl2]), h3(psA[q2].ap()[:, 0:W2]),
                                          colv(cols[:, q2 * HH:q2 * HH + HH]), ALU.subtract)
                    ctr.inc(s_v, 1); ins.then_inc(s_v, 1); v.wait_ge(s_v, ctr.get(s_v))
                    ctr.inc(s_v, 1)
                    v.tensor_tensor(h3(e_m.ap()[:, sl2]), h3(e_m.ap()[:, sl2]), mv(c_maskUD.ap()), ALU.add).then_inc(s_v, 1)
                d["EA"] = ctr.get(s_v)

            def S_exp(q2):
                d = st[q2]
                sl2 = slice(q2 * W2, (q2 + 1) * W2)
                @blk.scalar
                def _(s, q2=q2, sl2=sl2):
                    s.wait_ge(s_v, d["EA"])
                    ins = s.activation(qa[:, sl2], psA[q2][:, 0:W2], AF.Exp, bias=c_cols[:, 2:3])
                    ctr.inc(s_s, 1); ins.then_inc(s_s, 1); s.wait_ge(s_s, ctr.get(s_s))
                    ctr.inc(s_s, 1)
                    s.activation(e_m[:, sl2], e_m[:, sl2], AF.Exp).then_inc(s_s, 1)
                d["EX"] = ctr.get(s_s)

            def S_g(q2):
                d = st[q2]
                @blk.tensor
                def _(t, q2=q2, ci=ci):
                    t.wait_ge(s_s, d["EX"])
                    mm = None
                    for i in range(QH):
                        hq = q2 * QH + i
                        kf = qk_f.ap()[:, (8 + hq) * TPC + ci * 128:][:, :128]
                        qf = qk_f.ap()[:, hq * TPC + ci * 128:][:, :128]
                        mm = t.matmul(psA[q2][:, i * 128:(i + 1) * 128], kf, qf, start=True, stop=True)
                    ctr.inc(s_mm, 1)
                    mm.then_inc(s_mm, 1)
                d["GM"] = ctr.get(s_mm)

            def S_mt(q2):
                d = st[q2]
                sl2 = slice(q2 * W2, (q2 + 1) * W2)
                @blk.vector
                def _(v, q2=q2, sl2=sl2, ci=ci):
                    v.wait_ge(s_mm, d["GM"])
                    r4 = lambda ap: h3(ap).rearrange("p (q r) s -> p q r s", r=2)
                    ins = v.tensor_tensor(
                        r4(MT.ap()[:, sl2]),
                        psA[q2].ap()[:, 0:QH * 128].rearrange("p (q s) -> p q s", s=128).unsqueeze(2).broadcast_to((128, QH, 2, 128)),
                        r4(e_m.ap()[:, sl2]), ALU.mult)
                    qv = qk_f.ap().rearrange("p (hh t) -> p hh t", t=TPC)[:, q2 * QH:(q2 + 1) * QH, ci * 128:(ci + 1) * 128].unsqueeze(2).broadcast_to((128, QH, 2, 128))
                    ctr.inc(s_v, 1)
                    v.tensor_tensor(r4(qAf.ap()[:, sl2]), qv, r4(h3(qa.ap()[:, sl2]).rearrange("p (q r) s -> p q r s", r=2)) if False else h3(qa.ap()[:, sl2]).rearrange("p (q r) s -> p q r s", r=2), ALU.mult).then_inc(s_v, 1)
                d["MTV"] = ctr.get(s_v)

            def S_my(q2):
                d = st[q2]
                @blk.tensor
                def _(t, q2=q2):
                    t.wait_ge(s_v, d["MTV"])
                    t.wait_ge(ldp[buf], LDS)
                    mm = None
                    for i in range(HH):
                        o = q2 * W2 + i * 128
                        mm = t.matmul(psB[q2][:, i * 128:(i + 1) * 128], Yc[:, o:o + 128], MT[:, o:o + 128], start=True, stop=True)
                    ctr.inc(s_mm, 1)
                    mm.then_inc(s_mm, 1)
                d["MY"] = ctr.get(s_mm)

            def S_nm(q2):
                d = st[q2]
                @blk.scalar
                def _(s, q2=q2):
                    s.wait_ge(s_mm, d["MY"])
                    ctr.inc(s_s, 1)
                    s.activation(nMYT[:, q2 * W2:(q2 + 1) * W2], psB[q2][:, :], AF.Identity, scale=-1.0).then_inc(s_s, 1)
                d["NM"] = ctr.get(s_s)

            def S_o(q2):
                d = st[q2]
                @blk.tensor
                def _(t, q2=q2):
                    t.wait_ge(s_s, d["NM"])
                    mm = None
                    for i in range(HH):
                        o = q2 * W2 + i * 128
                        isl = slice(i * 128, (i + 1) * 128)
                        t.matmul(psB[q2][:, isl], MT[:, o:o + 128], U0[:, o:o + 128], start=True, stop=False)
                        t.matmul(psB[q2][:, isl], qAf[:, o:o + 128], state.ap()[:, o:o + 128], start=False, stop=False)
                        mm = t.matmul(psB[q2][:, isl], nMYT[:, o:o + 128], state.ap()[:, o:o + 128], start=False, stop=True)
                    ctr.inc(s_mm, 1)
                    mm.then_inc(s_mm, 1)
                d["OM"] = ctr.get(s_mm)

            def S_oe(q2):
                d = st[q2]
                sl2 = slice(q2 * W2, (q2 + 1) * W2)
                @blk.scalar
                def _(s, q2=q2, sl2=sl2):
                    s.wait_ge(s_mm, d["OM"])
                    ins = s.copy(Ofp[:, sl2], psB[q2][:, :])
                    ctr.inc(s_s, 1); ins.then_inc(s_s, 1); s.wait_ge(s_s, ctr.get(s_s))
                    ctr.inc(s_s, 1)
                    s.activation(sqf[:, sl2], Ofp[:, sl2], AF.Square).then_inc(s_s, 1)
                d["OE"] = ctr.get(s_s)

            def S_rm(q2):
                d = st[q2]
                sl2 = slice(q2 * W2, (q2 + 1) * W2)
                cl = slice(q2 * HH, q2 * HH + HH)
                @blk.vector
                def _(v, q2=q2, sl2=sl2, cl=cl):
                    v.wait_ge(s_s, d["OE"])
                    ins = v.tensor_reduce(rcol[:, cl].unsqueeze(2), h3(sqf.ap()[:, sl2]), AX.X, ALU.add)
                    ctr.inc(s_v, 1); ins.then_inc(s_v, 1); v.wait_ge(s_v, ctr.get(s_v))
                    ctr.inc(s_v, 1)
                    v.tensor_scalar(rcol[:, cl], rcol[:, cl], 1.0 / DV, EPS_RMS, ALU.mult, ALU.add).then_inc(s_v, 1)
                d["RM1"] = ctr.get(s_v)

            def S_sq(q2):
                d = st[q2]
                cl = slice(q2 * HH, q2 * HH + HH)
                @blk.scalar
                def _(s, q2=q2, cl=cl):
                    s.wait_ge(s_v, d["RM1"])
                    ctr.inc(s_s, 1)
                    s.activation(rcol[:, cl], rcol[:, cl], AF.Sqrt).then_inc(s_s, 1)
                d["RM2"] = ctr.get(s_s)

            def S_on(q2):
                d = st[q2]
                sl2 = slice(q2 * W2, (q2 + 1) * W2)
                cl = slice(q2 * HH, q2 * HH + HH)
                @blk.vector
                def _(v, q2=q2, sl2=sl2, cl=cl):
                    v.wait_ge(s_s, d["RM2"])
                    ins = v.reciprocal(rcol[:, cl], rcol[:, cl])
                    ctr.inc(s_v, 1); ins.then_inc(s_v, 1); v.wait_ge(s_v, ctr.get(s_v))
                    ctr.inc(s_v, 1)
                    v.tensor_tensor(h3(Onsb.ap()[:, sl2]), h3(Ofp.ap()[:, sl2]), colv(rcol[:, cl]), ALU.mult).then_inc(s_v, 1)
                d["ONV"] = ctr.get(s_v)

            def S_ot(q2):
                d = st[q2]
                @blk.tensor
                def _(t, q2=q2):
                    t.wait_ge(s_v, d["ONV"])
                    t.wait_ge(s_s, ctr.get(s_s))
                    mm = None
                    for i in range(HH):
                        o = q2 * W2 + i * 128
                        mm = t.matmul(psA[q2][:, i * 128:(i + 1) * 128], Onsb[:, o:o + 128], c_ident[:, :], start=True, stop=True)
                    ctr.inc(s_mm, 1)
                    mm.then_inc(s_mm, 1)
                d["OT"] = ctr.get(s_mm)

            def S_pg(q2):
                d = st[q2]
                sl2 = slice(q2 * W2, (q2 + 1) * W2)
                @blk.scalar
                def _(s, q2=q2, sl2=sl2, gtl=gtl):
                    s.wait_ge(s_mm, d["OT"])
                    ins = s.activation(gsil[:, sl2], gtl[:, sl2], AF.Silu)
                    ctr.inc(s_s, 1); ins.then_inc(s_s, 1); s.wait_ge(s_s, ctr.get(s_s))
                    ctr.inc(s_s, 1)
                    s.activation(Onsb[:, sl2], psA[q2][:, :], AF.Identity, scale=c_rmsw[:, 0:1]).then_inc(s_s, 1)
                d["PS1"] = ctr.get(s_s)

            def S_pv(q2):
                d = st[q2]
                sl2 = slice(q2 * W2, (q2 + 1) * W2)
                @blk.vector
                def _(v, q2=q2, sl2=sl2, pt=pt, buf=buf):
                    v.wait_ge(s_s, d["PS1"])
                    v.wait_ge(stp[buf], ctr.get(stp[buf]))
                    ctr.inc(s_v, 1)
                    v.tensor_tensor(pt[:, sl2], Onsb[:, sl2], gsil[:, sl2], ALU.mult).then_inc(s_v, 1)
                d["PV"] = ctr.get(s_v)

            def S_st(q2):
                d = st[q2]
                @blk.tensor
                def _(t, q2=q2):
                    t.wait_ge(s_s, ctr.get(s_s))
                    mm = None
                    for i in range(HH):
                        o = q2 * W2 + i * 128
                        mm = t.matmul(psB[q2][:, i * 128:(i + 1) * 128], PTl[:, o:o + 128], state.ap()[:, o:o + 128], start=True, stop=True)
                    ctr.inc(s_mm, 1)
                    mm.then_inc(s_mm, 1)
                d["SM"] = ctr.get(s_mm)

            def S_sv(q2):
                d = st[q2]
                sl2 = slice(q2 * W2, (q2 + 1) * W2)
                @blk.vector
                def _(v, q2=q2, sl2=sl2):
                    v.wait_ge(s_mm, d["SM"])
                    ctr.inc(s_v, 1)
                    v.tensor_tensor(state.ap()[:, sl2], psB[q2][:, :], Rl[:, sl2], ALU.add).then_inc(s_v, 1)

            stages = [S_rows, S_rowse, S_bcast, S_ea, S_exp, S_g, S_mt, S_my, S_nm,
                      S_o, S_oe, S_rm, S_sq, S_on, S_ot, S_pg, S_pv, S_st, S_sv]
            for stg in stages:
                stg(0)
                stg(1)

            PVmax = max(st[0]["PV"], st[1]["PV"])

            @blk.gpsimd
            def _(g, ci=ci, pt=pt, PVmax=PVmax, buf=buf):
                g.wait_ge(s_v, PVmax)
                g.wait_ge(stp[buf], ctr.get(stp[buf]))
                g.dma_start(out=p_scr[ci], in_=pt).then_inc(stp[buf], 16)
                ctr.inc(stp[buf], 16)


def _p3(nc, blk, ctr, sems, env):
    s_mm, s_v, s_s = sems["mm"], sems["v"], sems["s"]
    s_lda, s_ldb, s_sta, s_stb = sems["lda"], sems["ldb"], sems["sta"], sems["stb"]
    ldp = [s_lda, s_ldb]
    stp = [s_sta, s_stb]
    c_ident, c_cols = env["c_ident"], env["c_cols"]
    x_in, wout_b, p_scr, x2_scr = env["x_in"], env["wout_b"], env["p_scr"], env["x2_scr"]
    xq_f = env["xq_f"]
    from contextlib import ExitStack

    _phase_barrier(nc, blk, ctr, sems)
    with ExitStack() as st3:
        E1 = st3.enter_context
        pcol = E1(nc.sbuf_tensor("pcol", [128, 2 * 2048], BF16))
        woutb = E1(nc.sbuf_tensor("woutb", [128, 2 * 16 * 512], BF16))
        attn_all = E1(nc.sbuf_tensor("attn_all", [128, 8 * D], BF16))
        lnt_b = E1(nc.sbuf_tensor("lnt3", [128, 2 * 40], F32))
        x2t = E1(nc.sbuf_tensor("x2t", [128, 2 * D], F32))
        xnt_b = E1(nc.sbuf_tensor("xnt3", [128, 2 * D], F32))
        xtl = E1(nc.sbuf_tensor("xtl3", [128, 2 * D], F32))
        inti_b = E1(nc.sbuf_tensor("inti", [128, 2 * D], I32))
        xq_t_b = E1(nc.sbuf_tensor("xq_t", [128, 2 * D], BF16))
        ps_op0 = E1(nc.psum_tensor("ps_op0", [128, 512], F32))
        ps_op1 = E1(nc.psum_tensor("ps_op1", [128, 512], F32))
        ps_big = E1(nc.psum_tensor("ps_big3", [128, D], F32))

        evac_hist = {}
        projmm_hist = {}
        x2v_hist = {}
        tq_hist = {}
        cp_hist = {}

        def emit_lnq(tb, ready):
            at = attn_all[:, tb * D:(tb + 1) * D]
            xt = xtl[:, (tb % 2) * D:(tb % 2) * D + D]
            x2 = x2t[:, (tb % 2) * D:(tb % 2) * D + D]
            lnt = lnt_b[:, (tb % 2) * 40:(tb % 2) * 40 + 40]
            xnt = xnt_b[:, (tb % 2) * D:(tb % 2) * D + D]
            inti = inti_b[:, (tb % 2) * D:(tb % 2) * D + D]
            xq_t = xq_t_b[:, (tb % 2) * D:(tb % 2) * D + D]

            @blk.sync
            def _(q, tb=tb, xt=xt):
                if tb >= 2:
                    q.wait_ge(s_v, x2v_hist[tb - 2])
                q.wait_ge(ldp[tb % 2], ctr.get(ldp[tb % 2]))
                q.dma_start(out=xt, in_=x_in[HALO + tb * 128: HALO + (tb + 1) * 128, :]).then_inc(ldp[tb % 2], 16)
                ctr.inc(ldp[tb % 2], 16)
            XL = ctr.get(ldp[tb % 2])

            def emit_ln(src_ap, eps_val, dst_col):
                @blk.vector
                def _(v):
                    v.wait_ge(s_v, ctr.get(s_v))
                    ins = None
                    for i4 in range(4):
                        ins = v.bn_stats(lnt[:, i4 * 6:(i4 + 1) * 6], src_ap[:, i4 * 512:(i4 + 1) * 512])
                    ctr.inc(s_v, 1); ins.then_inc(s_v, 1); v.wait_ge(s_v, ctr.get(s_v))
                    ins = v.bn_aggr(lnt[:, 24:26], lnt[:, 0:24])
                    ctr.inc(s_v, 1); ins.then_inc(s_v, 1); v.wait_ge(s_v, ctr.get(s_v))
                    ctr.inc(s_v, 1)
                    v.tensor_scalar(lnt[:, 26:27], lnt[:, 25:26], eps_val, None, ALU.add).then_inc(s_v, 1)
                V1 = ctr.get(s_v)

                @blk.scalar
                def _(s, V1=V1):
                    s.wait_ge(s_v, V1)
                    ctr.inc(s_s, 1)
                    s.activation(lnt[:, 27:28], lnt[:, 26:27], AF.Sqrt).then_inc(s_s, 1)
                S1 = ctr.get(s_s)

                @blk.vector
                def _(v, S1=S1):
                    v.wait_ge(s_s, S1)
                    ins = v.reciprocal(lnt[:, dst_col:dst_col + 1], lnt[:, 27:28])
                    ctr.inc(s_v, 1); ins.then_inc(s_v, 1); v.wait_ge(s_v, ctr.get(s_v))
                    ctr.inc(s_v, 1)
                    v.scalar_tensor_tensor(out=lnt[:, dst_col + 1:dst_col + 2], in0=lnt[:, 24:25],
                                           scalar=-1.0, in1=lnt[:, dst_col:dst_col + 1],
                                           op0=ALU.mult, op1=ALU.mult).then_inc(s_v, 1)
                return ctr.get(s_v)

            @blk.vector
            def _(v, ready=ready):
                v.wait_ge(s_s, ready)
            LN1 = emit_ln(at, EPS_LN, 28)

            @blk.scalar
            def _(s, at=at, LN1=LN1):
                s.wait_ge(s_v, LN1)
                ctr.inc(s_s, 1)
                s.activation(xnt[:, :], at, AF.Identity,
                             bias=lnt[:, 29:30], scale=lnt[:, 28:29]).then_inc(s_s, 1)
            A1 = ctr.get(s_s)

            @blk.vector
            def _(v, x2=x2, xt=xt, A1=A1, XL=XL, tb=tb):
                v.wait_ge(s_s, A1)
                v.wait_ge(ldp[tb % 2], XL)
                v.wait_ge(stp[tb % 2], ctr.get(stp[tb % 2]))
                ctr.inc(s_v, 1)
                v.tensor_tensor(x2, xnt[:, :], xt, ALU.add).then_inc(s_v, 1)
            X2V = ctr.get(s_v)
            x2v_hist[tb] = X2V

            @blk.gpsimd
            def _(g, tb=tb, x2=x2, X2V=X2V):
                g.wait_ge(s_v, X2V)
                g.wait_ge(stp[tb % 2], ctr.get(stp[tb % 2]))
                g.dma_start(out=x2_scr[tb * 128:(tb + 1) * 128, :], in_=x2).then_inc(stp[tb % 2], 16)
                ctr.inc(stp[tb % 2], 16)

            LN2 = emit_ln(x2, EPS_FFN, 30)

            @blk.scalar
            def _(s, x2=x2, LN2=LN2):
                s.wait_ge(s_v, LN2)
                ctr.inc(s_s, 1)
                s.activation(xnt[:, :], x2, AF.Identity,
                             bias=lnt[:, 31:32], scale=lnt[:, 30:31]).then_inc(s_s, 1)
            A2 = ctr.get(s_s)

            @blk.vector
            def _(v, A2=A2, tb=tb):
                v.wait_ge(s_s, A2)
                if tb >= 2:
                    v.wait_ge(s_mm, tq_hist[tb - 2])
                ins = v.tensor_reduce(lnt[:, 32:33], xnt[:, :], AX.X, ALU.max, apply_absolute_value=True)
                ctr.inc(s_v, 1); ins.then_inc(s_v, 1); v.wait_ge(s_v, ctr.get(s_v))
                ins = v.reciprocal(lnt[:, 33:34], lnt[:, 32:33])
                ctr.inc(s_v, 1); ins.then_inc(s_v, 1); v.wait_ge(s_v, ctr.get(s_v))
                v.tensor_scalar(lnt[:, 34:35], lnt[:, 33:34], 127.0, None, ALU.mult)
                ins = v.tensor_scalar(lnt[:, 35:36], lnt[:, 32:33], 1.0 / 127.0, None, ALU.mult)
                ctr.inc(s_v, 1); ins.then_inc(s_v, 1); v.wait_ge(s_v, ctr.get(s_v))
                ins = v.tensor_scalar(xnt[:, :], xnt[:, :], lnt[:, 34:35], None, ALU.mult)
                ctr.inc(s_v, 1); ins.then_inc(s_v, 1); v.wait_ge(s_v, ctr.get(s_v))
                ins = v.tensor_copy(inti[:, :], xnt[:, :])
                ctr.inc(s_v, 1); ins.then_inc(s_v, 1); v.wait_ge(s_v, ctr.get(s_v))
                ctr.inc(s_v, 1)
                v.tensor_scalar(xq_t[:, :], inti[:, :], lnt[:, 35:36], None, ALU.mult).then_inc(s_v, 1)
            QV = ctr.get(s_v)

            @blk.tensor
            def _(t, tb=tb, QV=QV):
                t.wait_ge(s_v, QV)
                if tb > 0:
                    t.wait_ge(s_s, cp_hist[tb - 1])
                mm = None
                for db in range(NDB):
                    mm = t.matmul(ps_big[:, db * 128:(db + 1) * 128],
                                  xq_t[:, db * 128:(db + 1) * 128],
                                  c_ident[:, :], start=True, stop=True)
                ctr.inc(s_mm, 1)
                mm.then_inc(s_mm, 1)
            TQ = ctr.get(s_mm)
            tq_hist[tb] = TQ

            @blk.scalar
            def _(s, tb=tb, TQ=TQ):
                s.wait_ge(s_mm, TQ)
                ctr.inc(s_s, 1)
                s.copy(
                    xq_f.ap().rearrange("p (db c) -> p db c", db=NDB)[:, :, tb * 128:(tb + 1) * 128],
                    ps_big.ap().rearrange("p (db c) -> p db c", db=NDB),
                ).then_inc(s_s, 1)
            cp_hist[tb] = ctr.get(s_s)



        for fg in range(4):
            wsl = woutb[:, (fg % 2) * 16 * 512:][:, :16 * 512]

            @blk.gpsimd
            def _(g, fg=fg, wsl=wsl):
                if fg >= 2:
                    g.wait_ge(s_mm, ctr.get(s_mm))
                g.wait_ge(stp[fg % 2], ctr.get(stp[fg % 2]))
                g.dma_start(out=wsl.rearrange("p (db f) -> p db f", f=512),
                            in_=wout_b[fg].rearrange("(db p) f -> p db f", p=128)).then_inc(stp[fg % 2], 16)
                ctr.inc(stp[fg % 2], 16)
            WL = ctr.get(stp[fg % 2])

            ready_hist = {}
            for tb in range(8):
                it = fg * 8 + tb
                pb = it % 2
                psl = ps_op0 if tb % 2 == 0 else ps_op1
                pc = pcol[:, pb * 2048:pb * 2048 + 2048]

                @blk.sync
                def _(q, tb=tb, pc=pc, pb=pb, it=it):
                    if it >= 2:
                        q.wait_ge(s_mm, projmm_hist[it - 2])
                    q.wait_ge(ldp[pb], ctr.get(ldp[pb]))
                    q.dma_start(out=pc, in_=p_scr[tb]).then_inc(ldp[pb], 16)
                    ctr.inc(ldp[pb], 16)
                PL = ctr.get(ldp[pb])

                @blk.tensor
                def _(t, fg=fg, tb=tb, pc=pc, wsl=wsl, psl=psl, PL=PL, WL=WL, pb=pb, it=it):
                    t.wait_ge(ldp[pb], PL)
                    t.wait_ge(stp[fg % 2], WL)
                    if it >= 2:
                        t.wait_ge(s_s, evac_hist[it - 2])
                    mm = None
                    for db in range(NDB):
                        mm = t.matmul(psl[:, :],
                                      pc[:, db * 128:(db + 1) * 128],
                                      wsl.rearrange("p (db f) -> p db f", f=512)[:, db, :],
                                      start=(db == 0), stop=(db == NDB - 1))
                    ctr.inc(s_mm, 1)
                    mm.then_inc(s_mm, 1)
                OPM = ctr.get(s_mm)
                projmm_hist[it] = OPM

                @blk.scalar
                def _(s, fg=fg, tb=tb, psl=psl, OPM=OPM):
                    s.wait_ge(s_mm, OPM)
                    ctr.inc(s_s, 1)
                    s.copy(attn_all[:, tb * D + fg * 512: tb * D + fg * 512 + 512], psl[:, :]).then_inc(s_s, 1)
                evac_hist[it] = ctr.get(s_s)
                if fg == 3:
                    ready_hist[tb] = ctr.get(s_s)
            if fg == 3:
                for tb in range(8):
                    emit_lnq(tb, ready_hist[tb])
        OPDONE = ctr.get(s_s)

def _ffn_gu(nc, blk, ctr, sems, env):
    s_mm, s_v, s_s = sems["mm"], sems["v"], sems["s"]
    s_lda, s_ldb, s_sta, s_stb = sems["lda"], sems["ldb"], sems["sta"], sems["stb"]
    ldp = [sems["ga"], sems["gb"]]
    xq_f, p2 = env["xq_f"], env["p2"]
    wg_b, wu_b = env["wg_b"], env["wu_b"]
    from contextlib import ExitStack

    _phase_barrier(nc, blk, ctr, sems)
    with ExitStack() as stf:
        E1 = stf.enter_context
        wgb = E1(nc.sbuf_tensor("wgb", [128, 2 * 2048], BF16))
        wub = E1(nc.sbuf_tensor("wub", [128, 2 * 2048], BF16))
        gs = E1(nc.sbuf_tensor("gs", [128, 2 * 512], BF16))
        ps_g0 = E1(nc.psum_tensor("ps_g0", [128, 512], F32))
        ps_g1 = E1(nc.psum_tensor("ps_g1", [128, 512], F32))
        ps_u0 = E1(nc.psum_tensor("ps_u0", [128, 512], F32))
        ps_u1 = E1(nc.psum_tensor("ps_u1", [128, 512], F32))

        mm_hist = {}
        load_done = {}
        sign_done = {}

        def emit_load(hb):
            buf = hb % 2
            wg = wgb[:, buf * 2048:buf * 2048 + 2048]
            wu = wub[:, buf * 2048:buf * 2048 + 2048]

            @blk.gpsimd
            def _(g, hb=hb, wg=wg, wu=wu, buf=buf):
                if hb >= 2:
                    g.wait_ge(s_mm, mm_hist[hb - 2])
                g.wait_ge(ldp[buf], ctr.get(ldp[buf]))
                g.dma_start(out=wg.rearrange("p (db f) -> p db f", f=128),
                            in_=wg_b[hb].rearrange("(db p) f -> p db f", p=128)).then_inc(ldp[buf], 16)
                g.dma_start(out=wu.rearrange("p (db f) -> p db f", f=128),
                            in_=wu_b[hb].rearrange("(db p) f -> p db f", p=128)).then_inc(ldp[buf], 16)
                ctr.inc(ldp[buf], 32)
            load_done[hb] = ctr.get(ldp[buf])

        def emit_sign(hb):
            buf = hb % 2
            wg = wgb[:, buf * 2048:buf * 2048 + 2048]
            wu = wub[:, buf * 2048:buf * 2048 + 2048]

            @blk.scalar
            def _(s, hb=hb, wg=wg, wu=wu, buf=buf):
                s.wait_ge(ldp[buf], load_done[hb])
                s.sign(wg, wg)
                ctr.inc(s_s, 1)
                s.sign(wu, wu).then_inc(s_s, 1)
            sign_done[hb] = ctr.get(s_s)

        emit_load(0)
        emit_load(1)
        emit_sign(0)

        for hb in range(NHB):
            buf = hb % 2
            wg = wgb[:, buf * 2048:buf * 2048 + 2048]
            wu = wub[:, buf * 2048:buf * 2048 + 2048]
            if hb + 1 < NHB:
                emit_sign(hb + 1)
            for th in range(2):
                psg = ps_g0 if th == 0 else ps_g1
                psu = ps_u0 if th == 0 else ps_u1

                @blk.tensor
                def _(t, hb=hb, th=th, wg=wg, wu=wu, psg=psg, psu=psu):
                    t.wait_ge(s_s, sign_done[hb])
                    if hb > 0:
                        t.wait_ge(s_v, ctr.get(s_v) - 1)
                    mm = None
                    for db in range(NDB):
                        mm = t.matmul(psg[:, :], wg[:, db * 128:(db + 1) * 128],
                                      xq_f.ap()[:, db * TPC + th * 512:][:, :512],
                                      start=(db == 0), stop=(db == NDB - 1))
                    for db in range(NDB):
                        mm = t.matmul(psu[:, :], wu[:, db * 128:(db + 1) * 128],
                                      xq_f.ap()[:, db * TPC + th * 512:][:, :512],
                                      start=(db == 0), stop=(db == NDB - 1))
                    ctr.inc(s_mm, 1)
                    mm.then_inc(s_mm, 1)
                GM = ctr.get(s_mm)

                @blk.scalar
                def _(s, psg=psg, GM=GM, th=th):
                    s.wait_ge(s_mm, GM)
                    ctr.inc(s_s, 1)
                    s.activation(gs[:, th * 512:(th + 1) * 512], psg[:, :], AF.Silu).then_inc(s_s, 1)
                GS = ctr.get(s_s)

                @blk.vector
                def _(v, hb=hb, th=th, psu=psu, GS=GS):
                    v.wait_ge(s_s, GS)
                    ctr.inc(s_v, 1)
                    v.tensor_tensor(p2.ap()[:, hb * TPC + th * 512:][:, :512],
                                    psu[:, :], gs[:, th * 512:(th + 1) * 512], ALU.mult).then_inc(s_v, 1)
            mm_hist[hb] = ctr.get(s_mm)
            if hb + 2 < NHB:
                emit_load(hb + 2)


def _ffn_down(nc, blk, ctr, sems, env):
    s_mm, s_v, s_s = sems["mm"], sems["v"], sems["s"]
    s_lda, s_ldb, s_sta, s_stb = sems["lda"], sems["ldb"], sems["sta"], sems["stb"]
    ldp = [sems["ga"], sems["gb"]]
    stp = [s_sta, s_stb]
    p2, wd_b, ffn_scr = env["p2"], env["wd_b"], env["ffn_scr"]
    from contextlib import ExitStack

    _phase_barrier(nc, blk, ctr, sems)
    with ExitStack() as std_:
        E1 = std_.enter_context
        wdb = E1(nc.sbuf_tensor("wdb", [128, 2 * NHB * 128], BF16))
        fout = E1(nc.sbuf_tensor("fout", [128, 2 * TPC], BF16))
        ps_d0 = E1(nc.psum_tensor("ps_d0", [128, TPC], F32))
        ps_d1 = E1(nc.psum_tensor("ps_d1", [128, TPC], F32))

        mm_hist = {}
        load_done = {}
        sign_done = {}
        fo_hist = {}

        def emit_load(fcb):
            buf = fcb % 2
            wd = wdb[:, buf * NHB * 128:][:, :NHB * 128]

            @blk.gpsimd
            def _(g, fcb=fcb, wd=wd, buf=buf):
                if fcb >= 2:
                    g.wait_ge(s_mm, mm_hist[fcb - 2])
                g.wait_ge(ldp[buf], ctr.get(ldp[buf]))
                g.dma_start(out=wd.rearrange("p (hb f) -> p hb f", f=128),
                            in_=wd_b[fcb].rearrange("(hb p) f -> p hb f", p=128)).then_inc(ldp[buf], 16)
                ctr.inc(ldp[buf], 16)
            load_done[fcb] = ctr.get(ldp[buf])

        def emit_sign(fcb):
            buf = fcb % 2
            wd = wdb[:, buf * NHB * 128:][:, :NHB * 128]

            @blk.scalar
            def _(s, fcb=fcb, wd=wd, buf=buf):
                s.wait_ge(ldp[buf], load_done[fcb])
                ctr.inc(s_s, 1)
                s.sign(wd, wd).then_inc(s_s, 1)
            sign_done[fcb] = ctr.get(s_s)

        emit_load(0)
        emit_load(1)
        emit_sign(0)

        for fcb in range(16):
            buf = fcb % 2
            wd = wdb[:, buf * NHB * 128:][:, :NHB * 128]
            psd = ps_d0 if fcb % 2 == 0 else ps_d1
            fo = fout[:, buf * TPC:buf * TPC + TPC]
            if fcb + 1 < 16:
                emit_sign(fcb + 1)

            @blk.tensor
            def _(t, fcb=fcb, wd=wd, psd=psd):
                t.wait_ge(s_s, sign_done[fcb])
                if fcb >= 2:
                    t.wait_ge(s_s, fo_hist[fcb - 2])
                mm = None
                for th in range(2):
                    for hb in range(NHB):
                        mm = t.matmul(psd[:, th * 512:(th + 1) * 512],
                                      wd[:, hb * 128:(hb + 1) * 128],
                                      p2.ap()[:, hb * TPC + th * 512:][:, :512],
                                      start=(hb == 0), stop=(hb == NHB - 1))
                ctr.inc(s_mm, 1)
                mm.then_inc(s_mm, 1)
            DM = ctr.get(s_mm)

            @blk.scalar
            def _(s, psd=psd, fo=fo, DM=DM, buf=buf, fcb=fcb):
                s.wait_ge(s_mm, DM)
                if fcb >= 2:
                    s.wait_ge(stp[buf], ctr.get(stp[buf]))
                ctr.inc(s_s, 1)
                s.copy(fo, psd[:, :]).then_inc(s_s, 1)
            FE = ctr.get(s_s)
            fo_hist[fcb] = FE

            @blk.gpsimd
            def _(g, fcb=fcb, fo=fo, FE=FE, buf=buf):
                g.wait_ge(s_s, FE)
                g.wait_ge(stp[buf], ctr.get(stp[buf]))
                g.dma_start(out=ffn_scr[fcb], in_=fo).then_inc(stp[buf], 16)
                ctr.inc(stp[buf], 16)
            mm_hist[fcb] = ctr.get(s_mm)
            if fcb + 2 < 16:
                emit_load(fcb + 2)


def _final(nc, blk, ctr, sems, env):
    s_mm, s_v, s_s = sems["mm"], sems["v"], sems["s"]
    s_lda, s_ldb, s_sta, s_stb = sems["lda"], sems["ldb"], sems["sta"], sems["stb"]
    ldp = [s_lda, s_ldb]
    stp = [s_sta, s_stb]
    c_ident = env["c_ident"]
    ffn_scr, x2_scr, y_out = env["ffn_scr"], env["x2_scr"], env["y_out"]
    from contextlib import ExitStack

    _phase_barrier(nc, blk, ctr, sems)
    with ExitStack() as stf:
        E1 = stf.enter_context
        ftile = E1(nc.sbuf_tensor("ftile", [128, 2 * D], BF16))
        x2l = E1(nc.sbuf_tensor("x2l", [128, 2 * D], F32))
        yt = E1(nc.sbuf_tensor("yt", [128, 2 * D], F32))
        lnt_b = E1(nc.sbuf_tensor("lntF", [128, 64], F32))
        ps_fin0 = E1(nc.psum_tensor("ps_fin0", [128, D], F32))
        ps_fin1 = E1(nc.psum_tensor("ps_fin1", [128, D], F32))

        for tb in range(8):
            buf = tb % 2
            ft = ftile[:, buf * D:buf * D + D]
            xx = x2l[:, buf * D:buf * D + D]
            yy = yt[:, buf * D:buf * D + D]
            lnt = lnt_b[:, buf * 32:buf * 32 + 32]
            psf = ps_fin0 if buf == 0 else ps_fin1

            @blk.sync
            def _(q, tb=tb, ft=ft, xx=xx, buf=buf):
                if tb >= 2:
                    q.wait_ge(s_mm, ctr.get(s_mm))
                    q.wait_ge(s_v, ctr.get(s_v))
                q.wait_ge(ldp[buf], ctr.get(ldp[buf]))
                q.dma_start(out=ft.rearrange("p (db t) -> p db t", t=128),
                            in_=ffn_scr[:, :, tb * 128:(tb + 1) * 128].rearrange("db p t -> p db t")).then_inc(ldp[buf], 16)
                q.dma_start(out=xx, in_=x2_scr[tb * 128:(tb + 1) * 128, :]).then_inc(ldp[buf], 16)
                ctr.inc(ldp[buf], 32)
            FL = ctr.get(ldp[buf])

            @blk.tensor
            def _(t, tb=tb, ft=ft, psf=psf, FL=FL, buf=buf):
                t.wait_ge(ldp[buf], FL)
                if tb >= 2:
                    t.wait_ge(s_s, ctr.get(s_s))
                mm = None
                for db in range(NDB):
                    mm = t.matmul(psf[:, db * 128:(db + 1) * 128],
                                  ft[:, db * 128:(db + 1) * 128],
                                  c_ident[:, :], start=True, stop=True)
                ctr.inc(s_mm, 1)
                mm.then_inc(s_mm, 1)
            FM = ctr.get(s_mm)

            @blk.vector
            def _(v, psf=psf, FM=FM):
                v.wait_ge(s_mm, FM)
                ins = None
                for i4 in range(4):
                    ins = v.bn_stats(lnt[:, i4 * 6:(i4 + 1) * 6], psf[:, i4 * 512:(i4 + 1) * 512])
                ctr.inc(s_v, 1); ins.then_inc(s_v, 1); v.wait_ge(s_v, ctr.get(s_v))
                ins = v.bn_aggr(lnt[:, 24:26], lnt[:, 0:24])
                ctr.inc(s_v, 1); ins.then_inc(s_v, 1); v.wait_ge(s_v, ctr.get(s_v))
                ctr.inc(s_v, 1)
                v.tensor_scalar(lnt[:, 26:27], lnt[:, 25:26], EPS_LN, None, ALU.add).then_inc(s_v, 1)
            V1 = ctr.get(s_v)

            @blk.scalar
            def _(s, V1=V1):
                s.wait_ge(s_v, V1)
                ctr.inc(s_s, 1)
                s.activation(lnt[:, 27:28], lnt[:, 26:27], AF.Sqrt).then_inc(s_s, 1)
            S1 = ctr.get(s_s)

            @blk.vector
            def _(v, S1=S1):
                v.wait_ge(s_s, S1)
                ins = v.reciprocal(lnt[:, 28:29], lnt[:, 27:28])
                ctr.inc(s_v, 1); ins.then_inc(s_v, 1); v.wait_ge(s_v, ctr.get(s_v))
                ctr.inc(s_v, 1)
                v.scalar_tensor_tensor(out=lnt[:, 29:30], in0=lnt[:, 24:25],
                                       scalar=-1.0, in1=lnt[:, 28:29],
                                       op0=ALU.mult, op1=ALU.mult).then_inc(s_v, 1)
            V2 = ctr.get(s_v)

            @blk.scalar
            def _(s, psf=psf, yy=yy, V2=V2, buf=buf, tb=tb):
                s.wait_ge(s_v, V2)
                if tb >= 2:
                    s.wait_ge(stp[buf], ctr.get(stp[buf]))
                ctr.inc(s_s, 1)
                s.activation(yy, psf[:, :], AF.Identity,
                             bias=lnt[:, 29:30], scale=lnt[:, 28:29]).then_inc(s_s, 1)
            S2 = ctr.get(s_s)

            @blk.vector
            def _(v, yy=yy, xx=xx, S2=S2):
                v.wait_ge(s_s, S2)
                ctr.inc(s_v, 1)
                v.tensor_tensor(yy, yy, xx, ALU.add).then_inc(s_v, 1)
            YV = ctr.get(s_v)

            @blk.gpsimd
            def _(g, tb=tb, yy=yy, YV=YV, buf=buf):
                g.wait_ge(s_v, YV)
                g.wait_ge(stp[buf], ctr.get(stp[buf]))
                g.dma_start(out=y_out[tb * 128:(tb + 1) * 128, :], in_=yy).then_inc(stp[buf], 16)
                ctr.inc(stp[buf], 16)

        @blk.sync
        def _(q):
            q.wait_ge(s_sta, ctr.get(s_sta))
            q.wait_ge(s_stb, ctr.get(s_stb))


def host_prep(inputs):
    x = np.ascontiguousarray(np.asarray(inputs["L0_attn_x"], np.float32))
    w_qkv = np.asarray(inputs["L0_attn_w_qkv"], np.float32)
    w_gate = np.asarray(inputs["L0_attn_w_gate"], np.float32)
    w_beta = np.asarray(inputs["L0_attn_w_beta"], np.float32)
    w_alpha = np.asarray(inputs["L0_attn_w_alpha"], np.float32)
    log_A = np.asarray(inputs["L0_attn_log_A"], np.float32)
    dt_bias = np.asarray(inputs["L0_attn_dt_bias"], np.float32)
    conv_w = np.asarray(inputs["L0_attn_conv_w"], np.float32)
    rms_w = np.asarray(inputs["L0_attn_rms_w"], np.float32)
    w_out = np.asarray(inputs["L0_attn_w_out"], np.float32)
    W_g = np.asarray(inputs["L0_ffn_W_g"], np.float32)
    W_u = np.asarray(inputs["L0_ffn_W_u"], np.float32)
    W_d = np.asarray(inputs["L0_ffn_W_d"], np.float32)

    wqkv_bl = np.ascontiguousarray(w_qkv.T.reshape(D, NFQ, 128).transpose(1, 0, 2))
    wgate_bl = np.ascontiguousarray(w_gate.T.reshape(D, NFG, 128).transpose(1, 0, 2))
    wba = np.ascontiguousarray(np.concatenate([w_beta, w_alpha], axis=0).T)
    convw = np.ascontiguousarray(
        conv_w[:, 0, :].reshape(NFQ, 128, 4).transpose(1, 0, 2).reshape(128, NFQ * 4))
    la = np.ascontiguousarray(np.stack([log_A, dt_bias], axis=1))
    rmsw = np.ascontiguousarray(rms_w[:, None])
    wout_bl = np.ascontiguousarray(w_out.T.reshape(D, 4, 512).transpose(1, 0, 2))
    wg_bl = np.ascontiguousarray(W_g.T.reshape(D, NHB, 128).transpose(1, 0, 2))
    wu_bl = np.ascontiguousarray(W_u.T.reshape(D, NHB, 128).transpose(1, 0, 2))
    wd_bl = np.ascontiguousarray(W_d.T.reshape(FFH, 16, 128).transpose(1, 0, 2))

    ident_bf = np.eye(128, dtype=NPBF16)
    ident_f = np.eye(128, dtype=np.float32)
    r_idx, c_idx = np.meshgrid(np.arange(128), np.arange(128), indexing="ij")
    maskLT = np.where(c_idx < r_idx, 0.0, NEG).astype(np.float32)
    maskUT = np.where(r_idx < c_idx, 0.0, NEG).astype(np.float32)
    maskUD = np.where(r_idx <= c_idx, -LSD, NEG).astype(np.float32)
    ones_col = np.ones((128, 1), dtype=NPBF16)
    ones_row_f = np.ones((1, 128), dtype=np.float32)
    zeros_f = np.zeros((128, 128), dtype=np.float32)
    ccols = np.zeros((128, 8), np.float32)
    ccols[:, 0] = EPS_L2
    ccols[:, 1] = EPS_RMS
    ccols[:, 2] = -LSD
    ccols[:, 3] = 2.0
    ccols[:, 4] = EPS_LN
    ccols[:, 5] = EPS_FFN
    ccols[:, 6] = 127.0
    ccols[:, 7] = 1.0

    in_maps = []
    for core in range(8):
        b, g = core // 4, core % 4
        t0 = g * TPC
        xs = np.zeros((TP, D), np.float32)
        lo = max(0, t0 - HALO)
        xs[HALO - (t0 - lo): HALO - (t0 - lo) + (t0 + TPC - lo)] = x[b, lo:t0 + TPC]
        hop_m = np.zeros((128, 7), np.float32)
        hop_imi = np.zeros((128, 7 * 128), np.float32)
        for j in range(7):
            use = 1.0 if (j // 4 == b and j % 4 < g) else 0.0
            hop_m[:, j] = use
            hop_imi[:, j * 128:(j + 1) * 128] = (1.0 - use) * np.eye(128, dtype=np.float32)
        in_maps.append({
            "x": xs, "wqkv_b": wqkv_bl, "wgate_b": wgate_bl, "wba": wba,
            "convw": convw, "loga_params": la, "rmsw": rmsw,
            "wout_b": wout_bl, "wg_b": wg_bl, "wu_b": wu_bl, "wd_b": wd_bl,
            "ident_bf": ident_bf, "ident_f": ident_f, "maskLT": maskLT,
            "maskUT": maskUT, "maskUD": maskUD,
            "ones_col_bf": ones_col, "ones_row_f": ones_row_f, "ccols": ccols,
            "ones_row_bf": np.ones((1, 128), dtype=NPBF16), "zeros_f": zeros_f,
            "hop_mcol": hop_m, "hop_imi": hop_imi,
        })
    return in_maps


def kernel(**inputs):
    in_maps = host_prep(inputs)
    nc, _ = build()
    res = run_bass_kernel_spmd(nc, in_maps, core_ids=list(range(8)))
    y = np.empty((2, 4096, D), np.float32)
    for core in range(8):
        b, g = core // 4, core % 4
        y[b, g * TPC:(g + 1) * TPC] = res.results[core]["y"]
    return y


# version 17 (final)
# speedup vs baseline: 3.0640x; 1.0219x over previous
import numpy as np
import ml_dtypes
import concourse.bass as bass
import concourse.mybir as mybir
from concourse.bass_utils import run_bass_kernel_spmd
from concourse import library_config

F32 = mybir.dt.float32
BF16 = mybir.dt.bfloat16
I32 = mybir.dt.int32
AF = mybir.ActivationFunctionType
ALU = mybir.AluOpType
AX = mybir.AxisListType
NPBF16 = ml_dtypes.bfloat16

D = 2048
TPC = 1024
HALO = 3
TP = 1032
TG = 344
C = 128
NCH = 8
HV, HQ, DK, DV = 16, 8, 128, 128
NDB = 16
NFQ, NFG = 32, 16
NFB = NFQ + NFG + 1
FFH = 8192
NHB = 64
NEG = -1e30
EPS_LN = 1e-5
EPS_L2 = 1e-6
EPS_RMS = 1e-6
EPS_FFN = 1e-8
LSD = float(np.log(np.sqrt(DK)))
NEWTON_ITERS = 6


class Ctr:
    def __init__(self):
        self.v = {}

    def inc(self, sem, k=1):
        self.v[sem.name] = self.v.get(sem.name, 0) + k
        return self.v[sem.name]

    def get(self, sem):
        return self.v.get(sem.name, 0)


def build(probe=None, upto="all"):
    nc = bass.Bass("TRN2", target_bir_lowering=False, debug=False)

    x_in = nc.dram_tensor("x", [TP, D], F32, kind="ExternalInput").ap()
    wqkv_b = nc.dram_tensor("wqkv_b", [NFQ, D, 128], F32, kind="ExternalInput").ap()
    wgate_b = nc.dram_tensor("wgate_b", [NFG, D, 128], F32, kind="ExternalInput").ap()
    wba_in = nc.dram_tensor("wba", [D, 32], F32, kind="ExternalInput").ap()
    convw_in = nc.dram_tensor("convw", [128, NFQ * 4], F32, kind="ExternalInput").ap()
    la_in = nc.dram_tensor("loga_params", [HV, 2], F32, kind="ExternalInput").ap()
    rmsw_in = nc.dram_tensor("rmsw", [128, 1], F32, kind="ExternalInput").ap()
    wout_b = nc.dram_tensor("wout_b", [4, D, 512], F32, kind="ExternalInput").ap()
    wg_b = nc.dram_tensor("wg_b", [NHB, D, 128], F32, kind="ExternalInput").ap()
    wu_b = nc.dram_tensor("wu_b", [NHB, D, 128], F32, kind="ExternalInput").ap()
    wd_b = nc.dram_tensor("wd_b", [16, FFH, 128], F32, kind="ExternalInput").ap()
    ident_in = nc.dram_tensor("ident_bf", [128, 128], BF16, kind="ExternalInput").ap()
    identf_in = nc.dram_tensor("ident_f", [128, 128], F32, kind="ExternalInput").ap()
    maskLT_in = nc.dram_tensor("maskLT", [128, 128], F32, kind="ExternalInput").ap()
    maskUT_in = nc.dram_tensor("maskUT", [128, 128], F32, kind="ExternalInput").ap()
    maskUD_in = nc.dram_tensor("maskUD", [128, 128], F32, kind="ExternalInput").ap()
    onesc_in = nc.dram_tensor("ones_col_bf", [128, 1], BF16, kind="ExternalInput").ap()
    onesrf_in = nc.dram_tensor("ones_row_f", [1, 128], F32, kind="ExternalInput").ap()
    onesr_in = nc.dram_tensor("ones_row_bf", [1, 128], BF16, kind="ExternalInput").ap()
    zeros_in = nc.dram_tensor("zeros_f", [128, 128], F32, kind="ExternalInput").ap()
    ccols_in = nc.dram_tensor("ccols", [128, 8], F32, kind="ExternalInput").ap()
    hopm_in = nc.dram_tensor("hop_mcol", [128, 7], F32, kind="ExternalInput").ap()
    hopimi_in = nc.dram_tensor("hop_imi", [128, 7 * 128], F32, kind="ExternalInput").ap()

    y_out = nc.dram_tensor("y", [TPC, D], F32, kind="ExternalOutput").ap()

    PROBE_SHAPES = {
        "pre": ([NFB, 128, TP], BF16),
        "qkv": ([4, 1024, TPC], BF16),
        "rows": ([HV, 4 * TPC], F32),
        "spill": ([NCH, 4, 128, HV * 128], BF16),
        "comp": ([2, 128, HV * 128], BF16),
        "state": ([128, HV * 128], BF16),
        "pf": ([NCH, 128, NDB * 128], BF16),
        "x2": ([TPC, D], F32),
        "xq": ([128, NDB * TPC], BF16),
        "p2": ([128, NHB * TPC], BF16),
        "ffn": ([NDB, 128, TPC], BF16),
    }
    probe_out = None
    if probe is not None:
        shp, dt_ = PROBE_SHAPES[probe]
        probe_out = nc.dram_tensor("probe", shp, dt_, kind="ExternalOutput").ap()

    pre_scr = nc.dram_tensor("pre_scr", [NFB, 128, TP], BF16, kind="Internal").ap()
    spill = nc.dram_tensor("spill", [NCH, 4, 128, HV * 128], BF16, kind="Internal").ap()
    p_scr = nc.dram_tensor("p_scr", [NCH, 128, NDB * 128], BF16, kind="Internal").ap()
    x2_scr = nc.dram_tensor("x2_scr", [TPC, D], F32, kind="Internal").ap()
    ffn_scr = nc.dram_tensor("ffn_scr", [NDB, 128, TPC], BF16, kind="Internal").ap()
    cc_in = nc.dram_tensor("cc_in", [128, HV * 2 * 128], BF16, kind="Internal").ap()
    cc_out = nc.dram_tensor("cc_out", [8 * 128, HV * 2 * 128], BF16, kind="Internal").ap()

    ctr = Ctr()
    from contextlib import ExitStack

    with ExitStack() as top:
        E = top.enter_context
        s_ld = E(nc.semaphore("s_ld"))
        s_lda = E(nc.semaphore("s_lda"))
        s_ldb = E(nc.semaphore("s_ldb"))
        s_st = E(nc.semaphore("s_st"))
        s_sta = E(nc.semaphore("s_sta"))
        s_stb = E(nc.semaphore("s_stb"))
        s_ga = E(nc.semaphore("s_ga"))
        s_gb = E(nc.semaphore("s_gb"))
        s_mm = E(nc.semaphore("s_mm"))
        s_v = E(nc.semaphore("s_v"))
        s_s = E(nc.semaphore("s_s"))
        s_g = E(nc.semaphore("s_g"))
        s_cc = E(nc.semaphore("s_cc"))
        sems = dict(ld=s_ld, lda=s_lda, ldb=s_ldb, st=s_st, sta=s_sta, stb=s_stb,
                    ga=s_ga, gb=s_gb, mm=s_mm, v=s_v, s=s_s, g=s_g, cc=s_cc)

        c_ident = E(nc.sbuf_tensor("c_ident", [128, 128], BF16))
        c_identf = E(nc.sbuf_tensor("c_identf", [128, 128], F32))
        c_maskLT = E(nc.sbuf_tensor("c_maskLT", [128, 128], F32))
        c_maskUT = E(nc.sbuf_tensor("c_maskUT", [128, 128], F32))
        c_maskUD = E(nc.sbuf_tensor("c_maskUD", [128, 128], F32))
        c_onec = E(nc.sbuf_tensor("c_onec", [128, 1], BF16))
        c_onerf = E(nc.sbuf_tensor("c_onerf", [1, 128], F32))
        c_oner = E(nc.sbuf_tensor("c_oner", [1, 128], BF16))
        c_zero = E(nc.sbuf_tensor("c_zero", [128, 128], F32))
        c_cols = E(nc.sbuf_tensor("c_cols", [128, 8], F32))
        c_hopm = E(nc.sbuf_tensor("c_hopm", [128, 7], F32))
        c_hopimi = E(nc.sbuf_tensor("c_hopimi", [128, 7 * 128], F32))
        c_rmsw = E(nc.sbuf_tensor("c_rmsw", [128, 1], F32))
        c_convw = E(nc.sbuf_tensor("c_convw", [128, NFQ * 4], F32))
        c_la = E(nc.sbuf_tensor("c_la", [HV, 8], F32))

        blk = E(nc.Block())

        @blk.sync
        def _(q):
            for dst, src in [
                (c_ident[:, :], ident_in[:, :]), (c_identf[:, :], identf_in[:, :]),
                (c_maskLT[:, :], maskLT_in[:, :]), (c_maskUT[:, :], maskUT_in[:, :]),
                (c_maskUD[:, :], maskUD_in[:, :]), (c_onec[:, :], onesc_in[:, :]),
                (c_onerf[:, :], onesrf_in[:, :]), (c_oner[:, :], onesr_in[:, :]),
                (c_zero[:, :], zeros_in[:, :]), (c_cols[:, :], ccols_in[:, :]),
                (c_hopm[:, :], hopm_in[:, :]),
                (c_hopimi[:, :], hopimi_in[:, :]),
                (c_rmsw[:, :], rmsw_in[:, :]),
                (c_convw[:, :], convw_in[:, :]),
                (c_la[:, 0:2], la_in[:, :]),
            ]:
                q.dma_start(out=dst, in_=src).then_inc(s_ld, 16)
                ctr.inc(s_ld, 16)
        CONSTS = ctr.get(s_ld)

        @blk.scalar
        def _(s):
            s.wait_ge(s_ld, CONSTS)
            ctr.inc(s_s, 1)
            s.activation(c_la[:, 4:5], c_la[:, 0:1], AF.Exp).then_inc(s_s, 1)

        @blk.vector
        def _(v):
            v.wait_ge(s_s, ctr.get(s_s))
            v.tensor_scalar(c_la[:, 3:4], c_la[:, 4:5], -1.0, None, ALU.mult)
            ctr.inc(s_v, 1)
            v.tensor_scalar(c_la[:, 5:6], c_la[:, 1:2], -1.0, None, ALU.mult).then_inc(s_v, 1)

        with ExitStack() as sc2:
            E2 = sc2.enter_context
            qk_f = E2(nc.sbuf_tensor("qk_f", [128, 16 * TPC], BF16))
            k_tok = E2(nc.sbuf_tensor("k_tok", [128, HQ * TPC], BF16))
            v_tok = E2(nc.sbuf_tensor("v_tok", [128, HV * TPC], BF16))
            barows = E2(nc.sbuf_tensor("barows", [HV, 4 * TPC], F32))
            state = E2(nc.sbuf_tensor("state", [128, HV * 128], BF16))
            compP = E2(nc.sbuf_tensor("compP", [128, HV * 128], BF16))
            compR = E2(nc.sbuf_tensor("compR", [128, HV * 128], BF16))

            _p1(nc, blk, ctr, sems, locals())
            if probe == "pre":
                _probe_dump(nc, blk, ctr, sems, probe_out[0:NFB - 1, :, :], pre_scr[0:NFB - 1, :, :])
                _probe_dump(nc, blk, ctr, sems, probe_out[NFB - 1, 0:32, :], pre_scr[NFB - 1, 0:32, :])
                _finish_zero(nc, blk, ctr, sems, y_out)
                return nc, probe

            _p15(nc, blk, ctr, sems, locals())
            if probe == "qkv":
                @blk.sync
                def _(q):
                    q.wait_ge(s_v, ctr.get(s_v))
                    q.wait_ge(s_s, ctr.get(s_s))
                    for i, (src, off) in enumerate([(qk_f, 0), (qk_f, 8 * TPC), (k_tok, 0), (v_tok, 0)]):
                        q.dma_start(
                            out=probe_out[i].rearrange("(hb p) t -> p hb t", p=128)[:, 0:8, :],
                            in_=src.ap()[:, off:off + 8 * TPC].rearrange("p (hb t) -> p hb t", t=TPC),
                        ).then_inc(s_ld, 16)
                        ctr.inc(s_ld, 16)
                    q.wait_ge(s_ld, ctr.get(s_ld))
                _finish_zero(nc, blk, ctr, sems, y_out)
                return nc, probe
            if probe == "rows":
                @blk.sync
                def _(q):
                    q.wait_ge(s_v, ctr.get(s_v))
                    q.wait_ge(s_s, ctr.get(s_s))
                    q.dma_start(out=probe_out[:, :], in_=barows[:, :]).then_inc(s_ld, 16)
                    ctr.inc(s_ld, 16)
                    q.wait_ge(s_ld, ctr.get(s_ld))
                _finish_zero(nc, blk, ctr, sems, y_out)
                return nc, probe

            _p2a(nc, blk, ctr, sems, locals())
            if probe == "spill":
                _probe_dump(nc, blk, ctr, sems, probe_out[:, :, :, :], spill[:, :, :, :])
                _finish_zero(nc, blk, ctr, sems, y_out)
                return nc, probe
            if probe == "comp":
                @blk.sync
                def _(q):
                    q.wait_ge(s_v, ctr.get(s_v))
                    q.wait_ge(s_s, ctr.get(s_s))
                    q.dma_start(out=probe_out[0], in_=compP[:, :]).then_inc(s_ld, 16)
                    q.dma_start(out=probe_out[1], in_=compR[:, :]).then_inc(s_ld, 16)
                    ctr.inc(s_ld, 32)
                    q.wait_ge(s_ld, ctr.get(s_ld))
                _finish_zero(nc, blk, ctr, sems, y_out)
                return nc, probe

            _cc_hop(nc, blk, ctr, sems, locals())
            if probe == "state":
                @blk.sync
                def _(q):
                    q.wait_ge(s_v, ctr.get(s_v))
                    q.dma_start(out=probe_out[:, :], in_=state[:, :]).then_inc(s_ld, 16)
                    ctr.inc(s_ld, 16)
                    q.wait_ge(s_ld, ctr.get(s_ld))
                _finish_zero(nc, blk, ctr, sems, y_out)
                return nc, probe

            _p2b(nc, blk, ctr, sems, locals())
            if probe == "pf":
                _probe_dump(nc, blk, ctr, sems, probe_out[:, :, :], p_scr[:, :, :])
                _finish_zero(nc, blk, ctr, sems, y_out)
                return nc, probe

        with ExitStack() as sc3:
            E3 = sc3.enter_context
            xq_f = E3(nc.sbuf_tensor("xq_f", [128, NDB * TPC], BF16))

            _p3(nc, blk, ctr, sems, locals())
            if probe in ("x2", "xq"):
                if probe == "x2":
                    _probe_dump(nc, blk, ctr, sems, probe_out[:, :], x2_scr[:, :])
                else:
                    @blk.sync
                    def _(q):
                        q.wait_ge(s_s, ctr.get(s_s))
                        q.dma_start(out=probe_out[:, :], in_=xq_f[:, :]).then_inc(s_ld, 16)
                        ctr.inc(s_ld, 16)
                        q.wait_ge(s_ld, ctr.get(s_ld))
                _finish_zero(nc, blk, ctr, sems, y_out)
                return nc, probe

            with ExitStack() as sff:
                EF = sff.enter_context
                p2 = EF(nc.sbuf_tensor("p2", [128, NHB * TPC], BF16))
                _ffn_gu(nc, blk, ctr, sems, locals())
                if probe == "p2":
                    @blk.sync
                    def _(q):
                        q.wait_ge(s_v, ctr.get(s_v))
                        q.dma_start(out=probe_out[:, :], in_=p2[:, :]).then_inc(s_ld, 16)
                        ctr.inc(s_ld, 16)
                        q.wait_ge(s_ld, ctr.get(s_ld))
                    _finish_zero(nc, blk, ctr, sems, y_out)
                    return nc, probe
                _ffn_down(nc, blk, ctr, sems, locals())
            if probe == "ffn":
                _probe_dump(nc, blk, ctr, sems, probe_out[:, :, :], ffn_scr[:, :, :])
                _finish_zero(nc, blk, ctr, sems, y_out)
                return nc, probe

        _final(nc, blk, ctr, sems, locals())
    return nc, probe


def _phase_barrier(nc, blk, ctr, sems):
    names = ("ld", "lda", "ldb", "st", "sta", "stb", "ga", "gb", "mm", "v", "s", "g", "cc")
    vals = {nm: ctr.get(sems[nm]) for nm in names}
    for eng in ("sync", "tensor", "vector", "scalar", "gpsimd"):
        @getattr(blk, eng)
        def _(e, eng=eng):
            for nm in names:
                e.wait_ge(sems[nm], vals[nm])


def _probe_dump(nc, blk, ctr, sems, dst_ap, src_ap):
    @blk.sync
    def _(q):
        for nm in ("st", "sta", "stb", "v", "s", "g", "mm"):
            q.wait_ge(sems[nm], ctr.get(sems[nm]))
        q.dma_start(out=dst_ap, in_=src_ap).then_inc(sems["ld"], 16)
        ctr.inc(sems["ld"], 16)
        q.wait_ge(sems["ld"], ctr.get(sems["ld"]))


def _finish_zero(nc, blk, ctr, sems, y_out):
    s_st, s_g = sems["st"], sems["g"]
    _phase_barrier(nc, blk, ctr, sems)
    with nc.sbuf_tensor("yzero", [128, D], F32) as yz:
        @blk.gpsimd
        def _(g):
            ctr.inc(s_g, 1)
            g.memset(yz[:], 0.0).then_inc(s_g, 1)
        GV = ctr.get(s_g)

        @blk.sync
        def _(q):
            q.wait_ge(s_g, GV)
            for tb in range(TPC // 128):
                q.dma_start(out=y_out[tb * 128:(tb + 1) * 128, :], in_=yz[:]).then_inc(s_st, 16)
                ctr.inc(s_st, 16)
            q.wait_ge(s_st, ctr.get(s_st))


P1_STORE_HIST = {}


def _p1(nc, blk, ctr, sems, env):
    s_ld, s_st, s_mm, s_v, s_s = sems["ld"], sems["st"], sems["mm"], sems["v"], sems["s"]
    s_lda, s_ldb, s_sta, s_stb = sems["lda"], sems["ldb"], sems["sta"], sems["stb"]
    ldp = [s_lda, s_ldb]
    stp = [s_sta, s_stb]
    x_in, c_ident = env["x_in"], env["c_ident"]
    pre_scr = env["pre_scr"]
    wqkv_b, wgate_b, wba_in = env["wqkv_b"], env["wgate_b"], env["wba_in"]
    CONSTS = env["CONSTS"]
    from contextlib import ExitStack

    with ExitStack() as st1:
        E1 = st1.enter_context
        xtile = E1(nc.sbuf_tensor("xtile", [128, 2 * D], F32))
        xn_t = E1(nc.sbuf_tensor("xn_t", [128, D], BF16))
        xn_f = E1(nc.sbuf_tensor("xn_f", [128, NDB * TP], BF16))
        lnt = E1(nc.sbuf_tensor("lnt", [128, 32], F32))
        ps_ln0 = E1(nc.psum_tensor("ps_ln0", [128, D // 2], F32))
        ps_ln1 = E1(nc.psum_tensor("ps_ln1", [128, D // 2], F32))

        LN_HIST = {}
        for tb in range(9):
            rows = 128 if tb < 8 else TP - 8 * 128
            xt = xtile[:rows, (tb % 2) * D:(tb % 2) * D + D]
            psl = ps_ln0 if tb % 2 == 0 else ps_ln1

            sl = ldp[tb % 2]

            @blk.sync
            def _(q, tb=tb, rows=rows, xt=xt, sl=sl):
                if tb >= 2:
                    q.wait_ge(s_s, ctr.get(s_s))
                q.wait_ge(sl, ctr.get(sl))
                q.dma_start(out=xt, in_=x_in[tb * 128: tb * 128 + rows, :]).then_inc(sl, 16)
                ctr.inc(sl, 16)
            XW = ctr.get(sl)

            @blk.vector
            def _(v, rows=rows, xt=xt, XW=XW, sl=sl):
                v.wait_ge(sl, XW)
                ins = None
                for i4 in range(4):
                    ins = v.bn_stats(lnt[:rows, i4 * 6:(i4 + 1) * 6], xt[:, i4 * 512:(i4 + 1) * 512])
                ctr.inc(s_v, 1); ins.then_inc(s_v, 1); v.wait_ge(s_v, ctr.get(s_v))
                ins = v.bn_aggr(lnt[:rows, 24:26], lnt[:rows, 0:24])
                ctr.inc(s_v, 1); ins.then_inc(s_v, 1); v.wait_ge(s_v, ctr.get(s_v))
                ctr.inc(s_v, 1)
                v.tensor_scalar(lnt[:rows, 26:27], lnt[:rows, 25:26], EPS_LN, None, ALU.add).then_inc(s_v, 1)
            V1 = ctr.get(s_v)

            @blk.scalar
            def _(s, rows=rows, V1=V1):
                s.wait_ge(s_v, V1)
                ctr.inc(s_s, 1)
                s.activation(lnt[:rows, 27:28], lnt[:rows, 26:27], AF.Sqrt).then_inc(s_s, 1)
            S1 = ctr.get(s_s)

            @blk.vector
            def _(v, rows=rows, S1=S1):
                v.wait_ge(s_s, S1)
                ins = v.reciprocal(lnt[:rows, 28:29], lnt[:rows, 27:28])
                ctr.inc(s_v, 1); ins.then_inc(s_v, 1); v.wait_ge(s_v, ctr.get(s_v))
                ctr.inc(s_v, 1)
                v.scalar_tensor_tensor(out=lnt[:rows, 29:30], in0=lnt[:rows, 24:25],
                                       scalar=-1.0, in1=lnt[:rows, 28:29],
                                       op0=ALU.mult, op1=ALU.mult).then_inc(s_v, 1)
            V2 = ctr.get(s_v)

            @blk.scalar
            def _(s, tb=tb, rows=rows, xt=xt, V2=V2):
                s.wait_ge(s_v, V2)
                if tb >= 1:
                    s.wait_ge(s_mm, ctr.get(s_mm))
                ctr.inc(s_s, 1)
                s.activation(xn_t[:rows, :], xt, AF.Identity,
                             bias=lnt[:rows, 29:30], scale=lnt[:rows, 28:29]).then_inc(s_s, 1)
            S2 = ctr.get(s_s)

            @blk.tensor
            def _(t, tb=tb, rows=rows, S2=S2, psl=psl):
                t.wait_ge(s_s, S2)
                t.wait_ge(s_ld, CONSTS)
                if tb >= 1:
                    t.wait_ge(s_s, ctr.get(s_s))
                mm = None
                for db in range(NDB):
                    pp = (ps_ln0 if db < 8 else ps_ln1)
                    mm = t.matmul(pp[:, (db % 8) * 128:(db % 8) * 128 + rows],
                                  xn_t[:rows, db * 128:(db + 1) * 128],
                                  c_ident[:rows, :rows], start=True, stop=True)
                ctr.inc(s_mm, 1)
                mm.then_inc(s_mm, 1)
            M1 = ctr.get(s_mm)

            @blk.scalar
            def _(s, tb=tb, rows=rows, M1=M1):
                s.wait_ge(s_mm, M1)
                s.copy(
                    xn_f.ap().rearrange("p (db c) -> p db c", db=NDB)[:, 0:8, tb * 128:tb * 128 + rows],
                    ps_ln0.ap().rearrange("p (db c) -> p db c", db=8)[:, :, :rows],
                )
                ctr.inc(s_s, 1)
                s.copy(
                    xn_f.ap().rearrange("p (db c) -> p db c", db=NDB)[:, 8:16, tb * 128:tb * 128 + rows],
                    ps_ln1.ap().rearrange("p (db c) -> p db c", db=8)[:, :, :rows],
                ).then_inc(s_s, 1)
            LN_HIST[tb] = ctr.get(s_s)
        XNF = ctr.get(s_s)

        wf32 = E1(nc.sbuf_tensor("wf32", [128, 2 * NDB * 128], F32))
        wbf = E1(nc.sbuf_tensor("wbf", [128, 2 * NDB * 128], BF16))
        pre_sb = E1(nc.sbuf_tensor("pre_sb", [128, 2 * TP], BF16))
        ps_pa_t = E1(nc.psum_tensor("ps_pa_t", [128, 1536], F32))
        ps_pa = [ps_pa_t[:, 0:TG], ps_pa_t[:, 512:512 + TG], ps_pa_t[:, 1024:1024 + TG]]
        EV_HIST = {}

        def wsrc(fb):
            if fb < NFQ:
                return wqkv_b[fb]
            if fb < NFQ + NFG:
                return wgate_b[fb - NFQ]
            return wba_in

        for fb in range(NFB):
            buf = fb % 2
            fcols = 128 if fb < NFB - 1 else 32
            wsl = wf32[:, buf * NDB * 128:][:, :NDB * fcols]
            wbl = wbf[:, buf * NDB * 128:][:, :NDB * fcols]
            psl = ps_pa
            prl = pre_sb[:fcols, buf * TP:buf * TP + TP]

            sl = ldp[fb % 2]

            @blk.sync
            def _(q, fb=fb, wsl=wsl, fcols=fcols, sl=sl):
                if fb == 0:
                    q.wait_ge(s_lda, ctr.get(s_lda))
                    q.wait_ge(s_ldb, ctr.get(s_ldb))
                if fb >= 2:
                    q.wait_ge(s_v, ctr.get(s_v))
                q.wait_ge(sl, ctr.get(sl))
                q.dma_start(out=wsl.rearrange("p (db f) -> p db f", f=fcols),
                            in_=wsrc(fb).rearrange("(db p) f -> p db f", p=128)).then_inc(sl, 16)
                ctr.inc(sl, 16)
            WL = ctr.get(sl)

            @blk.vector
            def _(v, fb=fb, wsl=wsl, wbl=wbl, WL=WL, sl=sl):
                v.wait_ge(sl, WL)
                if fb >= 2:
                    v.wait_ge(s_mm, ctr.get(s_mm))
                ctr.inc(s_v, 1)
                v.tensor_copy(wbl, wsl).then_inc(s_v, 1)
            WC = ctr.get(s_v)

            TG_NEED = {0: LN_HIST[2], 1: LN_HIST[5], 2: LN_HIST[8]}

            @blk.tensor
            def _(t, fb=fb, wbl=wbl, psl=psl, fcols=fcols, WC=WC):
                t.wait_ge(s_v, WC)
                if fb >= 1:
                    t.wait_ge(s_s, EV_HIST[fb - 1])
                mm = None
                for tg in range(3):
                    if fb < 1:
                        t.wait_ge(s_s, TG_NEED[tg])
                    for db in range(NDB):
                        mm = t.matmul(psl[tg][:fcols, :],
                                      wbl.rearrange("p (db f) -> p db f", db=NDB)[:, db, :],
                                      xn_f.ap().rearrange("p (db c) -> p db c", db=NDB)[:, db, tg * TG:(tg + 1) * TG],
                                      start=(db == 0), stop=(db == NDB - 1))
                ctr.inc(s_mm, 1)
                mm.then_inc(s_mm, 1)
            PM = ctr.get(s_mm)

            @blk.scalar
            def _(s, fb=fb, psl=psl, prl=prl, fcols=fcols, PM=PM):
                s.wait_ge(s_mm, PM)
                if fb >= 2:
                    s.wait_ge(stp[fb % 2], ctr.get(stp[fb % 2]))
                cp = None
                for tg in range(3):
                    cp = s.copy(prl[:, tg * TG:(tg + 1) * TG], psl[tg][:fcols, :])
                ctr.inc(s_s, 1)
                cp.then_inc(s_s, 1)
            PE = ctr.get(s_s)
            EV_HIST[fb] = PE

            @blk.gpsimd
            def _(g, fb=fb, prl=prl, fcols=fcols, PE=PE):
                g.wait_ge(s_s, PE)
                g.wait_ge(stp[fb % 2], ctr.get(stp[fb % 2]))
                g.dma_start(out=pre_scr[fb, :fcols, :], in_=prl).then_inc(stp[fb % 2], 16)
                ctr.inc(stp[fb % 2], 16)
            P1_STORE_HIST[fb] = (ctr.get(s_sta), ctr.get(s_stb))


def _p15(nc, blk, ctr, sems, env):
    s_ld, s_st, s_mm, s_v, s_s = sems["ld"], sems["st"], sems["mm"], sems["v"], sems["s"]
    s_lda, s_ldb, s_sta, s_stb = sems["lda"], sems["ldb"], sems["sta"], sems["stb"]
    ldp = [s_lda, s_ldb]
    pre_scr, c_convw, c_ident, c_onec, c_onerf = (
        env["pre_scr"], env["c_convw"], env["c_ident"], env["c_onec"], env["c_onerf"])
    qk_f, k_tok, v_tok, barows, c_la, c_zero, c_cols = (
        env["qk_f"], env["k_tok"], env["v_tok"], env["barows"], env["c_la"],
        env["c_zero"], env["c_cols"])
    from contextlib import ExitStack

    with ExitStack() as st15:
        E1 = st15.enter_context
        pre_t = E1(nc.sbuf_tensor("pre_t", [128, 2 * TP], BF16))
        cv = E1(nc.sbuf_tensor("cv", [128, TPC], F32))
        sil = E1(nc.sbuf_tensor("sil", [128, TPC], F32))
        sqb = E1(nc.sbuf_tensor("sqb", [128, TPC], BF16))
        rs_row = E1(nc.sbuf_tensor("rs_row", [1, TPC], F32))
        rs_f32 = E1(nc.sbuf_tensor("rs_f32", [1, TPC], F32))
        vtile = E1(nc.sbuf_tensor("vtile", [128, TPC], BF16))
        ps_l2 = E1(nc.psum_tensor("ps_l2", [1, TPC], F32))
        ps_bc = E1(nc.psum_tensor("ps_bc", [128, TPC], F32))
        ps_tr = E1(nc.psum_tensor("ps_tr", [128, TPC], F32))

        for fb in range(NFQ):
            buf = fb % 2
            pt = pre_t[:, buf * TP:buf * TP + TP]
            is_q = fb < 8
            is_k = 8 <= fb < 16
            head = fb if is_q else (fb - 8 if is_k else fb - 16)

            sl = ldp[fb % 2]

            @blk.sync
            def _(q, fb=fb, pt=pt, sl=sl):
                if fb == 0:
                    q.wait_ge(s_lda, ctr.get(s_lda))
                    q.wait_ge(s_ldb, ctr.get(s_ldb))
                q.wait_ge(s_sta, P1_STORE_HIST[fb][0])
                q.wait_ge(s_stb, P1_STORE_HIST[fb][1])
                if fb >= 2:
                    q.wait_ge(s_v, ctr.get(s_v))
                q.wait_ge(sl, ctr.get(sl))
                q.dma_start(out=pt, in_=pre_scr[fb, :, :]).then_inc(sl, 16)
                ctr.inc(sl, 16)
            LD = ctr.get(sl)

            @blk.vector
            def _(v, fb=fb, pt=pt, LD=LD, sl=sl):
                v.wait_ge(sl, LD)
                if fb > 0:
                    v.wait_ge(s_s, ctr.get(s_s))
                ins = v.tensor_scalar(cv[:, :], pt[:, 0:TPC], c_convw[:, fb * 4:fb * 4 + 1], None, ALU.mult)
                for j in range(1, 4):
                    ctr.inc(s_v, 1); ins.then_inc(s_v, 1); v.wait_ge(s_v, ctr.get(s_v))
                    ins = v.scalar_tensor_tensor(
                        out=cv[:, :], in0=pt[:, j:j + TPC],
                        scalar=c_convw[:, fb * 4 + j:fb * 4 + j + 1], in1=cv[:, :],
                        op0=ALU.mult, op1=ALU.add)
                ctr.inc(s_v, 1)
                ins.then_inc(s_v, 1)
            CV = ctr.get(s_v)

            if is_q or is_k:
                @blk.scalar
                def _(s, CV=CV, fb=fb):
                    s.wait_ge(s_v, CV)
                    if fb >= 2:
                        s.wait_ge(s_v, ctr.get(s_v))
                    ins = s.activation(sil[:, :], cv[:, :], AF.Silu)
                    ctr.inc(s_s, 1); ins.then_inc(s_s, 1); s.wait_ge(s_s, ctr.get(s_s))
                    ctr.inc(s_s, 1)
                    s.activation(sqb[:, :], sil[:, :], AF.Square).then_inc(s_s, 1)
                SQ = ctr.get(s_s)

                @blk.tensor
                def _(t, SQ=SQ, fb=fb):
                    t.wait_ge(s_s, SQ)
                    if fb >= 1:
                        t.wait_ge(s_s, ctr.get(s_s))
                    mm = None
                    for half in range(2):
                        mm = t.matmul(ps_l2[0:1, half * 512:(half + 1) * 512], c_onec[:, :],
                                      sqb[:, half * 512:(half + 1) * 512], start=True, stop=True)
                    ctr.inc(s_mm, 1)
                    mm.then_inc(s_mm, 1)
                MMS = ctr.get(s_mm)

                @blk.scalar
                def _(s, MMS=MMS):
                    s.wait_ge(s_mm, MMS)
                    ctr.inc(s_s, 1)
                    s.activation(rs_f32[0:1, :], ps_l2[0:1, :], AF.Sqrt, bias=c_cols[0:1, 0:1]).then_inc(s_s, 1)
                RS = ctr.get(s_s)

                @blk.vector
                def _(v, RS=RS):
                    v.wait_ge(s_s, RS)
                    ctr.inc(s_v, 1)
                    v.reciprocal(rs_row[0:1, :], rs_f32[0:1, :]).then_inc(s_v, 1)
                RR = ctr.get(s_v)

                @blk.tensor
                def _(t, RR=RR, fb=fb):
                    t.wait_ge(s_v, RR)
                    if fb >= 1:
                        t.wait_ge(s_v, ctr.get(s_v))
                    mm = None
                    for half in range(2):
                        mm = t.matmul(ps_bc[:, half * 512:(half + 1) * 512],
                                      c_onerf[0:1, :],
                                      rs_row[0:1, half * 512:(half + 1) * 512], start=True, stop=True)
                    ctr.inc(s_mm, 1)
                    mm.then_inc(s_mm, 1)
                BC = ctr.get(s_mm)

                @blk.vector
                def _(v, fb=fb, head=head, is_q=is_q, BC=BC):
                    v.wait_ge(s_mm, BC)
                    dst = qk_f.ap()[:, (head + (0 if is_q else 8)) * TPC:][:, :TPC]
                    ctr.inc(s_v, 1)
                    v.tensor_tensor(dst, sil[:, :], ps_bc[:, :], ALU.mult).then_inc(s_v, 1)
                QN = ctr.get(s_v)

                if is_k:
                    @blk.tensor
                    def _(t, head=head, QN=QN, fb=fb):
                        t.wait_ge(s_v, QN)
                        if fb >= 9:
                            t.wait_ge(s_s, ctr.get(s_s))
                        src = qk_f.ap()[:, (head + 8) * TPC:][:, :TPC]
                        mm = None
                        for ci in range(NCH):
                            mm = t.matmul(ps_tr[:, ci * 128:(ci + 1) * 128],
                                          src[:, ci * 128:(ci + 1) * 128],
                                          c_ident[:, :], start=True, stop=True)
                        ctr.inc(s_mm, 1)
                        mm.then_inc(s_mm, 1)
                    KT = ctr.get(s_mm)

                    @blk.scalar
                    def _(s, head=head, KT=KT):
                        s.wait_ge(s_mm, KT)
                        ctr.inc(s_s, 1)
                        s.copy(k_tok.ap().rearrange("p (ci q s) -> p ci q s", q=HQ, s=128)[:, :, head, :],
                               ps_tr.ap().rearrange("p (ci s) -> p ci s", s=128)).then_inc(s_s, 1)
            else:
                @blk.scalar
                def _(s, CV=CV, fb=fb):
                    s.wait_ge(s_v, CV)
                    if fb >= 17:
                        s.wait_ge(s_mm, ctr.get(s_mm))
                    ctr.inc(s_s, 1)
                    s.activation(vtile[:, :], cv[:, :], AF.Silu).then_inc(s_s, 1)
                VS = ctr.get(s_s)

                @blk.tensor
                def _(t, head=head, VS=VS, fb=fb):
                    t.wait_ge(s_s, VS)
                    if fb >= 17:
                        t.wait_ge(s_s, ctr.get(s_s))
                    mm = None
                    for ci in range(NCH):
                        mm = t.matmul(ps_tr[:, ci * 128:(ci + 1) * 128],
                                      vtile[:, ci * 128:(ci + 1) * 128],
                                      c_ident[:, :], start=True, stop=True)
                    ctr.inc(s_mm, 1)
                    mm.then_inc(s_mm, 1)
                VT = ctr.get(s_mm)

                @blk.scalar
                def _(s, head=head, VT=VT):
                    s.wait_ge(s_mm, VT)
                    ctr.inc(s_s, 1)
                    s.copy(v_tok.ap().rearrange("p (ci hh s) -> p ci hh s", hh=HV, s=128)[:, :, head, :],
                           ps_tr.ap().rearrange("p (ci s) -> p ci s", s=128)).then_inc(s_s, 1)

        ba_t = E1(nc.sbuf_tensor("ba_t", [16, TPC], BF16))
        aa_t = E1(nc.sbuf_tensor("aa_t", [16, TPC], BF16))
        tmp16 = E1(nc.sbuf_tensor("tmp16", [HV, TPC], F32))

        @blk.sync
        def _(q):
            q.wait_ge(s_lda, ctr.get(s_lda))
            q.wait_ge(s_ldb, ctr.get(s_ldb))
            q.wait_ge(s_sta, P1_STORE_HIST[NFB - 1][0])
            q.wait_ge(s_stb, P1_STORE_HIST[NFB - 1][1])
            q.dma_start(out=ba_t[:, :], in_=pre_scr[NFB - 1, 0:16, HALO:HALO + TPC]).then_inc(s_lda, 16)
            q.dma_start(out=aa_t[:, :], in_=pre_scr[NFB - 1, 16:32, HALO:HALO + TPC]).then_inc(s_lda, 32 - 16)
            ctr.inc(s_lda, 32)
        BA = ctr.get(s_lda)

        @blk.scalar
        def _(s, BA=BA):
            s.wait_ge(s_lda, BA)
            s.wait_ge(s_v, ctr.get(s_v))
            ins = s.activation(barows[:, 0:TPC], ba_t[:, :], AF.Sigmoid)
            ctr.inc(s_s, 1); ins.then_inc(s_s, 1); s.wait_ge(s_s, ctr.get(s_s))
            s.activation(barows[:, TPC:2 * TPC], barows[:, 0:TPC], AF.Ln)
            ins = s.activation(tmp16[:, :], aa_t[:, :], AF.Sigmoid, scale=-1.0, bias=c_la[:, 5:6])
            ctr.inc(s_s, 1); ins.then_inc(s_s, 1); s.wait_ge(s_s, ctr.get(s_s))
            ctr.inc(s_s, 1)
            s.activation(tmp16[:, :], tmp16[:, :], AF.Ln).then_inc(s_s, 1)
        SB = ctr.get(s_s)

        @blk.vector
        def _(v, SB=SB):
            v.wait_ge(s_s, SB)
            ins = v.tensor_scalar(barows[:, 2 * TPC:3 * TPC], tmp16[:, :], c_la[:, 4:5], None, ALU.mult)
            ctr.inc(s_v, 1); ins.then_inc(s_v, 1); v.wait_ge(s_v, ctr.get(s_v))
            ins = None
            for ci in range(NCH):
                ins = v.tensor_tensor_scan(
                    out=barows[:, 3 * TPC + ci * 128:3 * TPC + (ci + 1) * 128],
                    data0=barows[:, 2 * TPC + ci * 128:2 * TPC + (ci + 1) * 128],
                    data1=barows[:, 2 * TPC + ci * 128:2 * TPC + (ci + 1) * 128],
                    initial=c_zero[0:HV, 0:1],
                    op0=ALU.add, op1=ALU.bypass)
            ctr.inc(s_v, 1)
            ins.then_inc(s_v, 1)


def _p2a(nc, blk, ctr, sems, env):
    s_mm, s_v, s_s, s_g = sems["mm"], sems["v"], sems["s"], sems["g"]
    s_sta, s_stb = sems["sta"], sems["stb"]
    stp = [s_sta, s_stb]
    qk_f, k_tok, v_tok, barows = env["qk_f"], env["k_tok"], env["v_tok"], env["barows"]
    compP, compR = env["compP"], env["compR"]
    c_ident, c_identf, c_onerf = env["c_ident"], env["c_identf"], env["c_onerf"]
    c_maskLT, c_maskUT = env["c_maskLT"], env["c_maskUT"]
    spill = env["spill"]
    from contextlib import ExitStack

    W = HV * 128
    NQ4 = 4
    W2 = W // NQ4
    HH = HV // NQ4
    QH = HQ // NQ4

    with ExitStack() as sta_:
        E1 = sta_.enter_context
        mrow_all = E1(nc.sbuf_tensor("mrow_all", [HV, 128], F32))
        cols = E1(nc.sbuf_tensor("colsA", [128, 64], F32))
        rowstall = E1(nc.sbuf_tensor("rowstall", [1, 2 * W], F32))
        e_t = E1(nc.sbuf_tensor("e_t_all", [128, W], F32))
        e_tt = E1(nc.sbuf_tensor("e_tt_all", [128, W], F32))
        X = E1(nc.sbuf_tensor("X_all", [128, W], BF16))
        Xt = E1(nc.sbuf_tensor("Xt_all", [128, W], BF16))
        NT = E1(nc.sbuf_tensor("NT_all", [128, W], BF16))
        Hb = E1(nc.sbuf_tensor("Hb_all", [128, W], BF16))
        G_sb = E1(nc.sbuf_tensor("G_all", [128, HQ * 128], BF16))
        bVt = E1(nc.sbuf_tensor("bVt_all", [128, W], BF16))
        bAKt = E1(nc.sbuf_tensor("bAKt_all", [128, W], BF16))
        kht = E1(nc.sbuf_tensor("kht_all", [128, W], BF16))
        U0 = E1(nc.sbuf_tensor("U0_all", [128, W], BF16))
        Yc = E1(nc.sbuf_tensor("Yc_all", [128, W], BF16))
        Rsb = E1(nc.sbuf_tensor("Rsb_all", [128, W], BF16))
        PT = E1(nc.sbuf_tensor("PT_all", [128, W], BF16))
        acb = E1(nc.sbuf_tensor("acbA", [128, 48], F32))
        psA = [E1(nc.psum_tensor(f"ps_A{q}", [128, W2], F32)) for q in range(NQ4)]
        psB = [E1(nc.psum_tensor(f"ps_B{q}", [128, W2], F32)) for q in range(NQ4)]

        ROWS_V = ctr.get(s_v)

        def colv(ap2):
            return ap2.unsqueeze(2).broadcast_to((128, HH, 128))

        def mv(ap2):
            return ap2.unsqueeze(1).broadcast_to((128, HH, 128))

        def gv(ap3):
            return ap3.unsqueeze(2).broadcast_to((128, QH, 2, 128))

        def h3(ap):
            return ap.rearrange("p (h s) -> p h s", s=128)

        for ci in range(NCH):
            ell_r = barows[:, 3 * TPC + ci * 128:3 * TPC + (ci + 1) * 128]
            lb_r = barows[:, TPC + ci * 128:TPC + (ci + 1) * 128]
            beta_r = barows[:, ci * 128:(ci + 1) * 128]

            @blk.vector
            def _(v, ell_r=ell_r, lb_r=lb_r, ci=ci):
                v.wait_ge(s_v, ROWS_V)
                if ci > 0:
                    v.wait_ge(s_mm, ctr.get(s_mm))
                ctr.inc(s_v, 1)
                v.tensor_tensor(mrow_all[:, :], ell_r, lb_r, ALU.add).then_inc(s_v, 1)
            MR = ctr.get(s_v)

            @blk.tensor
            def _(t, ell_r=ell_r, beta_r=beta_r, MR=MR, ci=ci):
                t.wait_ge(s_v, MR)
                if ci > 0:
                    t.wait_ge(s_s, ctr.get(s_s))
                    t.wait_ge(s_v, ctr.get(s_v))
                t.matmul(psA[0][:, 0:16], ell_r, c_identf[0:HV, 0:HV], start=True, stop=True)
                t.matmul(psA[0][:, 16:32], mrow_all[:, :], c_identf[0:HV, 0:HV], start=True, stop=True)
                ctr.inc(s_mm, 1)
                t.matmul(psA[0][:, 32:48], beta_r, c_identf[0:HV, 0:HV], start=True, stop=True).then_inc(s_mm, 1)
            CT = ctr.get(s_mm)

            @blk.scalar
            def _(s, CT=CT):
                s.wait_ge(s_mm, CT)
                ctr.inc(s_s, 1)
                s.copy(cols[:, 0:48], psA[0][:, 0:48]).then_inc(s_s, 1)
            COLS = ctr.get(s_s)

            st = [dict() for _ in range(NQ4)]

            def S_rows(q2):
                d = st[q2]
                @blk.tensor
                def _(t, q2=q2):
                    t.wait_ge(s_s, COLS)
                    mm = None
                    for i in range(HH):
                        h = q2 * HH + i
                        mm = t.matmul(psA[q2][0:1, i * 128:(i + 1) * 128], cols[:, h:h + 1],
                                      c_identf[:, :], start=True, stop=True)
                        mm = t.matmul(psB[q2][0:1, i * 128:(i + 1) * 128], cols[:, 16 + h:16 + h + 1],
                                      c_identf[:, :], start=True, stop=True)
                    ctr.inc(s_mm, 1)
                    mm.then_inc(s_mm, 1)
                d["RWM"] = ctr.get(s_mm)

            def S_rowse(q2):
                d = st[q2]
                @blk.scalar
                def _(s, q2=q2):
                    s.wait_ge(s_mm, d["RWM"])
                    s.copy(rowstall[0:1, q2 * W2:(q2 + 1) * W2], psA[q2][0:1, 0:W2])
                    ctr.inc(s_s, 1)
                    s.copy(rowstall[0:1, W + q2 * W2:W + (q2 + 1) * W2], psB[q2][0:1, 0:W2]).then_inc(s_s, 1)
                d["RWS"] = ctr.get(s_s)

            def S_bcast(q2):
                d = st[q2]
                @blk.tensor
                def _(t, q2=q2):
                    t.wait_ge(s_s, d["RWS"])
                    mm = None
                    for qq in range(W2 // 512):
                        sl = slice(qq * 512, (qq + 1) * 512)
                        slr = slice(q2 * W2 + qq * 512, q2 * W2 + (qq + 1) * 512)
                        t.matmul(psA[q2][:, sl], c_onerf[0:1, :], rowstall[0:1, slr], start=True, stop=True)
                        mm = t.matmul(psB[q2][:, sl], c_onerf[0:1, :],
                                      rowstall[0:1, W + q2 * W2 + qq * 512:W + q2 * W2 + (qq + 1) * 512],
                                      start=True, stop=True)
                    ctr.inc(s_mm, 1)
                    mm.then_inc(s_mm, 1)
                d["BC"] = ctr.get(s_mm)

            def S_ea(q2):
                d = st[q2]
                sl2 = slice(q2 * W2, (q2 + 1) * W2)
                cl = slice(q2 * HH, q2 * HH + HH)
                @blk.vector
                def _(v, q2=q2, sl2=sl2, cl=cl):
                    v.wait_ge(s_mm, d["BC"])
                    ins = v.tensor_tensor(h3(e_t.ap()[:, sl2]), colv(cols[:, 16 + q2 * HH:16 + q2 * HH + HH]),
                                          h3(psA[q2].ap()[:, 0:W2]), ALU.subtract)
                    ins = v.tensor_tensor(h3(e_tt.ap()[:, sl2]), h3(psB[q2].ap()[:, 0:W2]),
                                          colv(cols[:, q2 * HH:q2 * HH + HH]), ALU.subtract)
                    ins = v.tensor_tensor(acb[:, 16 + q2 * HH:16 + q2 * HH + HH], psA[q2].ap()[:, 127::128],
                                          cols[:, q2 * HH:q2 * HH + HH], ALU.subtract)
                    ins = v.tensor_copy(acb[:, q2 * HH:q2 * HH + HH], psA[q2].ap()[:, 127::128])
                    ctr.inc(s_v, 1); ins.then_inc(s_v, 1); v.wait_ge(s_v, ctr.get(s_v))
                    v.tensor_tensor(h3(e_t.ap()[:, sl2]), h3(e_t.ap()[:, sl2]), mv(c_maskLT.ap()), ALU.add)
                    ctr.inc(s_v, 1)
                    v.tensor_tensor(h3(e_tt.ap()[:, sl2]), h3(e_tt.ap()[:, sl2]), mv(c_maskUT.ap()), ALU.add).then_inc(s_v, 1)
                d["EA"] = ctr.get(s_v)

            def S_exp(q2):
                d = st[q2]
                sl2 = slice(q2 * W2, (q2 + 1) * W2)
                @blk.scalar
                def _(s, q2=q2, sl2=sl2):
                    s.wait_ge(s_v, d["EA"])
                    s.activation(e_t[:, sl2], e_t[:, sl2], AF.Exp)
                    s.activation(e_tt[:, sl2], e_tt[:, sl2], AF.Exp)
                    s.activation(acb[:, 32 + q2 * HH:32 + q2 * HH + HH], cols[:, 16 + q2 * HH:16 + q2 * HH + HH], AF.Exp)
                    ins = s.activation(acb[:, 16 + q2 * HH:16 + q2 * HH + HH], acb[:, 16 + q2 * HH:16 + q2 * HH + HH], AF.Exp)
                    ctr.inc(s_s, 1); ins.then_inc(s_s, 1); s.wait_ge(s_s, ctr.get(s_s))
                    ctr.inc(s_s, 1)
                    s.activation(acb[:, q2 * HH:q2 * HH + HH], acb[:, q2 * HH:q2 * HH + HH], AF.Exp).then_inc(s_s, 1)
                d["EX"] = ctr.get(s_s)

            def S_g(q2):
                d = st[q2]
                @blk.tensor
                def _(t, q2=q2, ci=ci):
                    t.wait_ge(s_s, d["EX"])
                    mm = None
                    for i in range(QH):
                        hq = q2 * QH + i
                        kf = qk_f.ap()[:, (8 + hq) * TPC + ci * 128:][:, :128]
                        mm = t.matmul(psA[q2][:, i * 128:(i + 1) * 128], kf, kf, start=True, stop=True)
                    ctr.inc(s_mm, 1)
                    mm.then_inc(s_mm, 1)
                d["GM"] = ctr.get(s_mm)

            def S_ge(q2):
                d = st[q2]
                @blk.scalar
                def _(s, q2=q2):
                    s.wait_ge(s_mm, d["GM"])
                    ctr.inc(s_s, 1)
                    s.copy(G_sb[:, q2 * QH * 128:(q2 + 1) * QH * 128], psA[q2][:, 0:QH * 128]).then_inc(s_s, 1)
                d["GE"] = ctr.get(s_s)

            def S_tx(q2):
                d = st[q2]
                sl2 = slice(q2 * W2, (q2 + 1) * W2)
                gsl = G_sb.ap()[:, q2 * QH * 128:(q2 + 1) * QH * 128].rearrange("p (q s) -> p q s", s=128)
                @blk.vector
                def _(v, q2=q2, sl2=sl2, gsl=gsl):
                    v.wait_ge(s_s, d["GE"])
                    r4 = lambda ap: h3(ap).rearrange("p (q r) s -> p q r s", r=2)
                    ins = v.tensor_tensor(r4(e_t.ap()[:, sl2]), gv(gsl), r4(e_t.ap()[:, sl2]), ALU.mult)
                    ins = v.tensor_tensor(r4(e_tt.ap()[:, sl2]), gv(gsl), r4(e_tt.ap()[:, sl2]), ALU.mult)
                    ctr.inc(s_v, 1); ins.then_inc(s_v, 1); v.wait_ge(s_v, ctr.get(s_v))
                    v.tensor_tensor(h3(X.ap()[:, sl2]), mv(c_identf.ap()), h3(e_t.ap()[:, sl2]), ALU.subtract)
                    v.tensor_tensor(h3(Xt.ap()[:, sl2]), mv(c_identf.ap()), h3(e_tt.ap()[:, sl2]), ALU.subtract)
                    ctr.inc(s_v, 1)
                    v.tensor_tensor(h3(NT.ap()[:, sl2]), mv(c_identf.ap()), h3(e_tt.ap()[:, sl2]), ALU.add).then_inc(s_v, 1)
                d["TX"] = ctr.get(s_v)
                d["last_v"] = d["TX"]
                d["last_s"] = ctr.get(s_s)

            def S_newt_h(q2):
                d = st[q2]
                @blk.tensor
                def _(t, q2=q2):
                    t.wait_ge(s_v, d["last_v"])
                    t.wait_ge(s_s, d["last_s"])
                    mm = None
                    for i in range(HH):
                        o = q2 * W2 + i * 128
                        mm = t.matmul(psA[q2][:, i * 128:(i + 1) * 128], NT[:, o:o + 128],
                                      X[:, o:o + 128], start=True, stop=True)
                    ctr.inc(s_mm, 1)
                    mm.then_inc(s_mm, 1)
                d["HM"] = ctr.get(s_mm)

            def S_newt_he(q2):
                d = st[q2]
                @blk.scalar
                def _(s, q2=q2):
                    s.wait_ge(s_mm, d["HM"])
                    ctr.inc(s_s, 1)
                    s.copy(Hb[:, q2 * W2:(q2 + 1) * W2], psA[q2][:, :]).then_inc(s_s, 1)
                d["HE"] = ctr.get(s_s)

            def S_newt_xh(q2):
                d = st[q2]
                @blk.tensor
                def _(t, q2=q2):
                    t.wait_ge(s_s, d["HE"])
                    mm = None
                    for i in range(HH):
                        o = q2 * W2 + i * 128
                        isl = slice(i * 128, (i + 1) * 128)
                        t.matmul(psB[q2][:, isl], Xt[:, o:o + 128], Hb[:, o:o + 128], start=True, stop=True)
                        mm = t.matmul(psA[q2][:, isl], Hb[:, o:o + 128], Xt[:, o:o + 128], start=True, stop=True)
                    ctr.inc(s_mm, 1)
                    mm.then_inc(s_mm, 1)
                d["XHM"] = ctr.get(s_mm)

            def S_newt_xn(q2):
                d = st[q2]
                @blk.vector
                def _(v, q2=q2):
                    v.wait_ge(s_mm, d["XHM"])
                    sl2 = slice(q2 * W2, (q2 + 1) * W2)
                    v.scalar_tensor_tensor(out=X[:, sl2], in0=X[:, sl2], scalar=2.0,
                                           in1=psB[q2][:, :], op0=ALU.mult, op1=ALU.subtract)
                    ctr.inc(s_v, 1)
                    v.scalar_tensor_tensor(out=Xt[:, sl2], in0=Xt[:, sl2], scalar=2.0,
                                           in1=psA[q2][:, :], op0=ALU.mult, op1=ALU.subtract).then_inc(s_v, 1)
                d["last_v"] = ctr.get(s_v)
                d["last_s"] = ctr.get(s_s)

            def S_ops(q2):
                d = st[q2]
                sl2 = slice(q2 * W2, (q2 + 1) * W2)
                ktok_chunk = k_tok.ap()[:, ci * HQ * 128 + q2 * QH * 128:][:, :QH * 128].rearrange("p (q s) -> p q s", s=128)
                vtok_chunk = v_tok.ap()[:, ci * W + q2 * W2:][:, :W2]
                @blk.vector
                def _(v, q2=q2, sl2=sl2, ktok_chunk=ktok_chunk, vtok_chunk=vtok_chunk):
                    v.wait_ge(s_s, d["last_s"])
                    r4 = lambda ap: h3(ap).rearrange("p (q r) s -> p q r s", r=2)
                    cv4 = lambda ap2: colv(ap2).rearrange("p (q r) s -> p q r s", r=2)
                    v.tensor_tensor(h3(bVt.ap()[:, sl2]), h3(vtok_chunk), colv(cols[:, 32 + q2 * HH:32 + q2 * HH + HH]), ALU.mult)
                    v.tensor_tensor(r4(bAKt.ap()[:, sl2]), gv(ktok_chunk), cv4(acb[:, 32 + q2 * HH:32 + q2 * HH + HH]), ALU.mult)
                    ctr.inc(s_v, 1)
                    v.tensor_tensor(r4(kht.ap()[:, sl2]), gv(ktok_chunk), cv4(acb[:, 16 + q2 * HH:16 + q2 * HH + HH]), ALU.mult).then_inc(s_v, 1)
                d["OPS"] = ctr.get(s_v)

            def S_sol(q2):
                d = st[q2]
                @blk.tensor
                def _(t, q2=q2):
                    t.wait_ge(s_v, d["OPS"])
                    mm = None
                    for i in range(HH):
                        o = q2 * W2 + i * 128
                        isl = slice(i * 128, (i + 1) * 128)
                        t.matmul(psA[q2][:, isl], Xt[:, o:o + 128], bVt[:, o:o + 128], start=True, stop=True)
                        mm = t.matmul(psB[q2][:, isl], Xt[:, o:o + 128], bAKt[:, o:o + 128], start=True, stop=True)
                    ctr.inc(s_mm, 1)
                    mm.then_inc(s_mm, 1)
                d["SOL"] = ctr.get(s_mm)

            def S_uy(q2):
                d = st[q2]
                @blk.scalar
                def _(s, q2=q2):
                    s.wait_ge(s_mm, d["SOL"])
                    s.wait_ge(s_sta, ctr.get(s_sta))
                    s.wait_ge(s_stb, ctr.get(s_stb))
                    s.copy(U0[:, q2 * W2:(q2 + 1) * W2], psA[q2][:, :])
                    ctr.inc(s_s, 1)
                    s.copy(Yc[:, q2 * W2:(q2 + 1) * W2], psB[q2][:, :]).then_inc(s_s, 1)
                d["UY"] = ctr.get(s_s)

            def S_spl1(q2):
                d = st[q2]
                @blk.gpsimd
                def _(g, q2=q2, ci=ci):
                    g.wait_ge(s_s, d["UY"])
                    g.wait_ge(stp[q2 % 2], ctr.get(stp[q2 % 2]))
                    g.dma_start(out=spill[ci, 0, :, q2 * W2:(q2 + 1) * W2], in_=U0[:, q2 * W2:(q2 + 1) * W2]).then_inc(stp[q2 % 2], 16)
                    g.dma_start(out=spill[ci, 1, :, q2 * W2:(q2 + 1) * W2], in_=Yc[:, q2 * W2:(q2 + 1) * W2]).then_inc(stp[q2 % 2], 16)
                    ctr.inc(stp[q2 % 2], 32)

            def S_rp(q2):
                d = st[q2]
                @blk.tensor
                def _(t, q2=q2):
                    t.wait_ge(s_s, d["UY"])
                    mm = None
                    for i in range(HH):
                        o = q2 * W2 + i * 128
                        isl = slice(i * 128, (i + 1) * 128)
                        t.matmul(psA[q2][:, isl], kht[:, o:o + 128], U0[:, o:o + 128], start=True, stop=True)
                        mm = t.matmul(psB[q2][:, isl], Yc[:, o:o + 128], kht[:, o:o + 128], start=True, stop=True)
                    ctr.inc(s_mm, 1)
                    mm.then_inc(s_mm, 1)
                d["RP"] = ctr.get(s_mm)

            def S_prv(q2):
                d = st[q2]
                sl2 = slice(q2 * W2, (q2 + 1) * W2)
                @blk.vector
                def _(v, q2=q2, sl2=sl2):
                    v.wait_ge(s_mm, d["RP"])
                    v.wait_ge(s_sta, ctr.get(s_sta))
                    v.wait_ge(s_stb, ctr.get(s_stb))
                    v.tensor_copy(Rsb[:, sl2], psA[q2][:, :])
                    ins = v.tensor_tensor(h3(e_t.ap()[:, sl2]), mv(c_identf.ap()),
                                          colv(acb[:, q2 * HH:q2 * HH + HH]), ALU.mult)
                    ctr.inc(s_v, 1); ins.then_inc(s_v, 1); v.wait_ge(s_v, ctr.get(s_v))
                    ctr.inc(s_v, 1)
                    v.tensor_tensor(PT[:, sl2], e_t[:, sl2], psB[q2][:, :], ALU.subtract).then_inc(s_v, 1)
                d["PRV"] = ctr.get(s_v)

            def S_spl2(q2):
                d = st[q2]
                @blk.gpsimd
                def _(g, q2=q2, ci=ci):
                    g.wait_ge(s_v, d["PRV"])
                    g.wait_ge(stp[q2 % 2], ctr.get(stp[q2 % 2]))
                    g.dma_start(out=spill[ci, 2, :, q2 * W2:(q2 + 1) * W2], in_=PT[:, q2 * W2:(q2 + 1) * W2]).then_inc(stp[q2 % 2], 16)
                    g.dma_start(out=spill[ci, 3, :, q2 * W2:(q2 + 1) * W2], in_=Rsb[:, q2 * W2:(q2 + 1) * W2]).then_inc(stp[q2 % 2], 16)
                    ctr.inc(stp[q2 % 2], 32)

            def S_cm(q2):
                d = st[q2]
                sl2 = slice(q2 * W2, (q2 + 1) * W2)
                if ci == 0:
                    @blk.vector
                    def _(v, q2=q2, sl2=sl2):
                        v.wait_ge(s_v, d["PRV"])
                        v.tensor_copy(compP.ap()[:, sl2], PT[:, sl2])
                        ctr.inc(s_v, 1)
                        v.tensor_copy(compR.ap()[:, sl2], Rsb[:, sl2]).then_inc(s_v, 1)
                else:
                    @blk.tensor
                    def _(t, q2=q2):
                        t.wait_ge(s_v, d["PRV"])
                        mm = None
                        for i in range(HH):
                            o = q2 * W2 + i * 128
                            isl = slice(i * 128, (i + 1) * 128)
                            t.matmul(psA[q2][:, isl], PT[:, o:o + 128], compP.ap()[:, o:o + 128], start=True, stop=True)
                            mm = t.matmul(psB[q2][:, isl], PT[:, o:o + 128], compR.ap()[:, o:o + 128], start=True, stop=True)
                        ctr.inc(s_mm, 1)
                        mm.then_inc(s_mm, 1)
                    d["CM"] = ctr.get(s_mm)

                    @blk.vector
                    def _(v, q2=q2, sl2=sl2):
                        v.wait_ge(s_mm, d["CM"])
                        v.tensor_copy(compP.ap()[:, sl2], psA[q2][:, :])
                        ctr.inc(s_v, 1)
                        v.tensor_tensor(compR.ap()[:, sl2], psB[q2][:, :], Rsb[:, sl2], ALU.add).then_inc(s_v, 1)

            stages = [S_rows, S_rowse, S_bcast, S_ea, S_exp, S_g, S_ge, S_tx]
            for _it in range(NEWTON_ITERS):
                stages += [S_newt_h, S_newt_he, S_newt_xh, S_newt_xn]
            stages += [S_ops, S_sol, S_uy, S_spl1, S_rp, S_prv, S_spl2, S_cm]
            for stg in stages:
                for qq4 in range(NQ4):
                    stg(qq4)


def _cc_hop(nc, blk, ctr, sems, env):
    s_ld, s_st, s_mm, s_v, s_s, s_g, s_cc = (
        sems["ld"], sems["st"], sems["mm"], sems["v"], sems["s"], sems["g"], sems["cc"])
    s_lda, s_ldb = sems["lda"], sems["ldb"]
    ldp = [s_lda, s_ldb]
    compP, compR, state = env["compP"], env["compR"], env["state"]
    c_ident, c_hopm, c_hopimi = env["c_ident"], env["c_hopm"], env["c_hopimi"]
    cc_in, cc_out = env["cc_in"], env["cc_out"]
    from contextlib import ExitStack

    _phase_barrier(nc, blk, ctr, sems)
    with ExitStack() as stc:
        E1 = stc.enter_context
        cPt = E1(nc.sbuf_tensor("cPt", [128, HV * 128], BF16))
        hopbuf = E1(nc.sbuf_tensor("hopbuf", [128, 2 * HV * 2 * 128], BF16))
        Pm = E1(nc.sbuf_tensor("Pm", [128, HV * 128], BF16))
        ps_hopA = E1(nc.psum_tensor("ps_hopA", [128, HV * 128], F32))
        CPD = ctr.get(s_v)

        @blk.tensor
        def _(t, CPD=CPD):
            t.wait_ge(s_v, CPD)
            mm = None
            for h in range(HV):
                mm = t.matmul(ps_hopA[:, h * 128:(h + 1) * 128], compP.ap()[:, h * 128:(h + 1) * 128],
                              c_ident[:, :], start=True, stop=True)
            ctr.inc(s_mm, 1)
            mm.then_inc(s_mm, 1)
        TM = ctr.get(s_mm)

        @blk.scalar
        def _(s, TM=TM):
            s.wait_ge(s_mm, TM)
            ctr.inc(s_s, 1)
            s.copy(cPt[:, :], ps_hopA[:, :]).then_inc(s_s, 1)
        CPT = ctr.get(s_s)

        @blk.sync
        def _(q, CPT=CPT):
            q.wait_ge(s_s, CPT)
            q.wait_ge(s_lda, ctr.get(s_lda))
            q.wait_ge(s_ldb, ctr.get(s_ldb))
            q.dma_start(out=cc_in[:, 0:HV * 128], in_=cPt[:, :]).then_inc(s_lda, 16)
            q.dma_start(out=cc_in[:, HV * 128:], in_=compR[:, :]).then_inc(s_lda, 16)
            ctr.inc(s_lda, 32)
        CCI = ctr.get(s_lda)

        @blk.gpsimd
        def _(g, CCI=CCI):
            g.wait_ge(s_lda, CCI)
            ctr.inc(s_cc, 1)
            g.collective_compute(
                "AllGather", ALU.bypass,
                ins=[cc_in[:, :]], outs=[cc_out[:, :]],
                replica_groups=[list(range(8))],
            ).then_inc(s_cc)
            ctr.inc(s_g, 1)
            g.memset(state[:, :], 0.0).then_inc(s_g, 1)
        STZ = ctr.get(s_g)

        W = HV * 128
        for j in range(7):
            sl = ldp[j % 2]
            hb = hopbuf[:, (j % 2) * HV * 256:(j % 2 + 1) * HV * 256]

            @blk.sync
            def _(q, j=j, sl=sl, hb=hb):
                q.wait_ge(s_cc, ctr.get(s_cc))
                if j >= 2:
                    q.wait_ge(s_v, ctr.get(s_v))
                q.wait_ge(sl, ctr.get(sl))
                q.dma_start(out=hb, in_=cc_out[j * 128:(j + 1) * 128, :]).then_inc(sl, 16)
                ctr.inc(sl, 16)
            HL = ctr.get(sl)

            @blk.vector
            def _(v, j=j, sl=sl, hb=hb, HL=HL, STZ=STZ):
                v.wait_ge(sl, HL)
                v.wait_ge(s_g, STZ)
                if j > 0:
                    v.wait_ge(s_mm, ctr.get(s_mm))
                ins = v.tensor_scalar(Pm[:, :], hb[:, 0:W], c_hopm[:, j:j + 1], None, ALU.mult)
                ctr.inc(s_v, 1); ins.then_inc(s_v, 1); v.wait_ge(s_v, ctr.get(s_v))
                ctr.inc(s_v, 1)
                v.tensor_tensor(Pm.ap().rearrange("p (h f) -> p h f", f=128),
                                Pm.ap().rearrange("p (h f) -> p h f", f=128),
                                c_hopimi.ap().rearrange("p (jj f) -> p jj f", f=128)[:, j:j + 1, :].broadcast_to((128, HV, 128)),
                                ALU.add).then_inc(s_v, 1)
            PMV = ctr.get(s_v)

            @blk.tensor
            def _(t, PMV=PMV):
                t.wait_ge(s_v, PMV)
                t.wait_ge(s_s, ctr.get(s_s))
                mm = None
                for h in range(HV):
                    hsl = slice(h * 128, (h + 1) * 128)
                    mm = t.matmul(ps_hopA[:, hsl], Pm[:, hsl],
                                  state.ap()[:, hsl], start=True, stop=True)
                ctr.inc(s_mm, 1)
                mm.then_inc(s_mm, 1)
            SMM = ctr.get(s_mm)

            @blk.vector
            def _(v, j=j, hb=hb, SMM=SMM):
                v.wait_ge(s_mm, SMM)
                ins = v.tensor_scalar(state.ap()[:, :], hb[:, W:2 * W], c_hopm[:, j:j + 1], None, ALU.mult)
                ctr.inc(s_v, 1); ins.then_inc(s_v, 1); v.wait_ge(s_v, ctr.get(s_v))
                ctr.inc(s_v, 1)
                v.tensor_tensor(state.ap()[:, :], state.ap()[:, :], ps_hopA[:, :], ALU.add).then_inc(s_v, 1)
def _p2b(nc, blk, ctr, sems, env):
    s_mm, s_v, s_s, s_g = sems["mm"], sems["v"], sems["s"], sems["g"]
    s_lda, s_ldb, s_sta, s_stb = sems["lda"], sems["ldb"], sems["sta"], sems["stb"]
    ldp = [s_lda, s_ldb]
    stp = [s_sta, s_stb]
    qk_f, barows, state = env["qk_f"], env["barows"], env["state"]
    c_ident, c_identf, c_maskUD, c_rmsw, c_onerf = (
        env["c_ident"], env["c_identf"], env["c_maskUD"], env["c_rmsw"], env["c_onerf"])
    c_cols = env["c_cols"]
    spill, pre_scr, p_scr = env["spill"], env["pre_scr"], env["p_scr"]
    from contextlib import ExitStack

    W = HV * 128
    W2 = W // 2
    HH = HV // 2
    QH = HQ // 2

    _phase_barrier(nc, blk, ctr, sems)
    with ExitStack() as stb_:
        E1 = stb_.enter_context
        cols = E1(nc.sbuf_tensor("colsB", [128, 16], F32))
        rowstall = E1(nc.sbuf_tensor("rowstallB", [1, W], F32))
        e_m = E1(nc.sbuf_tensor("e_mB", [128, W], F32))
        qa = E1(nc.sbuf_tensor("qaB", [128, W], BF16))
        MT = E1(nc.sbuf_tensor("MTB", [128, W], BF16))
        qAf = E1(nc.sbuf_tensor("qAfB", [128, W], BF16))
        nMYT = E1(nc.sbuf_tensor("nMYTB", [128, W], BF16))
        U0 = E1(nc.sbuf_tensor("U0B", [128, W], BF16))
        Yc = E1(nc.sbuf_tensor("YcB", [128, W], BF16))
        PTl = E1(nc.sbuf_tensor("PTlB", [128, W], BF16))
        Rl = E1(nc.sbuf_tensor("RlB", [128, W], BF16))
        gt = E1(nc.sbuf_tensor("gtB", [128, 2 * W], BF16))
        Ofp = E1(nc.sbuf_tensor("OfpB", [128, W], F32))
        sqf = E1(nc.sbuf_tensor("sqfB", [128, W], F32))
        Onsb = E1(nc.sbuf_tensor("OnsbB", [128, W], BF16))
        gsil = E1(nc.sbuf_tensor("gsilB", [128, W], BF16))
        ptile = E1(nc.sbuf_tensor("ptileB", [128, 2 * W], BF16))
        rcol = E1(nc.sbuf_tensor("rcolB", [128, 16], F32))
        psA = [E1(nc.psum_tensor("ps_AB0", [128, W2], F32)), E1(nc.psum_tensor("ps_AB1", [128, W2], F32))]
        psB = [E1(nc.psum_tensor("ps_BB0", [128, W2], F32)), E1(nc.psum_tensor("ps_BB1", [128, W2], F32))]

        def colv(ap2):
            return ap2.unsqueeze(2).broadcast_to((128, HH, 128))

        def mv(ap2):
            return ap2.unsqueeze(1).broadcast_to((128, HH, 128))

        def gv(ap3):
            return ap3.unsqueeze(2).broadcast_to((128, QH, 2, 128))

        def h3(ap):
            return ap.rearrange("p (h s) -> p h s", s=128)

        for ci in range(NCH):
            buf = ci % 2
            ell_r = barows[:, 3 * TPC + ci * 128:3 * TPC + (ci + 1) * 128]
            gtl = gt[:, buf * W:buf * W + W]
            pt = ptile[:, buf * W:buf * W + W]

            @blk.sync
            def _(q, ci=ci, buf=buf, gtl=gtl):
                if ci < 2:
                    q.wait_ge(s_sta, ctr.get(s_sta))
                    q.wait_ge(s_stb, ctr.get(s_stb))
                q.wait_ge(s_mm, ctr.get(s_mm))
                q.wait_ge(s_s, ctr.get(s_s))
                q.wait_ge(s_v, ctr.get(s_v))
                sl = ldp[buf]
                q.wait_ge(sl, ctr.get(sl))
                q.dma_start(out=U0[:, :], in_=spill[ci, 0]).then_inc(sl, 16)
                q.dma_start(out=Yc[:, :], in_=spill[ci, 1]).then_inc(sl, 16)
                q.dma_start(out=PTl[:, :], in_=spill[ci, 2]).then_inc(sl, 16)
                q.dma_start(out=Rl[:, :], in_=spill[ci, 3]).then_inc(sl, 16)
                q.dma_start(out=gtl.rearrange("p (h s) -> p h s", s=128),
                            in_=pre_scr[NFQ:NFQ + HV, :, HALO + ci * 128:HALO + (ci + 1) * 128].rearrange("h p s -> p h s")).then_inc(sl, 16)
                ctr.inc(ldp[buf], 80)
            LDS = ctr.get(ldp[buf])

            @blk.tensor
            def _(t, ell_r=ell_r, ci=ci):
                t.wait_ge(s_s, ctr.get(s_s))
                t.wait_ge(s_v, ctr.get(s_v))
                ctr.inc(s_mm, 1)
                t.matmul(psA[0][:, 0:16], ell_r, c_identf[0:HV, 0:HV], start=True, stop=True).then_inc(s_mm, 1)
            CT = ctr.get(s_mm)

            @blk.scalar
            def _(s, CT=CT):
                s.wait_ge(s_mm, CT)
                ctr.inc(s_s, 1)
                s.copy(cols[:, 0:16], psA[0][:, 0:16]).then_inc(s_s, 1)
            COLS = ctr.get(s_s)

            st = [dict() for _ in range(2)]

            def S_rows(q2):
                d = st[q2]
                @blk.tensor
                def _(t, q2=q2):
                    t.wait_ge(s_s, COLS)
                    mm = None
                    for i in range(HH):
                        h = q2 * HH + i
                        mm = t.matmul(psB[q2][0:1, i * 128:(i + 1) * 128], cols[:, h:h + 1],
                                      c_identf[:, :], start=True, stop=True)
                    ctr.inc(s_mm, 1)
                    mm.then_inc(s_mm, 1)
                d["RWM"] = ctr.get(s_mm)

            def S_rowse(q2):
                d = st[q2]
                @blk.scalar
                def _(s, q2=q2):
                    s.wait_ge(s_mm, d["RWM"])
                    ctr.inc(s_s, 1)
                    s.copy(rowstall[0:1, q2 * W2:(q2 + 1) * W2], psB[q2][0:1, 0:W2]).then_inc(s_s, 1)
                d["RWS"] = ctr.get(s_s)

            def S_bcast(q2):
                d = st[q2]
                @blk.tensor
                def _(t, q2=q2):
                    t.wait_ge(s_s, d["RWS"])
                    mm = None
                    for qq in range(2):
                        sl = slice(qq * 512, (qq + 1) * 512)
                        mm = t.matmul(psA[q2][:, sl], c_onerf[0:1, :],
                                      rowstall[0:1, q2 * W2 + qq * 512:q2 * W2 + (qq + 1) * 512],
                                      start=True, stop=True)
                    ctr.inc(s_mm, 1)
                    mm.then_inc(s_mm, 1)
                d["BC"] = ctr.get(s_mm)

            def S_ea(q2):
                d = st[q2]
                sl2 = slice(q2 * W2, (q2 + 1) * W2)
                @blk.vector
                def _(v, q2=q2, sl2=sl2):
                    v.wait_ge(s_mm, d["BC"])
                    ins = v.tensor_tensor(h3(e_m.ap()[:, sl2]), h3(psA[q2].ap()[:, 0:W2]),
                                          colv(cols[:, q2 * HH:q2 * HH + HH]), ALU.subtract)
                    ctr.inc(s_v, 1); ins.then_inc(s_v, 1); v.wait_ge(s_v, ctr.get(s_v))
                    ctr.inc(s_v, 1)
                    v.tensor_tensor(h3(e_m.ap()[:, sl2]), h3(e_m.ap()[:, sl2]), mv(c_maskUD.ap()), ALU.add).then_inc(s_v, 1)
                d["EA"] = ctr.get(s_v)

            def S_exp(q2):
                d = st[q2]
                sl2 = slice(q2 * W2, (q2 + 1) * W2)
                @blk.scalar
                def _(s, q2=q2, sl2=sl2):
                    s.wait_ge(s_v, d["EA"])
                    ins = s.activation(qa[:, sl2], psA[q2][:, 0:W2], AF.Exp, bias=c_cols[:, 2:3])
                    ctr.inc(s_s, 1); ins.then_inc(s_s, 1); s.wait_ge(s_s, ctr.get(s_s))
                    ctr.inc(s_s, 1)
                    s.activation(e_m[:, sl2], e_m[:, sl2], AF.Exp).then_inc(s_s, 1)
                d["EX"] = ctr.get(s_s)

            def S_g(q2):
                d = st[q2]
                @blk.tensor
                def _(t, q2=q2, ci=ci):
                    t.wait_ge(s_s, d["EX"])
                    mm = None
                    for i in range(QH):
                        hq = q2 * QH + i
                        kf = qk_f.ap()[:, (8 + hq) * TPC + ci * 128:][:, :128]
                        qf = qk_f.ap()[:, hq * TPC + ci * 128:][:, :128]
                        mm = t.matmul(psA[q2][:, i * 128:(i + 1) * 128], kf, qf, start=True, stop=True)
                    ctr.inc(s_mm, 1)
                    mm.then_inc(s_mm, 1)
                d["GM"] = ctr.get(s_mm)

            def S_mt(q2):
                d = st[q2]
                sl2 = slice(q2 * W2, (q2 + 1) * W2)
                @blk.vector
                def _(v, q2=q2, sl2=sl2, ci=ci):
                    v.wait_ge(s_mm, d["GM"])
                    r4 = lambda ap: h3(ap).rearrange("p (q r) s -> p q r s", r=2)
                    ins = v.tensor_tensor(
                        r4(MT.ap()[:, sl2]),
                        psA[q2].ap()[:, 0:QH * 128].rearrange("p (q s) -> p q s", s=128).unsqueeze(2).broadcast_to((128, QH, 2, 128)),
                        r4(e_m.ap()[:, sl2]), ALU.mult)
                    qv = qk_f.ap().rearrange("p (hh t) -> p hh t", t=TPC)[:, q2 * QH:(q2 + 1) * QH, ci * 128:(ci + 1) * 128].unsqueeze(2).broadcast_to((128, QH, 2, 128))
                    ctr.inc(s_v, 1)
                    v.tensor_tensor(r4(qAf.ap()[:, sl2]), qv, r4(h3(qa.ap()[:, sl2]).rearrange("p (q r) s -> p q r s", r=2)) if False else h3(qa.ap()[:, sl2]).rearrange("p (q r) s -> p q r s", r=2), ALU.mult).then_inc(s_v, 1)
                d["MTV"] = ctr.get(s_v)

            def S_my(q2):
                d = st[q2]
                @blk.tensor
                def _(t, q2=q2):
                    t.wait_ge(s_v, d["MTV"])
                    t.wait_ge(ldp[buf], LDS)
                    mm = None
                    for i in range(HH):
                        o = q2 * W2 + i * 128
                        mm = t.matmul(psB[q2][:, i * 128:(i + 1) * 128], Yc[:, o:o + 128], MT[:, o:o + 128], start=True, stop=True)
                    ctr.inc(s_mm, 1)
                    mm.then_inc(s_mm, 1)
                d["MY"] = ctr.get(s_mm)

            def S_nm(q2):
                d = st[q2]
                @blk.scalar
                def _(s, q2=q2):
                    s.wait_ge(s_mm, d["MY"])
                    ctr.inc(s_s, 1)
                    s.activation(nMYT[:, q2 * W2:(q2 + 1) * W2], psB[q2][:, :], AF.Identity, scale=-1.0).then_inc(s_s, 1)
                d["NM"] = ctr.get(s_s)

            def S_o(q2):
                d = st[q2]
                @blk.tensor
                def _(t, q2=q2):
                    t.wait_ge(s_s, d["NM"])
                    mm = None
                    for i in range(HH):
                        o = q2 * W2 + i * 128
                        isl = slice(i * 128, (i + 1) * 128)
                        t.matmul(psB[q2][:, isl], MT[:, o:o + 128], U0[:, o:o + 128], start=True, stop=False)
                        t.matmul(psB[q2][:, isl], qAf[:, o:o + 128], state.ap()[:, o:o + 128], start=False, stop=False)
                        mm = t.matmul(psB[q2][:, isl], nMYT[:, o:o + 128], state.ap()[:, o:o + 128], start=False, stop=True)
                    ctr.inc(s_mm, 1)
                    mm.then_inc(s_mm, 1)
                d["OM"] = ctr.get(s_mm)

            def S_oe(q2):
                d = st[q2]
                sl2 = slice(q2 * W2, (q2 + 1) * W2)
                @blk.scalar
                def _(s, q2=q2, sl2=sl2):
                    s.wait_ge(s_mm, d["OM"])
                    ins = s.copy(Ofp[:, sl2], psB[q2][:, :])
                    ctr.inc(s_s, 1); ins.then_inc(s_s, 1); s.wait_ge(s_s, ctr.get(s_s))
                    ctr.inc(s_s, 1)
                    s.activation(sqf[:, sl2], Ofp[:, sl2], AF.Square).then_inc(s_s, 1)
                d["OE"] = ctr.get(s_s)

            def S_rm(q2):
                d = st[q2]
                sl2 = slice(q2 * W2, (q2 + 1) * W2)
                cl = slice(q2 * HH, q2 * HH + HH)
                @blk.vector
                def _(v, q2=q2, sl2=sl2, cl=cl):
                    v.wait_ge(s_s, d["OE"])
                    ins = v.tensor_reduce(rcol[:, cl].unsqueeze(2), h3(sqf.ap()[:, sl2]), AX.X, ALU.add)
                    ctr.inc(s_v, 1); ins.then_inc(s_v, 1); v.wait_ge(s_v, ctr.get(s_v))
                    ctr.inc(s_v, 1)
                    v.tensor_scalar(rcol[:, cl], rcol[:, cl], 1.0 / DV, EPS_RMS, ALU.mult, ALU.add).then_inc(s_v, 1)
                d["RM1"] = ctr.get(s_v)

            def S_sq(q2):
                d = st[q2]
                cl = slice(q2 * HH, q2 * HH + HH)
                @blk.scalar
                def _(s, q2=q2, cl=cl):
                    s.wait_ge(s_v, d["RM1"])
                    ctr.inc(s_s, 1)
                    s.activation(rcol[:, cl], rcol[:, cl], AF.Sqrt).then_inc(s_s, 1)
                d["RM2"] = ctr.get(s_s)

            def S_on(q2):
                d = st[q2]
                sl2 = slice(q2 * W2, (q2 + 1) * W2)
                cl = slice(q2 * HH, q2 * HH + HH)
                @blk.vector
                def _(v, q2=q2, sl2=sl2, cl=cl):
                    v.wait_ge(s_s, d["RM2"])
                    ins = v.reciprocal(rcol[:, cl], rcol[:, cl])
                    ctr.inc(s_v, 1); ins.then_inc(s_v, 1); v.wait_ge(s_v, ctr.get(s_v))
                    ctr.inc(s_v, 1)
                    v.tensor_tensor(h3(Onsb.ap()[:, sl2]), h3(Ofp.ap()[:, sl2]), colv(rcol[:, cl]), ALU.mult).then_inc(s_v, 1)
                d["ONV"] = ctr.get(s_v)

            def S_ot(q2):
                d = st[q2]
                @blk.tensor
                def _(t, q2=q2):
                    t.wait_ge(s_v, d["ONV"])
                    t.wait_ge(s_s, ctr.get(s_s))
                    mm = None
                    for i in range(HH):
                        o = q2 * W2 + i * 128
                        mm = t.matmul(psA[q2][:, i * 128:(i + 1) * 128], Onsb[:, o:o + 128], c_ident[:, :], start=True, stop=True)
                    ctr.inc(s_mm, 1)
                    mm.then_inc(s_mm, 1)
                d["OT"] = ctr.get(s_mm)

            def S_pg(q2):
                d = st[q2]
                sl2 = slice(q2 * W2, (q2 + 1) * W2)
                @blk.scalar
                def _(s, q2=q2, sl2=sl2, gtl=gtl):
                    s.wait_ge(s_mm, d["OT"])
                    ins = s.activation(gsil[:, sl2], gtl[:, sl2], AF.Silu)
                    ctr.inc(s_s, 1); ins.then_inc(s_s, 1); s.wait_ge(s_s, ctr.get(s_s))
                    ctr.inc(s_s, 1)
                    s.activation(Onsb[:, sl2], psA[q2][:, :], AF.Identity, scale=c_rmsw[:, 0:1]).then_inc(s_s, 1)
                d["PS1"] = ctr.get(s_s)

            def S_pv(q2):
                d = st[q2]
                sl2 = slice(q2 * W2, (q2 + 1) * W2)
                @blk.vector
                def _(v, q2=q2, sl2=sl2, pt=pt, buf=buf):
                    v.wait_ge(s_s, d["PS1"])
                    v.wait_ge(stp[buf], ctr.get(stp[buf]))
                    ctr.inc(s_v, 1)
                    v.tensor_tensor(pt[:, sl2], Onsb[:, sl2], gsil[:, sl2], ALU.mult).then_inc(s_v, 1)
                d["PV"] = ctr.get(s_v)

            def S_st(q2):
                d = st[q2]
                @blk.tensor
                def _(t, q2=q2):
                    t.wait_ge(s_s, ctr.get(s_s))
                    mm = None
                    for i in range(HH):
                        o = q2 * W2 + i * 128
                        mm = t.matmul(psB[q2][:, i * 128:(i + 1) * 128], PTl[:, o:o + 128], state.ap()[:, o:o + 128], start=True, stop=True)
                    ctr.inc(s_mm, 1)
                    mm.then_inc(s_mm, 1)
                d["SM"] = ctr.get(s_mm)

            def S_sv(q2):
                d = st[q2]
                sl2 = slice(q2 * W2, (q2 + 1) * W2)
                @blk.vector
                def _(v, q2=q2, sl2=sl2):
                    v.wait_ge(s_mm, d["SM"])
                    ctr.inc(s_v, 1)
                    v.tensor_tensor(state.ap()[:, sl2], psB[q2][:, :], Rl[:, sl2], ALU.add).then_inc(s_v, 1)

            stages = [S_rows, S_rowse, S_bcast, S_ea, S_exp, S_g, S_mt, S_my, S_nm,
                      S_o, S_oe, S_rm, S_sq, S_on, S_ot, S_pg, S_pv, S_st, S_sv]
            for stg in stages:
                stg(0)
                stg(1)

            PVmax = max(st[0]["PV"], st[1]["PV"])

            @blk.gpsimd
            def _(g, ci=ci, pt=pt, PVmax=PVmax, buf=buf):
                g.wait_ge(s_v, PVmax)
                g.wait_ge(stp[buf], ctr.get(stp[buf]))
                g.dma_start(out=p_scr[ci], in_=pt).then_inc(stp[buf], 16)
                ctr.inc(stp[buf], 16)


def _p3(nc, blk, ctr, sems, env):
    s_mm, s_v, s_s = sems["mm"], sems["v"], sems["s"]
    s_lda, s_ldb, s_sta, s_stb = sems["lda"], sems["ldb"], sems["sta"], sems["stb"]
    ldp = [s_lda, s_ldb]
    stp = [s_sta, s_stb]
    c_ident, c_cols = env["c_ident"], env["c_cols"]
    x_in, wout_b, p_scr, x2_scr = env["x_in"], env["wout_b"], env["p_scr"], env["x2_scr"]
    xq_f = env["xq_f"]
    from contextlib import ExitStack

    _phase_barrier(nc, blk, ctr, sems)
    with ExitStack() as st3:
        E1 = st3.enter_context
        pcol = E1(nc.sbuf_tensor("pcol", [128, 2 * 2048], BF16))
        woutb = E1(nc.sbuf_tensor("woutb", [128, 2 * 16 * 512], BF16))
        attn_all = E1(nc.sbuf_tensor("attn_all", [128, 8 * D], BF16))
        lnt_b = E1(nc.sbuf_tensor("lnt3", [128, 2 * 40], F32))
        x2t = E1(nc.sbuf_tensor("x2t", [128, 2 * D], F32))
        xnt_b = E1(nc.sbuf_tensor("xnt3", [128, 2 * D], F32))
        xtl = E1(nc.sbuf_tensor("xtl3", [128, 2 * D], F32))
        inti_b = E1(nc.sbuf_tensor("inti", [128, 2 * D], I32))
        xq_t_b = E1(nc.sbuf_tensor("xq_t", [128, 2 * D], BF16))
        ps_op0 = E1(nc.psum_tensor("ps_op0", [128, 512], F32))
        ps_op1 = E1(nc.psum_tensor("ps_op1", [128, 512], F32))
        ps_big = E1(nc.psum_tensor("ps_big3", [128, D], F32))

        evac_hist = {}
        projmm_hist = {}
        x2v_hist = {}
        tq_hist = {}
        cp_hist = {}

        def emit_lnq(tb, ready):
            at = attn_all[:, tb * D:(tb + 1) * D]
            xt = xtl[:, (tb % 2) * D:(tb % 2) * D + D]
            x2 = x2t[:, (tb % 2) * D:(tb % 2) * D + D]
            lnt = lnt_b[:, (tb % 2) * 40:(tb % 2) * 40 + 40]
            xnt = xnt_b[:, (tb % 2) * D:(tb % 2) * D + D]
            inti = inti_b[:, (tb % 2) * D:(tb % 2) * D + D]
            xq_t = xq_t_b[:, (tb % 2) * D:(tb % 2) * D + D]

            @blk.sync
            def _(q, tb=tb, xt=xt):
                if tb >= 2:
                    q.wait_ge(s_v, x2v_hist[tb - 2])
                q.wait_ge(ldp[tb % 2], ctr.get(ldp[tb % 2]))
                q.dma_start(out=xt, in_=x_in[HALO + tb * 128: HALO + (tb + 1) * 128, :]).then_inc(ldp[tb % 2], 16)
                ctr.inc(ldp[tb % 2], 16)
            XL = ctr.get(ldp[tb % 2])

            def emit_ln(src_ap, eps_val, dst_col):
                @blk.vector
                def _(v):
                    v.wait_ge(s_v, ctr.get(s_v))
                    ins = None
                    for i4 in range(4):
                        ins = v.bn_stats(lnt[:, i4 * 6:(i4 + 1) * 6], src_ap[:, i4 * 512:(i4 + 1) * 512])
                    ctr.inc(s_v, 1); ins.then_inc(s_v, 1); v.wait_ge(s_v, ctr.get(s_v))
                    ins = v.bn_aggr(lnt[:, 24:26], lnt[:, 0:24])
                    ctr.inc(s_v, 1); ins.then_inc(s_v, 1); v.wait_ge(s_v, ctr.get(s_v))
                    ctr.inc(s_v, 1)
                    v.tensor_scalar(lnt[:, 26:27], lnt[:, 25:26], eps_val, None, ALU.add).then_inc(s_v, 1)
                V1 = ctr.get(s_v)

                @blk.scalar
                def _(s, V1=V1):
                    s.wait_ge(s_v, V1)
                    ctr.inc(s_s, 1)
                    s.activation(lnt[:, 27:28], lnt[:, 26:27], AF.Sqrt).then_inc(s_s, 1)
                S1 = ctr.get(s_s)

                @blk.vector
                def _(v, S1=S1):
                    v.wait_ge(s_s, S1)
                    ins = v.reciprocal(lnt[:, dst_col:dst_col + 1], lnt[:, 27:28])
                    ctr.inc(s_v, 1); ins.then_inc(s_v, 1); v.wait_ge(s_v, ctr.get(s_v))
                    ctr.inc(s_v, 1)
                    v.scalar_tensor_tensor(out=lnt[:, dst_col + 1:dst_col + 2], in0=lnt[:, 24:25],
                                           scalar=-1.0, in1=lnt[:, dst_col:dst_col + 1],
                                           op0=ALU.mult, op1=ALU.mult).then_inc(s_v, 1)
                return ctr.get(s_v)

            @blk.vector
            def _(v, ready=ready):
                v.wait_ge(s_s, ready)
            LN1 = emit_ln(at, EPS_LN, 28)

            @blk.scalar
            def _(s, at=at, LN1=LN1):
                s.wait_ge(s_v, LN1)
                ctr.inc(s_s, 1)
                s.activation(xnt[:, :], at, AF.Identity,
                             bias=lnt[:, 29:30], scale=lnt[:, 28:29]).then_inc(s_s, 1)
            A1 = ctr.get(s_s)

            @blk.vector
            def _(v, x2=x2, xt=xt, A1=A1, XL=XL, tb=tb):
                v.wait_ge(s_s, A1)
                v.wait_ge(ldp[tb % 2], XL)
                v.wait_ge(stp[tb % 2], ctr.get(stp[tb % 2]))
                ctr.inc(s_v, 1)
                v.tensor_tensor(x2, xnt[:, :], xt, ALU.add).then_inc(s_v, 1)
            X2V = ctr.get(s_v)
            x2v_hist[tb] = X2V

            @blk.gpsimd
            def _(g, tb=tb, x2=x2, X2V=X2V):
                g.wait_ge(s_v, X2V)
                g.wait_ge(stp[tb % 2], ctr.get(stp[tb % 2]))
                g.dma_start(out=x2_scr[tb * 128:(tb + 1) * 128, :], in_=x2).then_inc(stp[tb % 2], 16)
                ctr.inc(stp[tb % 2], 16)

            LN2 = emit_ln(x2, EPS_FFN, 30)

            @blk.scalar
            def _(s, x2=x2, LN2=LN2):
                s.wait_ge(s_v, LN2)
                ctr.inc(s_s, 1)
                s.activation(xnt[:, :], x2, AF.Identity,
                             bias=lnt[:, 31:32], scale=lnt[:, 30:31]).then_inc(s_s, 1)
            A2 = ctr.get(s_s)

            @blk.vector
            def _(v, A2=A2, tb=tb):
                v.wait_ge(s_s, A2)
                if tb >= 2:
                    v.wait_ge(s_mm, tq_hist[tb - 2])
                ins = v.tensor_reduce(lnt[:, 32:33], xnt[:, :], AX.X, ALU.max, apply_absolute_value=True)
                ctr.inc(s_v, 1); ins.then_inc(s_v, 1); v.wait_ge(s_v, ctr.get(s_v))
                ins = v.reciprocal(lnt[:, 33:34], lnt[:, 32:33])
                ctr.inc(s_v, 1); ins.then_inc(s_v, 1); v.wait_ge(s_v, ctr.get(s_v))
                v.tensor_scalar(lnt[:, 34:35], lnt[:, 33:34], 127.0, None, ALU.mult)
                ins = v.tensor_scalar(lnt[:, 35:36], lnt[:, 32:33], 1.0 / 127.0, None, ALU.mult)
                ctr.inc(s_v, 1); ins.then_inc(s_v, 1); v.wait_ge(s_v, ctr.get(s_v))
                ins = v.tensor_scalar(xnt[:, :], xnt[:, :], lnt[:, 34:35], None, ALU.mult)
                ctr.inc(s_v, 1); ins.then_inc(s_v, 1); v.wait_ge(s_v, ctr.get(s_v))
                ins = v.tensor_copy(inti[:, :], xnt[:, :])
                ctr.inc(s_v, 1); ins.then_inc(s_v, 1); v.wait_ge(s_v, ctr.get(s_v))
                ctr.inc(s_v, 1)
                v.tensor_scalar(xq_t[:, :], inti[:, :], lnt[:, 35:36], None, ALU.mult).then_inc(s_v, 1)
            QV = ctr.get(s_v)

            @blk.tensor
            def _(t, tb=tb, QV=QV):
                t.wait_ge(s_v, QV)
                if tb > 0:
                    t.wait_ge(s_s, cp_hist[tb - 1])
                mm = None
                for db in range(NDB):
                    mm = t.matmul(ps_big[:, db * 128:(db + 1) * 128],
                                  xq_t[:, db * 128:(db + 1) * 128],
                                  c_ident[:, :], start=True, stop=True)
                ctr.inc(s_mm, 1)
                mm.then_inc(s_mm, 1)
            TQ = ctr.get(s_mm)
            tq_hist[tb] = TQ

            @blk.scalar
            def _(s, tb=tb, TQ=TQ):
                s.wait_ge(s_mm, TQ)
                ctr.inc(s_s, 1)
                s.copy(
                    xq_f.ap().rearrange("p (db c) -> p db c", db=NDB)[:, :, tb * 128:(tb + 1) * 128],
                    ps_big.ap().rearrange("p (db c) -> p db c", db=NDB),
                ).then_inc(s_s, 1)
            cp_hist[tb] = ctr.get(s_s)



        for fg in range(4):
            wsl = woutb[:, (fg % 2) * 16 * 512:][:, :16 * 512]

            @blk.gpsimd
            def _(g, fg=fg, wsl=wsl):
                if fg >= 2:
                    g.wait_ge(s_mm, ctr.get(s_mm))
                g.wait_ge(stp[fg % 2], ctr.get(stp[fg % 2]))
                g.dma_start(out=wsl.rearrange("p (db f) -> p db f", f=512),
                            in_=wout_b[fg].rearrange("(db p) f -> p db f", p=128)).then_inc(stp[fg % 2], 16)
                ctr.inc(stp[fg % 2], 16)
            WL = ctr.get(stp[fg % 2])

            ready_hist = {}
            for tb in range(8):
                it = fg * 8 + tb
                pb = it % 2
                psl = ps_op0 if tb % 2 == 0 else ps_op1
                pc = pcol[:, pb * 2048:pb * 2048 + 2048]

                @blk.sync
                def _(q, tb=tb, pc=pc, pb=pb, it=it):
                    if it >= 2:
                        q.wait_ge(s_mm, projmm_hist[it - 2])
                    q.wait_ge(ldp[pb], ctr.get(ldp[pb]))
                    q.dma_start(out=pc, in_=p_scr[tb]).then_inc(ldp[pb], 16)
                    ctr.inc(ldp[pb], 16)
                PL = ctr.get(ldp[pb])

                @blk.tensor
                def _(t, fg=fg, tb=tb, pc=pc, wsl=wsl, psl=psl, PL=PL, WL=WL, pb=pb, it=it):
                    t.wait_ge(ldp[pb], PL)
                    t.wait_ge(stp[fg % 2], WL)
                    if it >= 2:
                        t.wait_ge(s_s, evac_hist[it - 2])
                    mm = None
                    for db in range(NDB):
                        mm = t.matmul(psl[:, :],
                                      pc[:, db * 128:(db + 1) * 128],
                                      wsl.rearrange("p (db f) -> p db f", f=512)[:, db, :],
                                      start=(db == 0), stop=(db == NDB - 1))
                    ctr.inc(s_mm, 1)
                    mm.then_inc(s_mm, 1)
                OPM = ctr.get(s_mm)
                projmm_hist[it] = OPM

                @blk.scalar
                def _(s, fg=fg, tb=tb, psl=psl, OPM=OPM):
                    s.wait_ge(s_mm, OPM)
                    ctr.inc(s_s, 1)
                    s.copy(attn_all[:, tb * D + fg * 512: tb * D + fg * 512 + 512], psl[:, :]).then_inc(s_s, 1)
                evac_hist[it] = ctr.get(s_s)
                if fg == 3:
                    ready_hist[tb] = ctr.get(s_s)
            if fg == 3:
                for tb in range(8):
                    emit_lnq(tb, ready_hist[tb])
        OPDONE = ctr.get(s_s)

def _ffn_gu(nc, blk, ctr, sems, env):
    s_mm, s_v, s_s = sems["mm"], sems["v"], sems["s"]
    s_lda, s_ldb, s_sta, s_stb = sems["lda"], sems["ldb"], sems["sta"], sems["stb"]
    ldp = [sems["ga"], sems["gb"]]
    xq_f, p2 = env["xq_f"], env["p2"]
    wg_b, wu_b = env["wg_b"], env["wu_b"]
    from contextlib import ExitStack

    _phase_barrier(nc, blk, ctr, sems)
    with ExitStack() as stf:
        E1 = stf.enter_context
        wgb = E1(nc.sbuf_tensor("wgb", [128, 2 * 2048], BF16))
        wub = E1(nc.sbuf_tensor("wub", [128, 2 * 2048], BF16))
        gs = E1(nc.sbuf_tensor("gs", [128, 2 * 512], BF16))
        ps_g0 = E1(nc.psum_tensor("ps_g0", [128, 512], F32))
        ps_g1 = E1(nc.psum_tensor("ps_g1", [128, 512], F32))
        ps_u0 = E1(nc.psum_tensor("ps_u0", [128, 512], F32))
        ps_u1 = E1(nc.psum_tensor("ps_u1", [128, 512], F32))

        mm_hist = {}
        load_done = {}
        sign_done = {}

        def emit_load(hb):
            buf = hb % 2
            wg = wgb[:, buf * 2048:buf * 2048 + 2048]
            wu = wub[:, buf * 2048:buf * 2048 + 2048]

            @blk.gpsimd
            def _(g, hb=hb, wg=wg, wu=wu, buf=buf):
                if hb >= 2:
                    g.wait_ge(s_mm, mm_hist[hb - 2])
                g.wait_ge(ldp[buf], ctr.get(ldp[buf]))
                g.dma_start(out=wg.rearrange("p (db f) -> p db f", f=128),
                            in_=wg_b[hb].rearrange("(db p) f -> p db f", p=128)).then_inc(ldp[buf], 16)
                g.dma_start(out=wu.rearrange("p (db f) -> p db f", f=128),
                            in_=wu_b[hb].rearrange("(db p) f -> p db f", p=128)).then_inc(ldp[buf], 16)
                ctr.inc(ldp[buf], 32)
            load_done[hb] = ctr.get(ldp[buf])

        def emit_sign(hb):
            buf = hb % 2
            wg = wgb[:, buf * 2048:buf * 2048 + 2048]
            wu = wub[:, buf * 2048:buf * 2048 + 2048]

            @blk.scalar
            def _(s, hb=hb, wg=wg, wu=wu, buf=buf):
                s.wait_ge(ldp[buf], load_done[hb])
                s.sign(wg, wg)
                ctr.inc(s_s, 1)
                s.sign(wu, wu).then_inc(s_s, 1)
            sign_done[hb] = ctr.get(s_s)

        emit_load(0)
        emit_load(1)
        emit_sign(0)

        for hb in range(NHB):
            buf = hb % 2
            wg = wgb[:, buf * 2048:buf * 2048 + 2048]
            wu = wub[:, buf * 2048:buf * 2048 + 2048]
            if hb + 1 < NHB:
                emit_sign(hb + 1)
            for th in range(2):
                psg = ps_g0 if th == 0 else ps_g1
                psu = ps_u0 if th == 0 else ps_u1

                @blk.tensor
                def _(t, hb=hb, th=th, wg=wg, wu=wu, psg=psg, psu=psu):
                    t.wait_ge(s_s, sign_done[hb])
                    if hb > 0:
                        t.wait_ge(s_v, ctr.get(s_v) - 1)
                    mm = None
                    for db in range(NDB):
                        mm = t.matmul(psg[:, :], wg[:, db * 128:(db + 1) * 128],
                                      xq_f.ap()[:, db * TPC + th * 512:][:, :512],
                                      start=(db == 0), stop=(db == NDB - 1))
                    for db in range(NDB):
                        mm = t.matmul(psu[:, :], wu[:, db * 128:(db + 1) * 128],
                                      xq_f.ap()[:, db * TPC + th * 512:][:, :512],
                                      start=(db == 0), stop=(db == NDB - 1))
                    ctr.inc(s_mm, 1)
                    mm.then_inc(s_mm, 1)
                GM = ctr.get(s_mm)

                @blk.scalar
                def _(s, psg=psg, GM=GM, th=th):
                    s.wait_ge(s_mm, GM)
                    ctr.inc(s_s, 1)
                    s.activation(gs[:, th * 512:(th + 1) * 512], psg[:, :], AF.Silu).then_inc(s_s, 1)
                GS = ctr.get(s_s)

                @blk.vector
                def _(v, hb=hb, th=th, psu=psu, GS=GS):
                    v.wait_ge(s_s, GS)
                    ctr.inc(s_v, 1)
                    v.tensor_tensor(p2.ap()[:, hb * TPC + th * 512:][:, :512],
                                    psu[:, :], gs[:, th * 512:(th + 1) * 512], ALU.mult).then_inc(s_v, 1)
            mm_hist[hb] = ctr.get(s_mm)
            if hb + 2 < NHB:
                emit_load(hb + 2)


def _ffn_down(nc, blk, ctr, sems, env):
    s_mm, s_v, s_s = sems["mm"], sems["v"], sems["s"]
    s_lda, s_ldb, s_sta, s_stb = sems["lda"], sems["ldb"], sems["sta"], sems["stb"]
    ldp = [sems["ga"], sems["gb"]]
    stp = [s_sta, s_stb]
    p2, wd_b, ffn_scr = env["p2"], env["wd_b"], env["ffn_scr"]
    from contextlib import ExitStack

    _phase_barrier(nc, blk, ctr, sems)
    with ExitStack() as std_:
        E1 = std_.enter_context
        wdb = E1(nc.sbuf_tensor("wdb", [128, 2 * NHB * 128], BF16))
        fout = E1(nc.sbuf_tensor("fout", [128, 2 * TPC], BF16))
        ps_d0 = E1(nc.psum_tensor("ps_d0", [128, TPC], F32))
        ps_d1 = E1(nc.psum_tensor("ps_d1", [128, TPC], F32))

        mm_hist = {}
        load_done = {}
        sign_done = {}
        fo_hist = {}

        def emit_load(fcb):
            buf = fcb % 2
            wd = wdb[:, buf * NHB * 128:][:, :NHB * 128]

            @blk.gpsimd
            def _(g, fcb=fcb, wd=wd, buf=buf):
                if fcb >= 2:
                    g.wait_ge(s_mm, mm_hist[fcb - 2])
                g.wait_ge(ldp[buf], ctr.get(ldp[buf]))
                g.dma_start(out=wd.rearrange("p (hb f) -> p hb f", f=128),
                            in_=wd_b[fcb].rearrange("(hb p) f -> p hb f", p=128)).then_inc(ldp[buf], 16)
                ctr.inc(ldp[buf], 16)
            load_done[fcb] = ctr.get(ldp[buf])

        def emit_sign(fcb):
            buf = fcb % 2
            wd = wdb[:, buf * NHB * 128:][:, :NHB * 128]

            @blk.scalar
            def _(s, fcb=fcb, wd=wd, buf=buf):
                s.wait_ge(ldp[buf], load_done[fcb])
                ctr.inc(s_s, 1)
                s.sign(wd, wd).then_inc(s_s, 1)
            sign_done[fcb] = ctr.get(s_s)

        emit_load(0)
        emit_load(1)
        emit_sign(0)

        for fcb in range(16):
            buf = fcb % 2
            wd = wdb[:, buf * NHB * 128:][:, :NHB * 128]
            psd = ps_d0 if fcb % 2 == 0 else ps_d1
            fo = fout[:, buf * TPC:buf * TPC + TPC]
            if fcb + 1 < 16:
                emit_sign(fcb + 1)

            @blk.tensor
            def _(t, fcb=fcb, wd=wd, psd=psd):
                t.wait_ge(s_s, sign_done[fcb])
                if fcb >= 2:
                    t.wait_ge(s_s, fo_hist[fcb - 2])
                mm = None
                for th in range(2):
                    for hb in range(NHB):
                        mm = t.matmul(psd[:, th * 512:(th + 1) * 512],
                                      wd[:, hb * 128:(hb + 1) * 128],
                                      p2.ap()[:, hb * TPC + th * 512:][:, :512],
                                      start=(hb == 0), stop=(hb == NHB - 1))
                ctr.inc(s_mm, 1)
                mm.then_inc(s_mm, 1)
            DM = ctr.get(s_mm)

            @blk.scalar
            def _(s, psd=psd, fo=fo, DM=DM, buf=buf, fcb=fcb):
                s.wait_ge(s_mm, DM)
                if fcb >= 2:
                    s.wait_ge(stp[buf], ctr.get(stp[buf]))
                ctr.inc(s_s, 1)
                s.copy(fo, psd[:, :]).then_inc(s_s, 1)
            FE = ctr.get(s_s)
            fo_hist[fcb] = FE

            @blk.gpsimd
            def _(g, fcb=fcb, fo=fo, FE=FE, buf=buf):
                g.wait_ge(s_s, FE)
                g.wait_ge(stp[buf], ctr.get(stp[buf]))
                g.dma_start(out=ffn_scr[fcb], in_=fo).then_inc(stp[buf], 16)
                ctr.inc(stp[buf], 16)
            mm_hist[fcb] = ctr.get(s_mm)
            if fcb + 2 < 16:
                emit_load(fcb + 2)


def _final(nc, blk, ctr, sems, env):
    s_mm, s_v, s_s = sems["mm"], sems["v"], sems["s"]
    s_lda, s_ldb, s_sta, s_stb = sems["lda"], sems["ldb"], sems["sta"], sems["stb"]
    ldp = [s_lda, s_ldb]
    stp = [s_sta, s_stb]
    c_ident = env["c_ident"]
    ffn_scr, x2_scr, y_out = env["ffn_scr"], env["x2_scr"], env["y_out"]
    from contextlib import ExitStack

    _phase_barrier(nc, blk, ctr, sems)
    with ExitStack() as stf:
        E1 = stf.enter_context
        ftile = E1(nc.sbuf_tensor("ftile", [128, 2 * D], BF16))
        x2l = E1(nc.sbuf_tensor("x2l", [128, 2 * D], F32))
        yt = E1(nc.sbuf_tensor("yt", [128, 2 * D], F32))
        lnt_b = E1(nc.sbuf_tensor("lntF", [128, 64], F32))
        ps_fin0 = E1(nc.psum_tensor("ps_fin0", [128, D], F32))
        ps_fin1 = E1(nc.psum_tensor("ps_fin1", [128, D], F32))

        fm_hist = {}
        s2_hist = {}
        yv_hist = {}
        for tb in range(8):
            buf = tb % 2
            ft = ftile[:, buf * D:buf * D + D]
            xx = x2l[:, buf * D:buf * D + D]
            yy = yt[:, buf * D:buf * D + D]
            lnt = lnt_b[:, buf * 32:buf * 32 + 32]
            psf = ps_fin0 if buf == 0 else ps_fin1

            @blk.sync
            def _(q, tb=tb, ft=ft, xx=xx, buf=buf):
                if tb >= 2:
                    q.wait_ge(s_mm, fm_hist[tb - 2])
                    q.wait_ge(s_v, yv_hist[tb - 2])
                q.wait_ge(ldp[buf], ctr.get(ldp[buf]))
                q.dma_start(out=ft.rearrange("p (db t) -> p db t", t=128),
                            in_=ffn_scr[:, :, tb * 128:(tb + 1) * 128].rearrange("db p t -> p db t")).then_inc(ldp[buf], 16)
                q.dma_start(out=xx, in_=x2_scr[tb * 128:(tb + 1) * 128, :]).then_inc(ldp[buf], 16)
                ctr.inc(ldp[buf], 32)
            FL = ctr.get(ldp[buf])

            @blk.tensor
            def _(t, tb=tb, ft=ft, psf=psf, FL=FL, buf=buf):
                t.wait_ge(ldp[buf], FL)
                if tb >= 2:
                    t.wait_ge(s_s, s2_hist[tb - 2])
                mm = None
                for db in range(NDB):
                    mm = t.matmul(psf[:, db * 128:(db + 1) * 128],
                                  ft[:, db * 128:(db + 1) * 128],
                                  c_ident[:, :], start=True, stop=True)
                ctr.inc(s_mm, 1)
                mm.then_inc(s_mm, 1)
            FM = ctr.get(s_mm)
            fm_hist[tb] = FM

            @blk.vector
            def _(v, psf=psf, FM=FM):
                v.wait_ge(s_mm, FM)
                ins = None
                for i4 in range(4):
                    ins = v.bn_stats(lnt[:, i4 * 6:(i4 + 1) * 6], psf[:, i4 * 512:(i4 + 1) * 512])
                ctr.inc(s_v, 1); ins.then_inc(s_v, 1); v.wait_ge(s_v, ctr.get(s_v))
                ins = v.bn_aggr(lnt[:, 24:26], lnt[:, 0:24])
                ctr.inc(s_v, 1); ins.then_inc(s_v, 1); v.wait_ge(s_v, ctr.get(s_v))
                ctr.inc(s_v, 1)
                v.tensor_scalar(lnt[:, 26:27], lnt[:, 25:26], EPS_LN, None, ALU.add).then_inc(s_v, 1)
            V1 = ctr.get(s_v)

            @blk.scalar
            def _(s, V1=V1):
                s.wait_ge(s_v, V1)
                ctr.inc(s_s, 1)
                s.activation(lnt[:, 27:28], lnt[:, 26:27], AF.Sqrt).then_inc(s_s, 1)
            S1 = ctr.get(s_s)

            @blk.vector
            def _(v, S1=S1):
                v.wait_ge(s_s, S1)
                ins = v.reciprocal(lnt[:, 28:29], lnt[:, 27:28])
                ctr.inc(s_v, 1); ins.then_inc(s_v, 1); v.wait_ge(s_v, ctr.get(s_v))
                ctr.inc(s_v, 1)
                v.scalar_tensor_tensor(out=lnt[:, 29:30], in0=lnt[:, 24:25],
                                       scalar=-1.0, in1=lnt[:, 28:29],
                                       op0=ALU.mult, op1=ALU.mult).then_inc(s_v, 1)
            V2 = ctr.get(s_v)

            @blk.scalar
            def _(s, psf=psf, yy=yy, V2=V2, buf=buf, tb=tb):
                s.wait_ge(s_v, V2)
                if tb >= 2:
                    s.wait_ge(stp[buf], ctr.get(stp[buf]))
                ctr.inc(s_s, 1)
                s.activation(yy, psf[:, :], AF.Identity,
                             bias=lnt[:, 29:30], scale=lnt[:, 28:29]).then_inc(s_s, 1)
            S2 = ctr.get(s_s)
            s2_hist[tb] = S2

            @blk.vector
            def _(v, yy=yy, xx=xx, S2=S2):
                v.wait_ge(s_s, S2)
                ctr.inc(s_v, 1)
                v.tensor_tensor(yy, yy, xx, ALU.add).then_inc(s_v, 1)
            YV = ctr.get(s_v)
            yv_hist[tb] = YV

            @blk.gpsimd
            def _(g, tb=tb, yy=yy, YV=YV, buf=buf):
                g.wait_ge(s_v, YV)
                g.wait_ge(stp[buf], ctr.get(stp[buf]))
                g.dma_start(out=y_out[tb * 128:(tb + 1) * 128, :], in_=yy).then_inc(stp[buf], 16)
                ctr.inc(stp[buf], 16)

        @blk.sync
        def _(q):
            q.wait_ge(s_sta, ctr.get(s_sta))
            q.wait_ge(s_stb, ctr.get(s_stb))


def host_prep(inputs):
    x = np.ascontiguousarray(np.asarray(inputs["L0_attn_x"], np.float32))
    w_qkv = np.asarray(inputs["L0_attn_w_qkv"], np.float32)
    w_gate = np.asarray(inputs["L0_attn_w_gate"], np.float32)
    w_beta = np.asarray(inputs["L0_attn_w_beta"], np.float32)
    w_alpha = np.asarray(inputs["L0_attn_w_alpha"], np.float32)
    log_A = np.asarray(inputs["L0_attn_log_A"], np.float32)
    dt_bias = np.asarray(inputs["L0_attn_dt_bias"], np.float32)
    conv_w = np.asarray(inputs["L0_attn_conv_w"], np.float32)
    rms_w = np.asarray(inputs["L0_attn_rms_w"], np.float32)
    w_out = np.asarray(inputs["L0_attn_w_out"], np.float32)
    W_g = np.asarray(inputs["L0_ffn_W_g"], np.float32)
    W_u = np.asarray(inputs["L0_ffn_W_u"], np.float32)
    W_d = np.asarray(inputs["L0_ffn_W_d"], np.float32)

    wqkv_bl = np.ascontiguousarray(w_qkv.T.reshape(D, NFQ, 128).transpose(1, 0, 2))
    wgate_bl = np.ascontiguousarray(w_gate.T.reshape(D, NFG, 128).transpose(1, 0, 2))
    wba = np.ascontiguousarray(np.concatenate([w_beta, w_alpha], axis=0).T)
    convw = np.ascontiguousarray(
        conv_w[:, 0, :].reshape(NFQ, 128, 4).transpose(1, 0, 2).reshape(128, NFQ * 4))
    la = np.ascontiguousarray(np.stack([log_A, dt_bias], axis=1))
    rmsw = np.ascontiguousarray(rms_w[:, None])
    wout_bl = np.ascontiguousarray(w_out.T.reshape(D, 4, 512).transpose(1, 0, 2))
    wg_bl = np.ascontiguousarray(W_g.T.reshape(D, NHB, 128).transpose(1, 0, 2))
    wu_bl = np.ascontiguousarray(W_u.T.reshape(D, NHB, 128).transpose(1, 0, 2))
    wd_bl = np.ascontiguousarray(W_d.T.reshape(FFH, 16, 128).transpose(1, 0, 2))

    ident_bf = np.eye(128, dtype=NPBF16)
    ident_f = np.eye(128, dtype=np.float32)
    r_idx, c_idx = np.meshgrid(np.arange(128), np.arange(128), indexing="ij")
    maskLT = np.where(c_idx < r_idx, 0.0, NEG).astype(np.float32)
    maskUT = np.where(r_idx < c_idx, 0.0, NEG).astype(np.float32)
    maskUD = np.where(r_idx <= c_idx, -LSD, NEG).astype(np.float32)
    ones_col = np.ones((128, 1), dtype=NPBF16)
    ones_row_f = np.ones((1, 128), dtype=np.float32)
    zeros_f = np.zeros((128, 128), dtype=np.float32)
    ccols = np.zeros((128, 8), np.float32)
    ccols[:, 0] = EPS_L2
    ccols[:, 1] = EPS_RMS
    ccols[:, 2] = -LSD
    ccols[:, 3] = 2.0
    ccols[:, 4] = EPS_LN
    ccols[:, 5] = EPS_FFN
    ccols[:, 6] = 127.0
    ccols[:, 7] = 1.0

    in_maps = []
    for core in range(8):
        b, g = core // 4, core % 4
        t0 = g * TPC
        xs = np.zeros((TP, D), np.float32)
        lo = max(0, t0 - HALO)
        xs[HALO - (t0 - lo): HALO - (t0 - lo) + (t0 + TPC - lo)] = x[b, lo:t0 + TPC]
        hop_m = np.zeros((128, 7), np.float32)
        hop_imi = np.zeros((128, 7 * 128), np.float32)
        for j in range(7):
            use = 1.0 if (j // 4 == b and j % 4 < g) else 0.0
            hop_m[:, j] = use
            hop_imi[:, j * 128:(j + 1) * 128] = (1.0 - use) * np.eye(128, dtype=np.float32)
        in_maps.append({
            "x": xs, "wqkv_b": wqkv_bl, "wgate_b": wgate_bl, "wba": wba,
            "convw": convw, "loga_params": la, "rmsw": rmsw,
            "wout_b": wout_bl, "wg_b": wg_bl, "wu_b": wu_bl, "wd_b": wd_bl,
            "ident_bf": ident_bf, "ident_f": ident_f, "maskLT": maskLT,
            "maskUT": maskUT, "maskUD": maskUD,
            "ones_col_bf": ones_col, "ones_row_f": ones_row_f, "ccols": ccols,
            "ones_row_bf": np.ones((1, 128), dtype=NPBF16), "zeros_f": zeros_f,
            "hop_mcol": hop_m, "hop_imi": hop_imi,
        })
    return in_maps


def kernel(**inputs):
    in_maps = host_prep(inputs)
    nc, _ = build()
    res = run_bass_kernel_spmd(nc, in_maps, core_ids=list(range(8)))
    y = np.empty((2, 4096, D), np.float32)
    for core in range(8):
        b, g = core // 4, core % 4
        y[b, g * TPC:(g + 1) * TPC] = res.results[core]["y"]
    return y
